# version 08
# baseline (speedup 1.0000x reference)
import contextlib

import numpy as np

import concourse.bass as bass
import concourse.mybir as mybir
from concourse.bass_utils import run_bass_kernel_spmd

F32 = mybir.dt.float32
BF16 = mybir.dt.bfloat16
AF = mybir.ActivationFunctionType
OP = mybir.AluOpType
AX = mybir.AxisListType

B, T, C = 4, 2048, 2048
NG, GS, DH = 4, 4, 128
WIN, SINK = 512, 4
SCALE = float(DH) ** -0.5
NE = 32
DFF = 768
EPS = 1e-5
CCH = C // 128
NKV_TILES = 13
NKV = NKV_TILES * 128
NQT = 8
QT0 = 5
BIG = 1.0e38
NCORES = 8


def _rope_tables():
    half = DH // 2
    inv = 1.0 / (10000.0 ** (np.arange(half, dtype=np.float64) * 2.0 / DH))
    t = np.arange(T, dtype=np.float64)
    ang = inv[:, None] * t[None, :]
    cos = np.cos(ang)
    sin = np.sin(ang)
    cos_dup = np.concatenate([cos, cos], 0).astype(np.float32)
    sin_sgn = np.concatenate([-sin, sin], 0).astype(np.float32)
    return cos_dup, sin_sgn


def _glob_tokens(h):
    w0 = h * 1024 - 512
    g = np.empty(NKV, dtype=np.int64)
    g[0:128] = np.arange(128)
    g[128:] = w0 + np.arange(12 * 128)
    return w0, g


def _masks_for_core(h):
    w0, _ = _glob_tokens(h)
    out = np.empty((128, NQT * 5 * 128), dtype=np.float32)
    j = np.arange(128)[:, None]
    i = np.arange(128)[None, :]
    for qt in range(NQT):
        qg = h * 1024 + 128 * qt
        for s in range(5):
            kg = w0 + 128 * (qt + s)
            kj = kg + j
            qi = qg + i
            keep = (kj >= 0) & (qi >= kj) & (((qi - kj) <= WIN) | (kj < SINK))
            out[:, (qt * 5 + s) * 128:(qt * 5 + s + 1) * 128] = np.where(
                keep, BIG, -BIG).astype(np.float32)
    return out


def _sinkbias_for_core(h):
    w0, _ = _glob_tokens(h)
    out = np.zeros((4, NQT), dtype=np.float32)
    for qt in range(NQT):
        if any(w0 + 128 * (qt + s) == 0 for s in range(5)):
            out[:, qt] = -BIG
    return out


def _xloc_for_core(x_b, h):
    rows = [x_b[0:128]]
    if h == 0:
        rows.append(np.zeros((512, C), dtype=x_b.dtype))
        rows.append(x_b[0:1024])
    else:
        rows.append(x_b[512:2048])
    return np.ascontiguousarray(np.concatenate(rows, axis=0))


class Sem:
    def __init__(self, stack, nc, name):
        self.h = stack.enter_context(nc.semaphore(name))
        self.n = 0


class SB:

    def __init__(self, handle, cols):
        self.h = handle
        self.cols = cols
        self.freelist = [(0, cols)]
        self.live = {}
        self.dead = {}

    def alloc(self, name, n):
        assert name not in self.live and name not in self.dead
        for idx, (c0, sz) in enumerate(self.freelist):
            if sz >= n:
                self.live[name] = (c0, n)
                if sz == n:
                    self.freelist.pop(idx)
                else:
                    self.freelist[idx] = (c0 + n, sz - n)
                return
        raise RuntimeError(f"SBUF arena overflow: {name} needs {n} cols; "
                           f"free={self.freelist}")

    def release(self, *names):
        for name in names:
            c0, n = self.live.pop(name)
            self.dead[name] = (c0, n)
            self.freelist.append((c0, n))
        self.freelist.sort()
        merged = []
        for c, s in self.freelist:
            if merged and merged[-1][0] + merged[-1][1] == c:
                merged[-1] = (merged[-1][0], merged[-1][1] + s)
            else:
                merged.append((c, s))
        self.freelist = merged

    def ap(self, name, col=0, p0=0, np_=128, n=1):
        c0, tot = self.live.get(name) or self.dead[name]
        assert col + n <= tot, (name, col, n, tot)
        return self.h[p0:p0 + np_, c0 + col:c0 + col + n]


class Emit:

    ENGINES = ("sync", "tensor", "vector", "scalar", "gpsimd")

    def __init__(self):
        self.ops = {k: [] for k in self.ENGINES}
        self.lastwait = {k: {} for k in self.ENGINES}

    def op(self, engine, fn, waits=(), inc=None, inc_by=1, drain=True):
        waits = [(s, v) for (s, v) in waits if v > 0]
        for s, v in waits:
            assert v <= s.n, f"wait on future count {v} > {s.n}"
        if inc is not None:
            inc.n += inc_by
        lw = self.lastwait[engine]
        eff = []
        for s, v in waits:
            if lw.get(id(s), 0) < v:
                lw[id(s)] = v
                eff.append((s, v))

        def run(eng, fn=fn, eff=tuple(eff), inc=inc, inc_by=inc_by):
            for s, v in eff:
                eng.wait_ge(s.h, v)
            inst = fn(eng)
            if inc is not None:
                inst.then_inc(inc.h, inc_by)

        self.ops[engine].append((run, drain))

    def replay(self, engine, eng):
        needs_drain = engine in ("vector", "scalar")
        for i, (fn, dr) in enumerate(self.ops[engine]):
            if needs_drain and dr and i > 0:
                eng.drain()
            fn(eng)


class DmaQ:

    def __init__(self, stack, nc, name, engine, em, pool):
        self.sems = [Sem(stack, nc, f"{name}{i}") for i in range(pool)]
        self.engine = engine
        self.em = em
        self.i = 0

    def issue(self, out_ap, in_ap, waits=()):
        s = self.sems[self.i % len(self.sems)]
        w = list(waits)
        if s.n:
            w.append((s, s.n))
        self.em.op(self.engine,
                   lambda e, o=out_ap, i=in_ap: e.dma_start(out=o, in_=i),
                   waits=w, inc=s, inc_by=16)
        self.i += 1
        return (s, s.n)


def build_program():
    nc = bass.Bass("TRN2", target_bir_lowering=False)

    d_x = nc.dram_tensor("xloc", [NKV, C], F32, kind="ExternalInput")
    d_wq = nc.dram_tensor("wq", [C, C], F32, kind="ExternalInput")
    d_wk = nc.dram_tensor("wk", [C, 512], F32, kind="ExternalInput")
    d_wv = nc.dram_tensor("wv", [C, 512], F32, kind="ExternalInput")
    d_wo = nc.dram_tensor("wo", [C, C], F32, kind="ExternalInput")
    d_bq = nc.dram_tensor("bqc", [128, CCH], F32, kind="ExternalInput")
    d_bk = nc.dram_tensor("bkc", [128, NG], F32, kind="ExternalInput")
    d_bv = nc.dram_tensor("bvr", [1, 512], F32, kind="ExternalInput")
    d_bo = nc.dram_tensor("boc", [128, CCH], F32, kind="ExternalInput")
    d_gateT = nc.dram_tensor("gateT", [C, NE], F32, kind="ExternalInput")
    d_wguT = nc.dram_tensor("wguT", [NE, C, 2 * DFF], F32,
                            kind="ExternalInput")
    d_wdT = nc.dram_tensor("wdT", [NE, DFF, C], F32, kind="ExternalInput")
    d_simB = nc.dram_tensor("simB", [128, NE * NE], F32,
                            kind="ExternalInput")
    d_masks = nc.dram_tensor("masks", [128, NQT * 5 * 128], BF16,
                             kind="ExternalInput")
    d_sinkb = nc.dram_tensor("sinkb", [4, NQT], F32, kind="ExternalInput")
    d_cos = nc.dram_tensor("cosdup", [128, NKV], F32, kind="ExternalInput")
    d_sin = nc.dram_tensor("sinsgn", [128, NKV], F32, kind="ExternalInput")
    d_id = nc.dram_tensor("ident", [128, 128], F32, kind="ExternalInput")
    d_rot = nc.dram_tensor("rotmat", [128, 128], F32, kind="ExternalInput")
    d_oc = nc.dram_tensor("onescol", [128, 1], F32, kind="ExternalInput")
    d_or = nc.dram_tensor("onesrow", [1, 128], F32, kind="ExternalInput")
    d_out = nc.dram_tensor("out", [1024, C], F32, kind="ExternalOutput")

    FP_TOT = 26072
    BF_TOT = 54272
    stack = contextlib.ExitStack()
    sb_h = stack.enter_context(nc.sbuf_tensor([128, FP_TOT], F32))
    sbf_h = stack.enter_context(nc.sbuf_tensor([128, BF_TOT], BF16))
    ps_h = stack.enter_context(nc.psum_tensor([128, 4096], F32))
    sb = SB(sb_h, FP_TOT)
    sbb = SB(sbf_h, BF_TOT)
    em = Emit()

    PE = Sem(stack, nc, "sPE")
    DV = Sem(stack, nc, "sDV")
    AC = Sem(stack, nc, "sAC")
    qld = DmaQ(stack, nc, "qL", "sync", em, pool=8)
    qst = DmaQ(stack, nc, "qS", "gpsimd", em, pool=4)

    def bank(b, off=0, n=512, p0=0, np_=128):
        assert 0 <= b < 8 and off + n <= 512
        return ps_h[p0:p0 + np_, b * 512 + off:b * 512 + off + n]

    def wide(b, n, p0=0, np_=128):
        return ps_h[p0:p0 + np_, b * 512:b * 512 + n]

    bankdep = [None] * 8

    def bw(*bs):
        return [bankdep[b] for b in bs if bankdep[b] is not None]

    def bset(b, sem):
        bankdep[b] = (sem, sem.n)

    def dma(out_ap, in_ap, waits=(), q=None):
        return (q or qld).issue(out_ap, in_ap, waits)

    A = sb.ap
    Bp = sbb.ap

    for nm, sz in [("c_id", 128), ("c_rot", 128), ("c_oc", 1), ("c_or", 128),
                   ("c_snk", NQT), ("c_bq", CCH), ("c_bo", CCH),
                   ("c_bk", NG), ("c_bv", 512),
                   ("c_cos", NKV), ("c_sin", NKV)]:
        sb.alloc(nm, sz)
    ctoks = [
        dma(A("c_id", 0, 0, 128, 128), d_id[:, :]),
        dma(A("c_rot", 0, 0, 128, 128), d_rot[:, :]),
        dma(A("c_oc", 0, 0, 128, 1), d_oc[:, :]),
        dma(A("c_or", 0, 0, 1, 128), d_or[:, :]),
        dma(A("c_snk", 0, 0, 4, NQT), d_sinkb[:, :]),
        dma(A("c_bq", 0, 0, 128, CCH), d_bq[:, :]),
        dma(A("c_bo", 0, 0, 128, CCH), d_bo[:, :]),
        dma(A("c_bk", 0, 0, 128, NG), d_bk[:, :]),
        dma(A("c_bv", 0, 0, 1, 512), d_bv[:, :]),
        dma(A("c_cos", 0, 0, 128, NKV), d_cos[:, :]),
        dma(A("c_sin", 0, 0, 128, NKV), d_sin[:, :]),
    ]

    sb.alloc("K", NG * NKV)
    sb.alloc("V", NKV_TILES * 512)
    sbb.alloc("xh", CCH * 1024)
    sbb.alloc("xl", CCH * 1024)

    def Kap(g, kvcol, n, p0=0, np_=128):
        return A("K", g * NKV + kvcol, p0, np_, n)

    def Vap(tile, d0, n, p0=0, np_=128):
        return A("V", tile * 512 + d0, p0, np_, n)

    def XH(cc, qcol, n):
        return Bp("xh", cc * 1024 + qcol, 0, 128, n)

    def XL(cc, qcol, n):
        return Bp("xl", cc * 1024 + qcol, 0, 128, n)

    sb.alloc("xtile", C)
    sb.alloc("xts", CCH * 128)
    sb.alloc("wkv", 2048)
    sb.alloc("kraw", 128)
    sb.alloc("rs1", 128)

    xtile_use = 0
    wkv_use = 0
    wchunks = [("k", g) for g in range(NG)] + [("v", vh) for vh in range(4)]
    for t in range(NKV_TILES):
        own = t >= QT0
        oc = (t - QT0) * 128
        tok_x = dma(A("xtile", 0, 0, 128, C), d_x[t * 128:(t + 1) * 128, :],
                    waits=[(PE, xtile_use)])
        for cc in range(CCH):
            b = cc % 2
            em.op("tensor",
                  lambda e, cc=cc, b=b: e.transpose(
                      bank(b, 0, 128),
                      A("xtile", cc * 128, 0, 128, 128),
                      A("c_id", 0, 0, 128, 128)),
                  waits=[tok_x] + ctoks + bw(b), inc=PE)
            em.op("scalar",
                  lambda e, cc=cc, b=b: e.copy(
                      A("xts", cc * 128, 0, 128, 128), bank(b, 0, 128)),
                  waits=[(PE, PE.n)], inc=AC, drain=False)
            if own:
                em.op("scalar",
                      lambda e, cc=cc, oc=oc, b=b: e.copy(
                          XH(cc, oc, 128), bank(b, 0, 128)),
                      inc=AC, drain=False)
                em.op("vector",
                      lambda e, cc=cc, oc=oc, b=b: e.tensor_tensor(
                          XL(cc, oc, 128), bank(b, 0, 128),
                          XH(cc, oc, 128), OP.subtract),
                      waits=[(AC, AC.n)], inc=DV, drain=False)
                bset(b, DV)
            else:
                bset(b, AC)
        xtile_use = PE.n
        xts_ready = AC.n

        for wi, (kind, idx) in enumerate(wchunks):
            wsrc = d_wk if kind == "k" else d_wv
            tok_w = dma(A("wkv", 0, 0, 128, 2048).rearrange(
                "p (cc d) -> p cc d", d=128),
                wsrc[:, idx * 128:(idx + 1) * 128].rearrange(
                    "(cc p) d -> p cc d", p=128),
                waits=[(PE, wkv_use)])
            if kind == "k":
                kb = 2 + wi % 2
                for cc in range(CCH):
                    w = ([tok_w, (AC, xts_ready)] + bw(kb)) if cc == 0 \
                        else ()
                    em.op("tensor",
                          lambda e, cc=cc, kb=kb: e.matmul(
                              bank(kb, 0, 128),
                              A("wkv", cc * 128, 0, 128, 128),
                              A("xts", cc * 128, 0, 128, 128),
                              start=(cc == 0), stop=(cc == CCH - 1)),
                          waits=w, inc=(PE if cc == CCH - 1 else None))
                mmk = PE.n
                em.op("scalar",
                      lambda e, idx=idx, kb=kb: e.activation(
                          A("kraw", 0, 0, 128, 128), bank(kb, 0, 128),
                          AF.Identity, bias=A("c_bk", idx, 0, 128, 1)),
                      waits=[(PE, mmk), (DV, DV.n)] + ctoks, inc=AC,
                      drain=False)
                bset(kb, AC)
                rb = 4 + wi % 2
                em.op("tensor",
                      lambda e, rb=rb: e.matmul(
                          bank(rb, 0, 128), A("c_rot", 0, 0, 128, 128),
                          A("kraw", 0, 0, 128, 128), start=True, stop=True),
                      waits=[(AC, AC.n)] + bw(rb), inc=PE)
                rotk = PE.n
                em.op("vector",
                      lambda e, t=t: e.tensor_mul(
                          A("rs1", 0, 0, 128, 128), A("kraw", 0, 0, 128, 128),
                          A("c_cos", t * 128, 0, 128, 128)),
                      waits=[(AC, AC.n)] + ctoks, drain=False)
                em.op("vector",
                      lambda e, idx=idx, t=t, rb=rb: e.tensor_mul(
                          Kap(idx, t * 128, 128), bank(rb, 0, 128),
                          A("c_sin", t * 128, 0, 128, 128)),
                      waits=[(PE, rotk)])
                em.op("vector",
                      lambda e, idx=idx, t=t: e.tensor_add(
                          Kap(idx, t * 128, 128), Kap(idx, t * 128, 128),
                          A("rs1", 0, 0, 128, 128)),
                      inc=DV)
                bset(rb, DV)
            else:
                vb = 6 + wi % 2
                for cc in range(CCH):
                    w = ([tok_w, (AC, xts_ready)] + bw(vb)) if cc == 0 \
                        else ()
                    em.op("tensor",
                          lambda e, cc=cc, vb=vb: e.matmul(
                              bank(vb, 0, 128),
                              A("xts", cc * 128, 0, 128, 128),
                              A("wkv", cc * 128, 0, 128, 128),
                              start=(cc == 0), stop=False),
                          waits=w)
                em.op("tensor",
                      lambda e, idx=idx, vb=vb: e.matmul(
                          bank(vb, 0, 128), A("c_or", 0, 0, 1, 128),
                          A("c_bv", idx * 128, 0, 1, 128),
                          start=False, stop=True),
                      inc=PE)
                em.op("scalar",
                      lambda e, t=t, idx=idx, vb=vb: e.copy(
                          Vap(t, idx * 128, 128), bank(vb, 0, 128)),
                      waits=[(PE, PE.n)], inc=AC, drain=False)
                bset(vb, AC)
            wkv_use = PE.n

    sb.release("xtile", "xts", "wkv", "kraw", "rs1", "c_cos", "c_sin")
    fenceB = [(PE, PE.n), (DV, DV.n), (AC, AC.n)]

    sbb.alloc("maskb", NQT * 5 * 128)
    sb.alloc("Qr1", CCH * 128)
    sb.alloc("xrc", CCH * 128)
    sb.alloc("attnT", CCH * 128)
    sb.alloc("PT", 5 * 128)
    sb.alloc("PTs", 128)
    sb.alloc("wb", 2048)
    sb.alloc("qs1", 128)
    sb.alloc("qs2", 128)
    sb.alloc("sqs", 128)
    sb.alloc("bcx", 256)
    sb.alloc("c_cosq", 1024)
    sb.alloc("c_sinq", 1024)

    tok_masks = dma(Bp("maskb", 0, 0, 128, NQT * 5 * 128), d_masks[:, :],
                    waits=fenceB)
    ctoks = ctoks + [
        dma(A("c_cosq", 0, 0, 128, 1024), d_cos[:, QT0 * 128:NKV],
            waits=fenceB),
        dma(A("c_sinq", 0, 0, 128, 1024), d_sin[:, QT0 * 128:NKV],
            waits=fenceB),
    ]

    wb_use = 0
    for qt in range(NQT):
        oc = qt * 128

        for cc in range(CCH):
            em.op("vector",
                  lambda e, cc=cc, oc=oc: e.tensor_add(
                      A("xrc", cc * 128, 0, 128, 128),
                      XH(cc, oc, 128), XL(cc, oc, 128)),
                  inc=(DV if cc == CCH - 1 else None), drain=False)
        xrc_ready = DV.n

        for dc in range(CCH):
            tok_wb = dma(A("wb", 0, 0, 128, 2048).rearrange(
                "p (cc d) -> p cc d", d=128),
                d_wq[:, dc * 128:(dc + 1) * 128].rearrange(
                    "(cc p) d -> p cc d", p=128),
                waits=[(PE, wb_use)] + fenceB)
            qb = dc % 2
            for cc in range(CCH):
                w = ([tok_wb, (DV, xrc_ready)] + bw(qb)) if cc == 0 else ()
                em.op("tensor",
                      lambda e, cc=cc, qb=qb: e.matmul(
                          bank(qb, 0, 128),
                          A("wb", cc * 128, 0, 128, 128),
                          A("xrc", cc * 128, 0, 128, 128),
                          start=(cc == 0), stop=(cc == CCH - 1)),
                      waits=w, inc=(PE if cc == CCH - 1 else None))
            wb_use = PE.n
            mmq = PE.n
            em.op("scalar",
                  lambda e, dc=dc, qb=qb: e.activation(
                      A("sqs", 0, 0, 128, 128), bank(qb, 0, 128),
                      AF.Identity, bias=A("c_bq", dc, 0, 128, 1)),
                  waits=[(PE, mmq), (DV, DV.n)] + ctoks, inc=AC,
                  drain=False)
            bset(qb, AC)
            rb = 2 + dc % 2
            em.op("tensor",
                  lambda e, rb=rb: e.matmul(
                      bank(rb, 0, 128), A("c_rot", 0, 0, 128, 128),
                      A("sqs", 0, 0, 128, 128), start=True, stop=True),
                  waits=[(AC, AC.n)] + bw(rb), inc=PE)
            rotq = PE.n
            em.op("vector",
                  lambda e, qt=qt: e.tensor_mul(
                      A("qs1", 0, 0, 128, 128), A("sqs", 0, 0, 128, 128),
                      A("c_cosq", qt * 128, 0, 128, 128)),
                  waits=[(AC, AC.n)], drain=False)
            em.op("vector",
                  lambda e, rb=rb, qt=qt: e.tensor_mul(
                      A("qs2", 0, 0, 128, 128), bank(rb, 0, 128),
                      A("c_sinq", qt * 128, 0, 128, 128)),
                  waits=[(PE, rotq)], drain=False)
            em.op("vector",
                  lambda e, dc=dc: e.tensor_add(
                      A("Qr1", dc * 128, 0, 128, 128),
                      A("qs1", 0, 0, 128, 128), A("qs2", 0, 0, 128, 128)),
                  inc=DV)
            bset(rb, DV)
        q_ready = DV.n

        for g in range(NG):
            for hh in range(GS):
                gh = g * GS + hh
                pp = gh % 2
                bA, bB, bav, bdn = 2 * pp, 2 * pp + 1, 4 + pp, 6 + pp
                for s in range(5):
                    tb, toff = (bA, s * 128) if s < 4 else (bB, 0)
                    w = [(DV, q_ready), tok_masks] + bw(bA, bB) \
                        if s == 0 else ()
                    em.op("tensor",
                          lambda e, s=s, g=g, gh=gh, qt=qt, tb=tb,
                          toff=toff: e.matmul(
                              bank(tb, toff, 128),
                              Kap(g, (1 + qt + s) * 128, 128),
                              A("Qr1", gh * 128, 0, 128, 128),
                              start=True, stop=True),
                          waits=w)
                em.op("tensor",
                      lambda e, g=g, gh=gh, bB=bB: e.matmul(
                          bank(bB, 128, 128, 0, 4),
                          Kap(g, 0, 4),
                          A("Qr1", gh * 128, 0, 128, 128),
                          start=True, stop=True),
                      inc=PE)
                smm = PE.n
                em.op("vector",
                      lambda e, bA=bA, qt=qt: e.tensor_tensor(
                          wide(bA, 640),
                          wide(bA, 640),
                          Bp("maskb", qt * 640, 0, 128, 640), OP.min),
                      waits=[(PE, smm), (AC, AC.n)], inc=DV, drain=False)
                em.op("scalar",
                      lambda e, bA=bA: e.activation(
                          A("PT", 0, 0, 128, 640),
                          wide(bA, 640), AF.Exp, scale=SCALE),
                      waits=[(DV, DV.n)], inc=AC, drain=False)
                em.op("scalar",
                      lambda e, qt=qt, bB=bB: e.activation(
                          A("PTs", 0, 0, 4, 128), bank(bB, 128, 128, 0, 4),
                          AF.Exp, bias=A("c_snk", qt, 0, 4, 1), scale=SCALE),
                      waits=[(PE, smm)], inc=AC, drain=False)
                exp_done = AC.n
                bset(bA, AC)
                bset(bB, AC)
                for s in range(5):
                    w = ([(AC, exp_done)] + bw(bav, bdn)) if s == 0 else ()
                    em.op("tensor",
                          lambda e, s=s, g=g, qt=qt, bav=bav: e.matmul(
                              bank(bav, 0, 128),
                              Vap(1 + qt + s, g * 128, 128),
                              A("PT", s * 128, 0, 128, 128),
                              start=(s == 0), stop=False),
                          waits=w)
                em.op("tensor",
                      lambda e, g=g, bav=bav: e.matmul(
                          bank(bav, 0, 128),
                          Vap(0, g * 128, 128, 0, 4),
                          A("PTs", 0, 0, 4, 128),
                          start=False, stop=True))
                for s in range(5):
                    em.op("tensor",
                          lambda e, s=s, bdn=bdn: e.matmul(
                              bank(bdn, 0, 128, 0, 1),
                              A("c_oc", 0, 0, 128, 1),
                              A("PT", s * 128, 0, 128, 128),
                              start=(s == 0), stop=False))
                em.op("tensor",
                      lambda e, bdn=bdn: e.matmul(
                          bank(bdn, 0, 128, 0, 1),
                          A("c_oc", 0, 0, 4, 1),
                          A("PTs", 0, 0, 4, 128),
                          start=False, stop=True),
                      inc=PE)
                av_done = PE.n
                em.op("vector",
                      lambda e, bdn=bdn: e.reciprocal(
                          A("bcx", 0, 0, 1, 128), bank(bdn, 0, 128, 0, 1)),
                      waits=[(PE, av_done)], inc=DV, drain=False)
                em.op("tensor",
                      lambda e, bdn=bdn: e.matmul(
                          bank(bdn, 128, 128), A("c_or", 0, 0, 1, 128),
                          A("bcx", 0, 0, 1, 128), start=True, stop=True),
                      waits=[(DV, DV.n)], inc=PE)
                em.op("vector",
                      lambda e, bdn=bdn: e.tensor_copy(
                          A("bcx", 128, 0, 128, 128), bank(bdn, 128, 128)),
                      waits=[(PE, PE.n)], drain=False)
                em.op("vector",
                      lambda e, gh=gh, bav=bav: e.tensor_mul(
                          A("attnT", gh * 128, 0, 128, 128),
                          bank(bav, 0, 128), A("bcx", 128, 0, 128, 128)),
                      inc=DV)
                bset(bav, DV)
                bset(bdn, DV)
        attn_done = DV.n

        for mt in range(CCH):
            tok_wb = dma(A("wb", 0, 0, 128, 2048).rearrange(
                "p (cc d) -> p cc d", d=128),
                d_wo[:, mt * 128:(mt + 1) * 128].rearrange(
                    "(cc p) d -> p cc d", p=128),
                waits=[(PE, wb_use)] + fenceB)
            ob = mt % 2
            for cc in range(CCH):
                w = ([tok_wb, (DV, attn_done)] + bw(ob)) if cc == 0 \
                    else ()
                em.op("tensor",
                      lambda e, cc=cc, ob=ob: e.matmul(
                          bank(ob, 0, 128),
                          A("wb", cc * 128, 0, 128, 128),
                          A("attnT", cc * 128, 0, 128, 128),
                          start=(cc == 0), stop=(cc == CCH - 1)),
                      waits=w, inc=(PE if cc == CCH - 1 else None))
            wb_use = PE.n
            em.op("vector",
                  lambda e, mt=mt, ob=ob: e.scalar_tensor_tensor(
                      A("Qr1", mt * 128, 0, 128, 128), bank(ob, 0, 128),
                      A("c_bo", mt, 0, 128, 1),
                      A("xrc", mt * 128, 0, 128, 128),
                      OP.add, OP.add),
                  waits=[(PE, PE.n)], inc=DV, drain=False)
            bset(ob, DV)
        r1_done = DV.n

        for cc in range(CCH):
            w = ([(DV, r1_done)] + bw(4)) if cc == 0 else ()
            em.op("tensor",
                  lambda e, cc=cc: e.matmul(
                      bank(4, 0, 128, 0, 1), A("c_oc", 0, 0, 128, 1),
                      A("Qr1", cc * 128, 0, 128, 128),
                      start=(cc == 0), stop=(cc == CCH - 1)),
                  waits=w, inc=(PE if cc == CCH - 1 else None))
        for cc in range(CCH):
            em.op("scalar",
                  lambda e, cc=cc: e.activation(
                      A("sqs", 0, 0, 128, 128), A("Qr1", cc * 128, 0, 128, 128),
                      AF.Square),
                  waits=([(DV, r1_done), (PE, PE.n)] if cc == 0
                         else [(PE, PE.n)]),
                  inc=AC, drain=False)
            w = ([(AC, AC.n)] + bw(5)) if cc == 0 else [(AC, AC.n)]
            em.op("tensor",
                  lambda e, cc=cc: e.matmul(
                      bank(5, 0, 128, 0, 1), A("c_oc", 0, 0, 128, 1),
                      A("sqs", 0, 0, 128, 128),
                      start=(cc == 0), stop=(cc == CCH - 1)),
                  waits=w, inc=PE)
        stat_mm = PE.n
        em.op("vector",
              lambda e: e.tensor_scalar(
                  A("qs1", 0, 0, 1, 128), bank(4, 0, 128, 0, 1),
                  1.0 / C, None, OP.mult),
              waits=[(PE, stat_mm)])
        em.op("vector",
              lambda e: e.tensor_scalar(
                  A("qs2", 0, 0, 1, 128), bank(5, 0, 128, 0, 1),
                  1.0 / C, None, OP.mult))
        em.op("vector",
              lambda e: e.tensor_mul(
                  A("sqs", 0, 0, 1, 128), A("qs1", 0, 0, 1, 128),
                  A("qs1", 0, 0, 1, 128)))
        em.op("vector",
              lambda e: e.tensor_sub(
                  A("qs2", 0, 0, 1, 128), A("qs2", 0, 0, 1, 128),
                  A("sqs", 0, 0, 1, 128)))
        em.op("vector",
              lambda e: e.tensor_scalar(
                  A("qs2", 0, 0, 1, 128), A("qs2", 0, 0, 1, 128),
                  EPS, None, OP.add))
        em.op("vector",
              lambda e: e.reciprocal(
                  A("sqs", 0, 0, 1, 128), A("qs2", 0, 0, 1, 128)),
              inc=DV)
        em.op("scalar",
              lambda e: e.activation(
                  A("qs2", 0, 0, 1, 128), A("sqs", 0, 0, 1, 128), AF.Sqrt),
              waits=[(DV, DV.n)], inc=AC)
        em.op("vector",
              lambda e: e.tensor_mul(
                  A("sqs", 0, 0, 1, 128), A("qs1", 0, 0, 1, 128),
                  A("qs2", 0, 0, 1, 128)),
              waits=[(AC, AC.n)])
        em.op("vector",
              lambda e: e.tensor_scalar(
                  A("sqs", 0, 0, 1, 128), A("sqs", 0, 0, 1, 128),
                  -1.0, None, OP.mult),
              inc=DV)
        ln_rows = DV.n
        em.op("tensor",
              lambda e: e.matmul(
                  bank(4, 0, 128), A("c_or", 0, 0, 1, 128),
                  A("qs2", 0, 0, 1, 128), start=True, stop=True),
              waits=[(DV, ln_rows)] + bw(4))
        em.op("tensor",
              lambda e: e.matmul(
                  bank(5, 0, 128), A("c_or", 0, 0, 1, 128),
                  A("sqs", 0, 0, 1, 128), start=True, stop=True),
              waits=bw(5), inc=PE)
        em.op("vector",
              lambda e: e.tensor_copy(A("bcx", 0, 0, 128, 128),
                                      bank(4, 0, 128)),
              waits=[(PE, PE.n)])
        em.op("vector",
              lambda e: e.tensor_copy(A("bcx", 128, 0, 128, 128),
                                      bank(5, 0, 128)),
              inc=DV)
        bset(4, DV)
        bset(5, DV)
        for cc in range(CCH):
            em.op("vector",
                  lambda e, cc=cc: e.tensor_mul(
                      A("qs1", 0, 0, 128, 128),
                      A("Qr1", cc * 128, 0, 128, 128),
                      A("bcx", 0, 0, 128, 128)),
                  waits=([(DV, DV.n)] if cc == 0 else ()))
            em.op("vector",
                  lambda e, cc=cc: e.tensor_add(
                      A("Qr1", cc * 128, 0, 128, 128),
                      A("qs1", 0, 0, 128, 128),
                      A("bcx", 128, 0, 128, 128)),
                  inc=DV)
            em.op("scalar",
                  lambda e, cc=cc, oc=oc: e.copy(
                      XH(cc, oc, 128), A("Qr1", cc * 128, 0, 128, 128)),
                  waits=[(DV, DV.n)], inc=AC, drain=False)
            em.op("vector",
                  lambda e, cc=cc, oc=oc: e.tensor_tensor(
                      XL(cc, oc, 128), A("Qr1", cc * 128, 0, 128, 128),
                      XH(cc, oc, 128), OP.subtract),
                  waits=[(AC, AC.n)], inc=DV, drain=False)

    sb.release("Qr1", "xrc", "attnT", "PT", "PTs", "wb",
               "qs1", "qs2", "sqs", "bcx", "c_cosq", "c_sinq", "K", "V")
    sbb.release("maskb")
    z_done = DV.n
    fenceC = [(PE, PE.n), (DV, DV.n), (AC, AC.n)]

    sb.alloc("probs", NE)
    sb.alloc("prim", NE)
    sb.alloc("sec", NE)
    sb.alloc("rw", NE)
    sb.alloc("bs", NE)
    sb.alloc("comb", NQT * NE)
    sb.alloc("mx", 8)
    sb.alloc("rcol", 4)
    sb.alloc("rscr", 2 * NE)
    sb.alloc("cwrow", 512)
    sb.alloc("gtmp", 512)
    sb.alloc("c_simB", NE * NE)
    sbb.alloc("gateh", CCH * NE)
    sbb.alloc("gatel", CCH * NE)

    tok_simB = dma(A("c_simB", 0, 0, 128, NE * NE), d_simB[:, :],
                   waits=fenceC)

    tok_gt = dma(A("gtmp", 0, 0, 128, CCH * NE).rearrange(
        "p (cc ee) -> p cc ee", ee=NE),
        d_gateT[:, :].rearrange("(cc p) ee -> p cc ee", p=128),
        waits=fenceC)
    em.op("scalar",
          lambda e: e.copy(Bp("gateh", 0, 0, 128, CCH * NE),
                           A("gtmp", 0, 0, 128, CCH * NE)),
          waits=[tok_gt] + fenceC, inc=AC)
    em.op("vector",
          lambda e: e.tensor_tensor(
              Bp("gatel", 0, 0, 128, CCH * NE),
              A("gtmp", 0, 0, 128, CCH * NE),
              Bp("gateh", 0, 0, 128, CCH * NE), OP.subtract),
          waits=[(AC, AC.n)] + fenceC, inc=DV)
    gate_ready = DV.n

    for r in range(NQT):
        oc = r * 128
        lb = r % 2
        passes = [("h", "h"), ("h", "l"), ("l", "h")]
        nmm = 3 * CCH
        k = 0
        for cc in range(CCH):
            for zs, gs_ in passes:
                w = ([(DV, max(z_done, gate_ready))] + bw(lb)) if k == 0 \
                    else ()
                em.op("tensor",
                      lambda e, cc=cc, oc=oc, lb=lb, zs=zs, gs_=gs_,
                      k=k: e.matmul(
                          bank(lb, 0, NE),
                          (XH if zs == "h" else XL)(cc, oc, 128),
                          Bp("gateh" if gs_ == "h" else "gatel",
                             cc * NE, 0, 128, NE),
                          start=(k == 0), stop=(k == nmm - 1)),
                      waits=w, inc=(PE if k == nmm - 1 else None))
                k += 1
        em.op("vector",
              lambda e, lb=lb: e.reduce_max(
                  A("rcol", 0, 0, 128, 1), bank(lb, 0, NE), AX.X,
                  negate=True),
              waits=[(PE, PE.n)], inc=DV)
        em.op("scalar",
              lambda e, lb=lb: e.activation(
                  A("probs", 0, 0, 128, NE), bank(lb, 0, NE), AF.Exp,
                  bias=A("rcol", 0, 0, 128, 1),
                  accum_out=A("rcol", 1, 0, 128, 1)),
              waits=[(DV, DV.n)], inc=AC)
        bset(lb, AC)
        em.op("vector",
              lambda e: e.reciprocal(A("rcol", 2, 0, 128, 1),
                                     A("rcol", 1, 0, 128, 1)),
              waits=[(AC, AC.n)])
        em.op("vector",
              lambda e: e.tensor_scalar(
                  A("probs", 0, 0, 128, NE), A("probs", 0, 0, 128, NE),
                  A("rcol", 2, 0, 128, 1), None, OP.mult))
        em.op("vector",
              lambda e: e.max(A("mx", 0, 0, 128, 8),
                              A("probs", 0, 0, 128, NE)))
        em.op("vector",
              lambda e: e.reduce_sum(
                  A("rcol", 0, 0, 128, 1), A("mx", 0, 0, 128, 8), AX.X))
        em.op("vector",
              lambda e: e.reciprocal(A("rcol", 1, 0, 128, 1),
                                     A("rcol", 0, 0, 128, 1)))
        em.op("vector",
              lambda e: e.tensor_scalar(
                  A("prim", 0, 0, 128, NE), A("probs", 0, 0, 128, NE),
                  A("mx", 3, 0, 128, 1), None, OP.is_ge))
        em.op("vector",
              lambda e: e.tensor_scalar(
                  A("rscr", 0, 0, 128, NE), A("probs", 0, 0, 128, NE),
                  A("mx", 7, 0, 128, 1), None, OP.is_ge))
        em.op("vector",
              lambda e: e.tensor_sub(
                  A("sec", 0, 0, 128, NE), A("rscr", 0, 0, 128, NE),
                  A("prim", 0, 0, 128, NE)))
        em.op("vector",
              lambda e: e.tensor_scalar(
                  A("rw", 0, 0, 128, NE), A("probs", 0, 0, 128, NE),
                  A("rcol", 1, 0, 128, 1), None, OP.mult))
        em.op("vector",
              lambda e: e.tensor_mul(
                  A("rw", 0, 0, 128, NE), A("rw", 0, 0, 128, NE),
                  A("rscr", 0, 0, 128, NE)))
        for s in range(NE):
            em.op("vector",
                  lambda e, s=s: e.tensor_mul(
                      A("rscr", NE, 0, 128, NE), A("prim", 0, 0, 128, NE),
                      A("c_simB", s * NE, 0, 128, NE)),
                  waits=([tok_simB] if s == 0 else ()))
            em.op("vector",
                  lambda e, s=s: e.reduce_max(
                      A("bs", s, 0, 128, 1),
                      A("rscr", NE, 0, 128, NE), AX.X))
        em.op("vector",
              lambda e: e.tensor_scalar(
                  A("rscr", 0, 0, 128, NE), A("bs", 0, 0, 128, NE),
                  0.5, None, OP.is_lt))
        em.op("vector",
              lambda e: e.tensor_mul(
                  A("rscr", NE, 0, 128, NE), A("sec", 0, 0, 128, NE),
                  A("rscr", 0, 0, 128, NE)))
        em.op("vector",
              lambda e: e.tensor_add(
                  A("rscr", NE, 0, 128, NE), A("rscr", NE, 0, 128, NE),
                  A("prim", 0, 0, 128, NE)))
        em.op("vector",
              lambda e, r=r: e.tensor_mul(
                  A("comb", r * NE, 0, 128, NE), A("rw", 0, 0, 128, NE),
                  A("rscr", NE, 0, 128, NE)))
        em.op("vector",
              lambda e: e.tensor_scalar(
                  A("rscr", 0, 0, 128, NE), A("rscr", 0, 0, 128, NE),
                  -1.0, 1.0, OP.mult, OP.add))
        em.op("vector",
              lambda e: e.tensor_mul(
                  A("sec", 0, 0, 128, NE), A("sec", 0, 0, 128, NE),
                  A("rw", 0, 0, 128, NE)))
        em.op("vector",
              lambda e: e.tensor_mul(
                  A("sec", 0, 0, 128, NE), A("sec", 0, 0, 128, NE),
                  A("rscr", 0, 0, 128, NE)))
        for s in range(NE):
            em.op("vector",
                  lambda e, s=s: e.tensor_scalar(
                      A("rscr", 0, 0, 128, NE), A("c_simB", s * NE, 0, 128, NE),
                      A("bs", s, 0, 128, 1), None, OP.is_ge))
            em.op("vector",
                  lambda e: e.tensor_mul(
                      A("rscr", 0, 0, 128, NE), A("rscr", 0, 0, 128, NE),
                      A("prim", 0, 0, 128, NE)))
            em.op("vector",
                  lambda e, r=r, s=s: e.scalar_tensor_tensor(
                      A("comb", r * NE, 0, 128, NE),
                      A("rscr", 0, 0, 128, NE),
                      A("sec", s, 0, 128, 1),
                      A("comb", r * NE, 0, 128, NE),
                      OP.mult, OP.add),
                  inc=(DV if s == NE - 1 else None))
    routing_done = DV.n

    sb.release("probs", "prim", "rw", "bs", "mx", "rcol", "rscr", "sec",
               "gtmp", "c_simB")
    fenceC = [(PE, PE.n), (DV, DV.n), (AC, AC.n)]

    sb.alloc("wgu", 2 * CCH * 128)
    sb.alloc("wd", 6 * 256)
    sbb.alloc("wguh", 2 * CCH * 128)
    sbb.alloc("wgul", 2 * CCH * 128)
    sbb.alloc("wdh", 2 * 6 * 256)
    sbb.alloc("wdl", 2 * 6 * 256)
    sb.alloc("sg", 6 * 512)
    sb.alloc("hsc", 512)
    sbb.alloc("hhh", 6 * 512)
    sbb.alloc("hhl", 6 * 512)
    sb.alloc("y", CCH * 512)
    sb.alloc("cw", 512)
    sb.alloc("zrc", 512)
    sb.alloc("sq2", 512)
    sb.alloc("lrow", 2 * 512)
    sb.alloc("lbc", 1024)
    sb.alloc("otm", C)

    def ZH(cc, ih, n=512):
        return Bp("xh", cc * 1024 + ih * 512, 0, 128, n)

    def ZL(cc, ih, n=512):
        return Bp("xl", cc * 1024 + ih * 512, 0, 128, n)

    wgu_use = [0, 0]
    wd_use = [0, 0]
    wgu_f32_free = [[], []]
    wd_f32_free = [[]]
    tok_st = None
    for ih in range(2):
        def emit_down(e_ix, h_done):
            for mp in range(8):
                par = mp % 2
                tok_d = dma(
                    A("wd", 0, 0, 128, 6 * 256).rearrange(
                        "p (fc m) -> p fc m", m=256),
                    d_wdT[e_ix, :, mp * 256:(mp + 1) * 256].rearrange(
                        "(fc p) m -> p fc m", p=128),
                    waits=wd_f32_free[0] + fenceC)
                em.op("scalar",
                      lambda e, par=par: e.copy(
                          Bp("wdh", par * 6 * 256, 0, 128, 6 * 256),
                          A("wd", 0, 0, 128, 6 * 256)),
                      waits=[tok_d, (PE, wd_use[par])], inc=AC,
                      drain=False)
                dcast_n = AC.n
                em.op("vector",
                      lambda e, par=par: e.tensor_tensor(
                          Bp("wdl", par * 6 * 256, 0, 128, 6 * 256),
                          A("wd", 0, 0, 128, 6 * 256),
                          Bp("wdh", par * 6 * 256, 0, 128, 6 * 256),
                          OP.subtract),
                      waits=[(AC, dcast_n)], inc=DV, drain=False)
                dsub_n = DV.n
                wd_f32_free[0] = [(AC, dcast_n), (DV, dsub_n)]
                for sub in range(2):
                    mt = mp * 2 + sub
                    db = 2 + mt % 2
                    k = 0
                    for fc in range(6):
                        for ws, hs in [("h", "h"), ("h", "l"), ("l", "h")]:
                            w = ([(AC, dcast_n), (DV, max(dsub_n, h_done))]
                                 + bw(db)) if k == 0 else ()
                            em.op("tensor",
                                  lambda e, fc=fc, par=par, sub=sub, db=db,
                                  ws=ws, hs=hs, k=k: e.matmul(
                                      bank(db, 0, 512),
                                      Bp("wdh" if ws == "h" else "wdl",
                                         par * 6 * 256 + fc * 256
                                         + sub * 128, 0, 128, 128),
                                      Bp("hhh" if hs == "h" else "hhl",
                                         fc * 512, 0, 128, 512),
                                      start=(k == 0), stop=(k == 17)),
                                  waits=w, inc=(PE if k == 17 else None))
                            k += 1
                    wd_use[par] = PE.n
                    if e_ix == 0:
                        em.op("vector",
                              lambda e, mt=mt, db=db: e.tensor_copy(
                                  A("y", mt * 512, 0, 128, 512),
                                  bank(db, 0, 512)),
                              waits=[(PE, PE.n)], inc=DV, drain=False)
                    else:
                        em.op("vector",
                              lambda e, mt=mt, db=db: e.tensor_add(
                                  A("y", mt * 512, 0, 128, 512),
                                  A("y", mt * 512, 0, 128, 512),
                                  bank(db, 0, 512)),
                              waits=[(PE, PE.n)], inc=DV, drain=False)
                    bset(db, DV)

        pending = None
        for e_ix in range(NE):
            for rr in range(4):
                em.op("tensor",
                      lambda e, rr=rr, e_ix=e_ix, ih=ih: e.transpose(
                          bank(4, rr * 128, 128, 0, 1),
                          A("comb", (ih * 4 + rr) * NE + e_ix, 0, 128, 1),
                          A("c_id", 0, 0, 128, 128)),
                      waits=([(DV, routing_done)] + bw(4)) if rr == 0
                      else (), inc=(PE if rr == 3 else None))
            em.op("scalar",
                  lambda e: e.copy(A("cwrow", 0, 0, 1, 512),
                                   bank(4, 0, 512, 0, 1)),
                  waits=[(PE, PE.n)], inc=AC, drain=False)
            em.op("tensor",
                  lambda e: e.matmul(
                      bank(4, 0, 512), A("c_or", 0, 0, 1, 128),
                      A("cwrow", 0, 0, 1, 512),
                      start=True, stop=True),
                  waits=[(AC, AC.n)], inc=PE)
            em.op("vector",
                  lambda e: e.tensor_copy(A("cw", 0, 0, 128, 512),
                                          bank(4, 0, 512)),
                  waits=[(PE, PE.n)], inc=DV, drain=False)
            bset(4, DV)
            cw_done = DV.n
            for ft in range(12):
                par = ft % 2
                tok_g = dma(
                    A("wgu", par * CCH * 128, 0, 128, CCH * 128).rearrange(
                        "p (cc f) -> p cc f", f=128),
                    d_wguT[e_ix, :, ft * 128:(ft + 1) * 128].rearrange(
                        "(cc p) f -> p cc f", p=128),
                    waits=wgu_f32_free[par] + fenceC)
                em.op("scalar",
                      lambda e, par=par: e.copy(
                          Bp("wguh", par * CCH * 128, 0, 128, CCH * 128),
                          A("wgu", par * CCH * 128, 0, 128, CCH * 128)),
                      waits=[tok_g, (PE, wgu_use[par])], inc=AC,
                      drain=False)
                cast_n = AC.n
                em.op("vector",
                      lambda e, par=par: e.tensor_tensor(
                          Bp("wgul", par * CCH * 128, 0, 128, CCH * 128),
                          A("wgu", par * CCH * 128, 0, 128, CCH * 128),
                          Bp("wguh", par * CCH * 128, 0, 128, CCH * 128),
                          OP.subtract),
                      waits=[(AC, cast_n)], inc=DV, drain=False)
                sub_n = DV.n
                wgu_f32_free[par] = [(AC, cast_n), (DV, sub_n)]
                gb = ft % 2
                k = 0
                for cc in range(CCH):
                    for ws, zs in [("h", "h"), ("h", "l"), ("l", "h")]:
                        w = ([(AC, cast_n), (DV, sub_n)] + bw(gb)) \
                            if k == 0 else ()
                        em.op("tensor",
                              lambda e, cc=cc, par=par, gb=gb, ih=ih,
                              ws=ws, zs=zs, k=k: e.matmul(
                                  bank(gb, 0, 512),
                                  Bp("wguh" if ws == "h" else "wgul",
                                     par * CCH * 128 + cc * 128,
                                     0, 128, 128),
                                  (ZH if zs == "h" else ZL)(cc, ih),
                                  start=(k == 0), stop=(k == 3 * CCH - 1)),
                              waits=w,
                              inc=(PE if k == 3 * CCH - 1 else None))
                        k += 1
                wgu_use[par] = PE.n
                gmm = PE.n
                if ft < 6:
                    em.op("scalar",
                          lambda e, ft=ft, gb=gb: e.activation(
                              A("sg", ft * 512, 0, 128, 512),
                              bank(gb, 0, 512), AF.Silu),
                          waits=[(PE, gmm), (DV, DV.n)], inc=AC,
                          drain=False)
                    bset(gb, AC)
                else:
                    kk = ft - 6
                    em.op("vector",
                          lambda e, kk=kk, gb=gb: e.tensor_mul(
                              A("hsc", 0, 0, 128, 512),
                              A("sg", kk * 512, 0, 128, 512),
                              bank(gb, 0, 512)),
                          waits=[(PE, gmm), (AC, AC.n), (DV, cw_done)])
                    em.op("vector",
                          lambda e: e.tensor_mul(
                              A("hsc", 0, 0, 128, 512),
                              A("hsc", 0, 0, 128, 512),
                              A("cw", 0, 0, 128, 512)),
                          inc=DV)
                    bset(gb, DV)
                    em.op("scalar",
                          lambda e, kk=kk: e.copy(
                              Bp("hhh", kk * 512, 0, 128, 512),
                              A("hsc", 0, 0, 128, 512)),
                          waits=[(DV, DV.n), (PE, gmm)], inc=AC,
                          drain=False)
                    em.op("vector",
                          lambda e, kk=kk: e.tensor_tensor(
                              Bp("hhl", kk * 512, 0, 128, 512),
                              A("hsc", 0, 0, 128, 512),
                              Bp("hhh", kk * 512, 0, 128, 512),
                              OP.subtract),
                          waits=[(AC, AC.n), (PE, gmm)], inc=DV)
                if ft == 1 and pending is not None:
                    emit_down(*pending)
                    pending = None
            pending = (e_ix, DV.n)
        emit_down(*pending)
        pending = None

        for mt in range(CCH):
            em.op("vector",
                  lambda e, mt=mt, ih=ih: e.tensor_add(
                      A("zrc", 0, 0, 128, 512), ZH(mt, ih), ZL(mt, ih)))
            em.op("vector",
                  lambda e, mt=mt: e.tensor_add(
                      A("y", mt * 512, 0, 128, 512),
                      A("y", mt * 512, 0, 128, 512),
                      A("zrc", 0, 0, 128, 512)),
                  inc=DV)
        r2_done = DV.n
        for cc in range(CCH):
            w = ([(DV, r2_done)] + bw(5)) if cc == 0 else ()
            em.op("tensor",
                  lambda e, cc=cc: e.matmul(
                      bank(5, 0, 512, 0, 1), A("c_oc", 0, 0, 128, 1),
                      A("y", cc * 512, 0, 128, 512),
                      start=(cc == 0), stop=(cc == CCH - 1)),
                  waits=w, inc=(PE if cc == CCH - 1 else None))
        for cc in range(CCH):
            em.op("scalar",
                  lambda e, cc=cc: e.activation(
                      A("sq2", 0, 0, 128, 512), A("y", cc * 512, 0, 128, 512),
                      AF.Square),
                  waits=([(DV, r2_done), (PE, PE.n)] if cc == 0
                         else [(PE, PE.n)]),
                  inc=AC, drain=False)
            w = ([(AC, AC.n)] + bw(6)) if cc == 0 else [(AC, AC.n)]
            em.op("tensor",
                  lambda e, cc=cc: e.matmul(
                      bank(6, 0, 512, 0, 1), A("c_oc", 0, 0, 128, 1),
                      A("sq2", 0, 0, 128, 512),
                      start=(cc == 0), stop=(cc == CCH - 1)),
                  waits=w, inc=PE)
        st2 = PE.n
        em.op("vector",
              lambda e: e.tensor_scalar(
                  A("lrow", 0, 0, 1, 512), bank(5, 0, 512, 0, 1),
                  1.0 / C, None, OP.mult),
              waits=[(PE, st2)])
        em.op("vector",
              lambda e: e.tensor_scalar(
                  A("lrow", 512, 0, 1, 512), bank(6, 0, 512, 0, 1),
                  1.0 / C, None, OP.mult))
        em.op("vector",
              lambda e: e.tensor_mul(
                  A("zrc", 0, 0, 1, 512), A("lrow", 0, 0, 1, 512),
                  A("lrow", 0, 0, 1, 512)))
        em.op("vector",
              lambda e: e.tensor_sub(
                  A("lrow", 512, 0, 1, 512), A("lrow", 512, 0, 1, 512),
                  A("zrc", 0, 0, 1, 512)))
        em.op("vector",
              lambda e: e.tensor_scalar(
                  A("lrow", 512, 0, 1, 512), A("lrow", 512, 0, 1, 512),
                  EPS, None, OP.add))
        em.op("vector",
              lambda e: e.reciprocal(
                  A("zrc", 0, 0, 1, 512), A("lrow", 512, 0, 1, 512)),
              inc=DV)
        em.op("scalar",
              lambda e: e.activation(
                  A("lrow", 512, 0, 1, 512), A("zrc", 0, 0, 1, 512),
                  AF.Sqrt),
              waits=[(DV, DV.n)], inc=AC)
        em.op("vector",
              lambda e: e.tensor_mul(
                  A("zrc", 0, 0, 1, 512), A("lrow", 0, 0, 1, 512),
                  A("lrow", 512, 0, 1, 512)),
              waits=[(AC, AC.n)])
        em.op("vector",
              lambda e: e.tensor_scalar(
                  A("zrc", 0, 0, 1, 512), A("zrc", 0, 0, 1, 512),
                  -1.0, None, OP.mult),
              inc=DV)
        lnr = DV.n
        em.op("tensor",
              lambda e: e.matmul(
                  bank(5, 0, 512), A("c_or", 0, 0, 1, 128),
                  A("lrow", 512, 0, 1, 512), start=True, stop=True),
              waits=[(DV, lnr)] + bw(5))
        em.op("tensor",
              lambda e: e.matmul(
                  bank(6, 0, 512), A("c_or", 0, 0, 1, 128),
                  A("zrc", 0, 0, 1, 512), start=True, stop=True),
              waits=bw(6), inc=PE)
        em.op("vector",
              lambda e: e.tensor_copy(A("lbc", 0, 0, 128, 512),
                                      bank(5, 0, 512)),
              waits=[(PE, PE.n)])
        em.op("vector",
              lambda e: e.tensor_copy(A("lbc", 512, 0, 128, 512),
                                      bank(6, 0, 512)),
              inc=DV)
        bset(5, DV)
        bset(6, DV)
        bc_done = DV.n
        for j in range(4):
            for mt in range(CCH):
                em.op("vector",
                      lambda e, mt=mt, j=j: e.tensor_mul(
                          A("sq2", 0, 0, 128, 128),
                          A("y", mt * 512 + j * 128, 0, 128, 128),
                          A("lbc", j * 128, 0, 128, 128)),
                      waits=(([(DV, bc_done), (PE, PE.n)]
                              + ([tok_st] if tok_st else []))
                             if mt == 0 else [(PE, PE.n)]), drain=False)
                em.op("vector",
                      lambda e, mt=mt, j=j: e.tensor_add(
                          A("sq2", 0, 0, 128, 128),
                          A("sq2", 0, 0, 128, 128),
                          A("lbc", 512 + j * 128, 0, 128, 128)),
                      inc=DV)
                tb = 7 if mt % 2 == 0 else 0
                em.op("tensor",
                      lambda e, tb=tb: e.transpose(
                          bank(tb, 0, 128), A("sq2", 0, 0, 128, 128),
                          A("c_id", 0, 0, 128, 128)),
                      waits=[(DV, DV.n)] + bw(tb), inc=PE)
                em.op("scalar",
                      lambda e, mt=mt, tb=tb: e.copy(
                          A("otm", mt * 128, 0, 128, 128), bank(tb, 0, 128)),
                      waits=[(PE, PE.n)], inc=AC, drain=False)
                bset(tb, AC)
            r0 = ih * 512 + j * 128
            tok_st = dma(d_out[r0:r0 + 128, :], A("otm", 0, 0, 128, C),
                         waits=[(AC, AC.n)], q=qst)

    em.op("gpsimd", lambda e: e.nop(),
          waits=[(s, s.n) for s in qst.sems])

    with nc.Block() as blk:
        blk.sync(lambda e: em.replay("sync", e))
        blk.tensor(lambda e: em.replay("tensor", e))
        blk.vector(lambda e: em.replay("vector", e))
        blk.scalar(lambda e: em.replay("scalar", e))
        blk.gpsimd(lambda e: em.replay("gpsimd", e))

    nc.dbg_sb = sb
    nc.dbg_sbb = sbb
    nc.dbg_ps = ps_h
    return nc


def kernel(**inputs):
    import ml_dtypes

    x = np.asarray(inputs["L0_attn_x"], dtype=np.float32)
    wq = np.asarray(inputs["L0_attn_Wq"], dtype=np.float32)
    bq = np.asarray(inputs["L0_attn_bq"], dtype=np.float32)
    wk = np.asarray(inputs["L0_attn_Wk"], dtype=np.float32)
    bk = np.asarray(inputs["L0_attn_bk"], dtype=np.float32)
    wv = np.asarray(inputs["L0_attn_Wv"], dtype=np.float32)
    bv = np.asarray(inputs["L0_attn_bv"], dtype=np.float32)
    wo = np.asarray(inputs["L0_attn_Wo"], dtype=np.float32)
    bo = np.asarray(inputs["L0_attn_bo"], dtype=np.float32)
    gate = np.asarray(inputs["L0_ffn_gate_weight"], dtype=np.float32)
    gup = np.asarray(inputs["L0_ffn_gate_up_proj"], dtype=np.float32)
    dwn = np.asarray(inputs["L0_ffn_down_proj"], dtype=np.float32)
    sim = np.asarray(inputs["L0_ffn_similarity"], dtype=np.float32)

    wguT = np.ascontiguousarray(np.transpose(gup, (0, 2, 1)))
    wdT = np.ascontiguousarray(np.transpose(dwn, (0, 2, 1)))
    gateT = np.ascontiguousarray(gate.T)
    simd = sim.copy()
    np.fill_diagonal(simd, 1.0)
    simB = np.ascontiguousarray(
        np.tile(simd.reshape(1, NE * NE), (128, 1)))

    cos_dup, sin_sgn = _rope_tables()
    ident = np.eye(128, dtype=np.float32)
    rot = np.zeros((128, 128), dtype=np.float32)
    for m in range(128):
        rot[(m + 64) % 128, m] = 1.0
    onescol = np.ones((128, 1), dtype=np.float32)
    onesrow = np.ones((1, 128), dtype=np.float32)
    bqc = np.ascontiguousarray(bq.reshape(CCH, 128).T)
    boc = np.ascontiguousarray(bo.reshape(CCH, 128).T)
    bkc = np.ascontiguousarray(bk.reshape(NG, 128).T)
    bvr = np.ascontiguousarray(bv.reshape(1, 512))

    in_maps = []
    for core in range(NCORES):
        b, h = core // 2, core % 2
        _, glob = _glob_tokens(h)
        gl = np.clip(glob, 0, T - 1)
        in_maps.append({
            "xloc": _xloc_for_core(x[b], h),
            "wq": wq, "wk": wk, "wv": wv, "wo": wo,
            "bqc": bqc, "bkc": bkc, "bvr": bvr, "boc": boc,
            "gateT": gateT, "wguT": wguT, "wdT": wdT, "simB": simB,
            "masks": _masks_for_core(h).astype(ml_dtypes.bfloat16),
            "sinkb": _sinkbias_for_core(h),
            "cosdup": np.ascontiguousarray(cos_dup[:, gl]),
            "sinsgn": np.ascontiguousarray(sin_sgn[:, gl]),
            "ident": ident, "rotmat": rot,
            "onescol": onescol, "onesrow": onesrow,
        })

    nc = build_program()
    res = run_bass_kernel_spmd(nc, in_maps, core_ids=list(range(NCORES)))
    out = np.empty((B, T, C), dtype=np.float32)
    for core in range(NCORES):
        b, h = core // 2, core % 2
        out[b, h * 1024:(h + 1) * 1024] = res.results[core]["out"]
    return out


# version 09
# speedup vs baseline: 1.5783x; 1.5783x over previous
import contextlib

import numpy as np

import concourse.bass as bass
import concourse.mybir as mybir
from concourse.bass_utils import run_bass_kernel_spmd

F32 = mybir.dt.float32
BF16 = mybir.dt.bfloat16
AF = mybir.ActivationFunctionType
OP = mybir.AluOpType
AX = mybir.AxisListType

B, T, C = 4, 2048, 2048
NG, GS, DH = 4, 4, 128
WIN, SINK = 512, 4
SCALE = float(DH) ** -0.5
NE = 32
DFF = 768
EPS = 1e-5
CCH = C // 128
NKV_TILES = 13
NKV = NKV_TILES * 128
NQT = 8
QT0 = 5
BIG = 1.0e38
NCORES = 8


def _rope_tables():
    half = DH // 2
    inv = 1.0 / (10000.0 ** (np.arange(half, dtype=np.float64) * 2.0 / DH))
    t = np.arange(T, dtype=np.float64)
    ang = inv[:, None] * t[None, :]
    cos = np.cos(ang)
    sin = np.sin(ang)
    cos_dup = np.concatenate([cos, cos], 0).astype(np.float32)
    sin_sgn = np.concatenate([-sin, sin], 0).astype(np.float32)
    return cos_dup, sin_sgn


def _glob_tokens(h):
    w0 = h * 1024 - 512
    g = np.empty(NKV, dtype=np.int64)
    g[0:128] = np.arange(128)
    g[128:] = w0 + np.arange(12 * 128)
    return w0, g


def _masks_for_core(h):
    w0, _ = _glob_tokens(h)
    out = np.empty((128, NQT * 5 * 128), dtype=np.float32)
    j = np.arange(128)[:, None]
    i = np.arange(128)[None, :]
    for qt in range(NQT):
        qg = h * 1024 + 128 * qt
        for s in range(5):
            kg = w0 + 128 * (qt + s)
            kj = kg + j
            qi = qg + i
            keep = (kj >= 0) & (qi >= kj) & (((qi - kj) <= WIN) | (kj < SINK))
            out[:, (qt * 5 + s) * 128:(qt * 5 + s + 1) * 128] = np.where(
                keep, BIG, -BIG).astype(np.float32)
    return out


def _sinkbias_for_core(h):
    w0, _ = _glob_tokens(h)
    out = np.zeros((4, NQT), dtype=np.float32)
    for qt in range(NQT):
        if any(w0 + 128 * (qt + s) == 0 for s in range(5)):
            out[:, qt] = -BIG
    return out


def _xloc_for_core(x_b, h):
    rows = [x_b[0:128]]
    if h == 0:
        rows.append(np.zeros((512, C), dtype=x_b.dtype))
        rows.append(x_b[0:1024])
    else:
        rows.append(x_b[512:2048])
    return np.ascontiguousarray(np.concatenate(rows, axis=0))


class Sem:
    def __init__(self, stack, nc, name):
        self.h = stack.enter_context(nc.semaphore(name))
        self.n = 0


class SB:

    def __init__(self, handle, cols):
        self.h = handle
        self.cols = cols
        self.freelist = [(0, cols)]
        self.live = {}
        self.dead = {}

    def alloc(self, name, n):
        assert name not in self.live and name not in self.dead
        for idx, (c0, sz) in enumerate(self.freelist):
            if sz >= n:
                self.live[name] = (c0, n)
                if sz == n:
                    self.freelist.pop(idx)
                else:
                    self.freelist[idx] = (c0 + n, sz - n)
                return
        raise RuntimeError(f"SBUF arena overflow: {name} needs {n} cols; "
                           f"free={self.freelist}")

    def release(self, *names):
        for name in names:
            c0, n = self.live.pop(name)
            self.dead[name] = (c0, n)
            self.freelist.append((c0, n))
        self.freelist.sort()
        merged = []
        for c, s in self.freelist:
            if merged and merged[-1][0] + merged[-1][1] == c:
                merged[-1] = (merged[-1][0], merged[-1][1] + s)
            else:
                merged.append((c, s))
        self.freelist = merged

    def ap(self, name, col=0, p0=0, np_=128, n=1):
        c0, tot = self.live.get(name) or self.dead[name]
        assert col + n <= tot, (name, col, n, tot)
        return self.h[p0:p0 + np_, c0 + col:c0 + col + n]


class Emit:

    ENGINES = ("sync", "tensor", "vector", "scalar", "gpsimd")

    def __init__(self):
        self.ops = {k: [] for k in self.ENGINES}
        self.lastwait = {k: {} for k in self.ENGINES}

    def op(self, engine, fn, waits=(), inc=None, inc_by=1, drain=True):
        waits = [(s, v) for (s, v) in waits if v > 0]
        for s, v in waits:
            assert v <= s.n, f"wait on future count {v} > {s.n}"
        if inc is not None:
            inc.n += inc_by
        lw = self.lastwait[engine]
        eff = []
        for s, v in waits:
            if lw.get(id(s), 0) < v:
                lw[id(s)] = v
                eff.append((s, v))

        def run(eng, fn=fn, eff=tuple(eff), inc=inc, inc_by=inc_by):
            for s, v in eff:
                eng.wait_ge(s.h, v)
            inst = fn(eng)
            if inc is not None:
                inst.then_inc(inc.h, inc_by)

        self.ops[engine].append((run, drain))

    def replay(self, engine, eng):
        needs_drain = engine in ("vector", "scalar")
        for i, (fn, dr) in enumerate(self.ops[engine]):
            if needs_drain and dr and i > 0:
                eng.drain()
            fn(eng)


class DmaQ:

    def __init__(self, stack, nc, name, engine, em, pool):
        self.sems = [Sem(stack, nc, f"{name}{i}") for i in range(pool)]
        self.engine = engine
        self.em = em
        self.i = 0

    def issue(self, out_ap, in_ap, waits=()):
        s = self.sems[self.i % len(self.sems)]
        w = list(waits)
        if s.n:
            w.append((s, s.n))
        self.em.op(self.engine,
                   lambda e, o=out_ap, i=in_ap: e.dma_start(out=o, in_=i),
                   waits=w, inc=s, inc_by=16)
        self.i += 1
        return (s, s.n)


def build_program():
    nc = bass.Bass("TRN2", target_bir_lowering=False)

    d_x = nc.dram_tensor("xloc", [NKV, C], F32, kind="ExternalInput")
    d_wq = nc.dram_tensor("wq", [C, C], F32, kind="ExternalInput")
    d_wk = nc.dram_tensor("wk", [C, 512], F32, kind="ExternalInput")
    d_wv = nc.dram_tensor("wv", [C, 512], F32, kind="ExternalInput")
    d_wo = nc.dram_tensor("wo", [C, C], F32, kind="ExternalInput")
    d_bq = nc.dram_tensor("bqc", [128, CCH], F32, kind="ExternalInput")
    d_bk = nc.dram_tensor("bkc", [128, NG], F32, kind="ExternalInput")
    d_bv = nc.dram_tensor("bvr", [1, 512], F32, kind="ExternalInput")
    d_bo = nc.dram_tensor("boc", [128, CCH], F32, kind="ExternalInput")
    d_gateT = nc.dram_tensor("gateT", [C, NE], F32, kind="ExternalInput")
    d_wguT = nc.dram_tensor("wguT", [NE, C, 2 * DFF], F32,
                            kind="ExternalInput")
    d_wdT = nc.dram_tensor("wdT", [NE, DFF, C], F32, kind="ExternalInput")
    d_simB = nc.dram_tensor("simB", [128, NE * NE], F32,
                            kind="ExternalInput")
    d_masks = nc.dram_tensor("masks", [128, NQT * 5 * 128], BF16,
                             kind="ExternalInput")
    d_sinkb = nc.dram_tensor("sinkb", [4, NQT], F32, kind="ExternalInput")
    d_cos = nc.dram_tensor("cosdup", [128, NKV], F32, kind="ExternalInput")
    d_sin = nc.dram_tensor("sinsgn", [128, NKV], F32, kind="ExternalInput")
    d_id = nc.dram_tensor("ident", [128, 128], F32, kind="ExternalInput")
    d_rot = nc.dram_tensor("rotmat", [128, 128], F32, kind="ExternalInput")
    d_oc = nc.dram_tensor("onescol", [128, 1], F32, kind="ExternalInput")
    d_or = nc.dram_tensor("onesrow", [1, 128], F32, kind="ExternalInput")
    d_out = nc.dram_tensor("out", [1024, C], F32, kind="ExternalOutput")

    FP_TOT = 26072
    BF_TOT = 54272
    stack = contextlib.ExitStack()
    sb_h = stack.enter_context(nc.sbuf_tensor([128, FP_TOT], F32))
    sbf_h = stack.enter_context(nc.sbuf_tensor([128, BF_TOT], BF16))
    ps_h = stack.enter_context(nc.psum_tensor([128, 4096], F32))
    sb = SB(sb_h, FP_TOT)
    sbb = SB(sbf_h, BF_TOT)
    em = Emit()

    PE = Sem(stack, nc, "sPE")
    DV = Sem(stack, nc, "sDV")
    AC = Sem(stack, nc, "sAC")
    qld = DmaQ(stack, nc, "qL", "sync", em, pool=8)
    qst = DmaQ(stack, nc, "qS", "gpsimd", em, pool=4)

    def bank(b, off=0, n=512, p0=0, np_=128):
        assert 0 <= b < 8 and off + n <= 512
        return ps_h[p0:p0 + np_, b * 512 + off:b * 512 + off + n]

    def wide(b, n, p0=0, np_=128):
        return ps_h[p0:p0 + np_, b * 512:b * 512 + n]

    bankdep = [None] * 8

    def bw(*bs):
        return [bankdep[b] for b in bs if bankdep[b] is not None]

    def bset(b, sem):
        bankdep[b] = (sem, sem.n)

    def dma(out_ap, in_ap, waits=(), q=None):
        return (q or qld).issue(out_ap, in_ap, waits)

    A = sb.ap
    Bp = sbb.ap

    for nm, sz in [("c_id", 128), ("c_rot", 128), ("c_oc", 1), ("c_or", 128),
                   ("c_snk", NQT), ("c_bq", CCH), ("c_bo", CCH),
                   ("c_bk", NG), ("c_bv", 512),
                   ("c_cos", NKV), ("c_sin", NKV)]:
        sb.alloc(nm, sz)
    ctoks = [
        dma(A("c_id", 0, 0, 128, 128), d_id[:, :]),
        dma(A("c_rot", 0, 0, 128, 128), d_rot[:, :]),
        dma(A("c_oc", 0, 0, 128, 1), d_oc[:, :]),
        dma(A("c_or", 0, 0, 1, 128), d_or[:, :]),
        dma(A("c_snk", 0, 0, 4, NQT), d_sinkb[:, :]),
        dma(A("c_bq", 0, 0, 128, CCH), d_bq[:, :]),
        dma(A("c_bo", 0, 0, 128, CCH), d_bo[:, :]),
        dma(A("c_bk", 0, 0, 128, NG), d_bk[:, :]),
        dma(A("c_bv", 0, 0, 1, 512), d_bv[:, :]),
        dma(A("c_cos", 0, 0, 128, NKV), d_cos[:, :]),
        dma(A("c_sin", 0, 0, 128, NKV), d_sin[:, :]),
    ]

    sb.alloc("K", NG * NKV)
    sb.alloc("V", NKV_TILES * 512)
    sbb.alloc("xh", CCH * 1024)
    sbb.alloc("xl", CCH * 1024)

    def Kap(g, kvcol, n, p0=0, np_=128):
        return A("K", g * NKV + kvcol, p0, np_, n)

    def Vap(tile, d0, n, p0=0, np_=128):
        return A("V", tile * 512 + d0, p0, np_, n)

    def XH(cc, qcol, n):
        return Bp("xh", cc * 1024 + qcol, 0, 128, n)

    def XL(cc, qcol, n):
        return Bp("xl", cc * 1024 + qcol, 0, 128, n)

    sb.alloc("xtile", C)
    sb.alloc("xts", CCH * 128)
    sb.alloc("wkv", 2048)
    sb.alloc("kraw", 128)
    sb.alloc("rs1", 128)

    xtile_use = 0
    wkv_use = 0
    wchunks = [("k", g) for g in range(NG)] + [("v", vh) for vh in range(4)]
    for t in range(NKV_TILES):
        own = t >= QT0
        oc = (t - QT0) * 128
        tok_x = dma(A("xtile", 0, 0, 128, C), d_x[t * 128:(t + 1) * 128, :],
                    waits=[(PE, xtile_use)])
        for cc in range(CCH):
            b = cc % 2
            em.op("tensor",
                  lambda e, cc=cc, b=b: e.transpose(
                      bank(b, 0, 128),
                      A("xtile", cc * 128, 0, 128, 128),
                      A("c_id", 0, 0, 128, 128)),
                  waits=[tok_x] + ctoks + bw(b), inc=PE)
            em.op("scalar",
                  lambda e, cc=cc, b=b: e.copy(
                      A("xts", cc * 128, 0, 128, 128), bank(b, 0, 128)),
                  waits=[(PE, PE.n)], inc=AC, drain=False)
            if own:
                em.op("scalar",
                      lambda e, cc=cc, oc=oc, b=b: e.copy(
                          XH(cc, oc, 128), bank(b, 0, 128)),
                      inc=AC, drain=False)
                em.op("vector",
                      lambda e, cc=cc, oc=oc, b=b: e.tensor_tensor(
                          XL(cc, oc, 128), bank(b, 0, 128),
                          XH(cc, oc, 128), OP.subtract),
                      waits=[(AC, AC.n)], inc=DV, drain=False)
                bset(b, DV)
            else:
                bset(b, AC)
        xtile_use = PE.n
        xts_ready = AC.n

        for wi, (kind, idx) in enumerate(wchunks):
            wsrc = d_wk if kind == "k" else d_wv
            tok_w = dma(A("wkv", 0, 0, 128, 2048).rearrange(
                "p (cc d) -> p cc d", d=128),
                wsrc[:, idx * 128:(idx + 1) * 128].rearrange(
                    "(cc p) d -> p cc d", p=128),
                waits=[(PE, wkv_use)])
            if kind == "k":
                kb = 2 + wi % 2
                for cc in range(CCH):
                    w = ([tok_w, (AC, xts_ready)] + bw(kb)) if cc == 0 \
                        else ()
                    em.op("tensor",
                          lambda e, cc=cc, kb=kb: e.matmul(
                              bank(kb, 0, 128),
                              A("wkv", cc * 128, 0, 128, 128),
                              A("xts", cc * 128, 0, 128, 128),
                              start=(cc == 0), stop=(cc == CCH - 1)),
                          waits=w, inc=(PE if cc == CCH - 1 else None))
                mmk = PE.n
                em.op("scalar",
                      lambda e, idx=idx, kb=kb: e.activation(
                          A("kraw", 0, 0, 128, 128), bank(kb, 0, 128),
                          AF.Identity, bias=A("c_bk", idx, 0, 128, 1)),
                      waits=[(PE, mmk), (DV, DV.n)] + ctoks, inc=AC,
                      drain=False)
                bset(kb, AC)
                rb = 4 + wi % 2
                em.op("tensor",
                      lambda e, rb=rb: e.matmul(
                          bank(rb, 0, 128), A("c_rot", 0, 0, 128, 128),
                          A("kraw", 0, 0, 128, 128), start=True, stop=True),
                      waits=[(AC, AC.n)] + bw(rb), inc=PE)
                rotk = PE.n
                em.op("vector",
                      lambda e, t=t: e.tensor_mul(
                          A("rs1", 0, 0, 128, 128), A("kraw", 0, 0, 128, 128),
                          A("c_cos", t * 128, 0, 128, 128)),
                      waits=[(AC, AC.n)] + ctoks, drain=False)
                em.op("vector",
                      lambda e, idx=idx, t=t, rb=rb: e.tensor_mul(
                          Kap(idx, t * 128, 128), bank(rb, 0, 128),
                          A("c_sin", t * 128, 0, 128, 128)),
                      waits=[(PE, rotk)])
                em.op("vector",
                      lambda e, idx=idx, t=t: e.tensor_add(
                          Kap(idx, t * 128, 128), Kap(idx, t * 128, 128),
                          A("rs1", 0, 0, 128, 128)),
                      inc=DV)
                bset(rb, DV)
            else:
                vb = 6 + wi % 2
                for cc in range(CCH):
                    w = ([tok_w, (AC, xts_ready)] + bw(vb)) if cc == 0 \
                        else ()
                    em.op("tensor",
                          lambda e, cc=cc, vb=vb: e.matmul(
                              bank(vb, 0, 128),
                              A("xts", cc * 128, 0, 128, 128),
                              A("wkv", cc * 128, 0, 128, 128),
                              start=(cc == 0), stop=False),
                          waits=w)
                em.op("tensor",
                      lambda e, idx=idx, vb=vb: e.matmul(
                          bank(vb, 0, 128), A("c_or", 0, 0, 1, 128),
                          A("c_bv", idx * 128, 0, 1, 128),
                          start=False, stop=True),
                      inc=PE)
                em.op("scalar",
                      lambda e, t=t, idx=idx, vb=vb: e.copy(
                          Vap(t, idx * 128, 128), bank(vb, 0, 128)),
                      waits=[(PE, PE.n)], inc=AC, drain=False)
                bset(vb, AC)
            wkv_use = PE.n

    sb.release("xtile", "xts", "wkv", "kraw", "rs1", "c_cos", "c_sin")
    fenceB = [(PE, PE.n), (DV, DV.n), (AC, AC.n)]

    sbb.alloc("maskb", NQT * 5 * 128)
    sb.alloc("Qr1", CCH * 128)
    sb.alloc("xrc", CCH * 128)
    sb.alloc("attnT", CCH * 128)
    sb.alloc("PT", 5 * 128)
    sb.alloc("PTs", 128)
    sb.alloc("wb", 2048)
    sb.alloc("qs1", 128)
    sb.alloc("qs2", 128)
    sb.alloc("sqs", 128)
    sb.alloc("bcx", 256)
    sb.alloc("c_cosq", 1024)
    sb.alloc("c_sinq", 1024)

    tok_masks = dma(Bp("maskb", 0, 0, 128, NQT * 5 * 128), d_masks[:, :],
                    waits=fenceB)
    ctoks = ctoks + [
        dma(A("c_cosq", 0, 0, 128, 1024), d_cos[:, QT0 * 128:NKV],
            waits=fenceB),
        dma(A("c_sinq", 0, 0, 128, 1024), d_sin[:, QT0 * 128:NKV],
            waits=fenceB),
    ]

    wb_use = 0
    for qt in range(NQT):
        oc = qt * 128

        for cc in range(CCH):
            em.op("vector",
                  lambda e, cc=cc, oc=oc: e.tensor_add(
                      A("xrc", cc * 128, 0, 128, 128),
                      XH(cc, oc, 128), XL(cc, oc, 128)),
                  inc=(DV if cc == CCH - 1 else None), drain=False)
        xrc_ready = DV.n

        for dc in range(CCH):
            tok_wb = dma(A("wb", 0, 0, 128, 2048).rearrange(
                "p (cc d) -> p cc d", d=128),
                d_wq[:, dc * 128:(dc + 1) * 128].rearrange(
                    "(cc p) d -> p cc d", p=128),
                waits=[(PE, wb_use)] + fenceB)
            qb = dc % 2
            for cc in range(CCH):
                w = ([tok_wb, (DV, xrc_ready)] + bw(qb)) if cc == 0 else ()
                em.op("tensor",
                      lambda e, cc=cc, qb=qb: e.matmul(
                          bank(qb, 0, 128),
                          A("wb", cc * 128, 0, 128, 128),
                          A("xrc", cc * 128, 0, 128, 128),
                          start=(cc == 0), stop=(cc == CCH - 1)),
                      waits=w, inc=(PE if cc == CCH - 1 else None))
            wb_use = PE.n
            mmq = PE.n
            em.op("scalar",
                  lambda e, dc=dc, qb=qb: e.activation(
                      A("sqs", 0, 0, 128, 128), bank(qb, 0, 128),
                      AF.Identity, bias=A("c_bq", dc, 0, 128, 1)),
                  waits=[(PE, mmq), (DV, DV.n)] + ctoks, inc=AC,
                  drain=False)
            bset(qb, AC)
            rb = 2 + dc % 2
            em.op("tensor",
                  lambda e, rb=rb: e.matmul(
                      bank(rb, 0, 128), A("c_rot", 0, 0, 128, 128),
                      A("sqs", 0, 0, 128, 128), start=True, stop=True),
                  waits=[(AC, AC.n)] + bw(rb), inc=PE)
            rotq = PE.n
            em.op("vector",
                  lambda e, qt=qt: e.tensor_mul(
                      A("qs1", 0, 0, 128, 128), A("sqs", 0, 0, 128, 128),
                      A("c_cosq", qt * 128, 0, 128, 128)),
                  waits=[(AC, AC.n)], drain=False)
            em.op("vector",
                  lambda e, rb=rb, qt=qt: e.tensor_mul(
                      A("qs2", 0, 0, 128, 128), bank(rb, 0, 128),
                      A("c_sinq", qt * 128, 0, 128, 128)),
                  waits=[(PE, rotq)], drain=False)
            em.op("vector",
                  lambda e, dc=dc: e.tensor_add(
                      A("Qr1", dc * 128, 0, 128, 128),
                      A("qs1", 0, 0, 128, 128), A("qs2", 0, 0, 128, 128)),
                  inc=DV)
            bset(rb, DV)
        q_ready = DV.n

        for g in range(NG):
            for hh in range(GS):
                gh = g * GS + hh
                pp = gh % 2
                bA, bB, bav, bdn = 2 * pp, 2 * pp + 1, 4 + pp, 6 + pp
                for s in range(5):
                    tb, toff = (bA, s * 128) if s < 4 else (bB, 0)
                    w = [(DV, q_ready), tok_masks] + bw(bA, bB) \
                        if s == 0 else ()
                    em.op("tensor",
                          lambda e, s=s, g=g, gh=gh, qt=qt, tb=tb,
                          toff=toff: e.matmul(
                              bank(tb, toff, 128),
                              Kap(g, (1 + qt + s) * 128, 128),
                              A("Qr1", gh * 128, 0, 128, 128),
                              start=True, stop=True),
                          waits=w)
                em.op("tensor",
                      lambda e, g=g, gh=gh, bB=bB: e.matmul(
                          bank(bB, 128, 128, 0, 4),
                          Kap(g, 0, 4),
                          A("Qr1", gh * 128, 0, 128, 128),
                          start=True, stop=True),
                      inc=PE)
                smm = PE.n
                em.op("vector",
                      lambda e, bA=bA, qt=qt: e.tensor_tensor(
                          wide(bA, 640),
                          wide(bA, 640),
                          Bp("maskb", qt * 640, 0, 128, 640), OP.min),
                      waits=[(PE, smm), (AC, AC.n)], inc=DV, drain=False)
                em.op("scalar",
                      lambda e, bA=bA: e.activation(
                          A("PT", 0, 0, 128, 640),
                          wide(bA, 640), AF.Exp, scale=SCALE),
                      waits=[(DV, DV.n)], inc=AC, drain=False)
                em.op("scalar",
                      lambda e, qt=qt, bB=bB: e.activation(
                          A("PTs", 0, 0, 4, 128), bank(bB, 128, 128, 0, 4),
                          AF.Exp, bias=A("c_snk", qt, 0, 4, 1), scale=SCALE),
                      waits=[(PE, smm)], inc=AC, drain=False)
                exp_done = AC.n
                bset(bA, AC)
                bset(bB, AC)
                for s in range(5):
                    w = ([(AC, exp_done)] + bw(bav, bdn)) if s == 0 else ()
                    em.op("tensor",
                          lambda e, s=s, g=g, qt=qt, bav=bav: e.matmul(
                              bank(bav, 0, 128),
                              Vap(1 + qt + s, g * 128, 128),
                              A("PT", s * 128, 0, 128, 128),
                              start=(s == 0), stop=False),
                          waits=w)
                em.op("tensor",
                      lambda e, g=g, bav=bav: e.matmul(
                          bank(bav, 0, 128),
                          Vap(0, g * 128, 128, 0, 4),
                          A("PTs", 0, 0, 4, 128),
                          start=False, stop=True))
                for s in range(5):
                    em.op("tensor",
                          lambda e, s=s, bdn=bdn: e.matmul(
                              bank(bdn, 0, 128, 0, 1),
                              A("c_oc", 0, 0, 128, 1),
                              A("PT", s * 128, 0, 128, 128),
                              start=(s == 0), stop=False))
                em.op("tensor",
                      lambda e, bdn=bdn: e.matmul(
                          bank(bdn, 0, 128, 0, 1),
                          A("c_oc", 0, 0, 4, 1),
                          A("PTs", 0, 0, 4, 128),
                          start=False, stop=True),
                      inc=PE)
                av_done = PE.n
                em.op("vector",
                      lambda e, bdn=bdn: e.reciprocal(
                          A("bcx", 0, 0, 1, 128), bank(bdn, 0, 128, 0, 1)),
                      waits=[(PE, av_done)], inc=DV, drain=False)
                em.op("tensor",
                      lambda e, bdn=bdn: e.matmul(
                          bank(bdn, 128, 128), A("c_or", 0, 0, 1, 128),
                          A("bcx", 0, 0, 1, 128), start=True, stop=True),
                      waits=[(DV, DV.n)], inc=PE)
                em.op("vector",
                      lambda e, bdn=bdn: e.tensor_copy(
                          A("bcx", 128, 0, 128, 128), bank(bdn, 128, 128)),
                      waits=[(PE, PE.n)], drain=False)
                em.op("vector",
                      lambda e, gh=gh, bav=bav: e.tensor_mul(
                          A("attnT", gh * 128, 0, 128, 128),
                          bank(bav, 0, 128), A("bcx", 128, 0, 128, 128)),
                      inc=DV)
                bset(bav, DV)
                bset(bdn, DV)
        attn_done = DV.n

        for mt in range(CCH):
            tok_wb = dma(A("wb", 0, 0, 128, 2048).rearrange(
                "p (cc d) -> p cc d", d=128),
                d_wo[:, mt * 128:(mt + 1) * 128].rearrange(
                    "(cc p) d -> p cc d", p=128),
                waits=[(PE, wb_use)] + fenceB)
            ob = mt % 2
            for cc in range(CCH):
                w = ([tok_wb, (DV, attn_done)] + bw(ob)) if cc == 0 \
                    else ()
                em.op("tensor",
                      lambda e, cc=cc, ob=ob: e.matmul(
                          bank(ob, 0, 128),
                          A("wb", cc * 128, 0, 128, 128),
                          A("attnT", cc * 128, 0, 128, 128),
                          start=(cc == 0), stop=(cc == CCH - 1)),
                      waits=w, inc=(PE if cc == CCH - 1 else None))
            wb_use = PE.n
            em.op("vector",
                  lambda e, mt=mt, ob=ob: e.scalar_tensor_tensor(
                      A("Qr1", mt * 128, 0, 128, 128), bank(ob, 0, 128),
                      A("c_bo", mt, 0, 128, 1),
                      A("xrc", mt * 128, 0, 128, 128),
                      OP.add, OP.add),
                  waits=[(PE, PE.n)], inc=DV, drain=False)
            bset(ob, DV)
        r1_done = DV.n

        for cc in range(CCH):
            w = ([(DV, r1_done)] + bw(4)) if cc == 0 else ()
            em.op("tensor",
                  lambda e, cc=cc: e.matmul(
                      bank(4, 0, 128, 0, 1), A("c_oc", 0, 0, 128, 1),
                      A("Qr1", cc * 128, 0, 128, 128),
                      start=(cc == 0), stop=(cc == CCH - 1)),
                  waits=w, inc=(PE if cc == CCH - 1 else None))
        for cc in range(CCH):
            em.op("scalar",
                  lambda e, cc=cc: e.activation(
                      A("sqs", 0, 0, 128, 128), A("Qr1", cc * 128, 0, 128, 128),
                      AF.Square),
                  waits=([(DV, r1_done), (PE, PE.n)] if cc == 0
                         else [(PE, PE.n)]),
                  inc=AC, drain=False)
            w = ([(AC, AC.n)] + bw(5)) if cc == 0 else [(AC, AC.n)]
            em.op("tensor",
                  lambda e, cc=cc: e.matmul(
                      bank(5, 0, 128, 0, 1), A("c_oc", 0, 0, 128, 1),
                      A("sqs", 0, 0, 128, 128),
                      start=(cc == 0), stop=(cc == CCH - 1)),
                  waits=w, inc=PE)
        stat_mm = PE.n
        em.op("vector",
              lambda e: e.tensor_scalar(
                  A("qs1", 0, 0, 1, 128), bank(4, 0, 128, 0, 1),
                  1.0 / C, None, OP.mult),
              waits=[(PE, stat_mm)])
        em.op("vector",
              lambda e: e.tensor_scalar(
                  A("qs2", 0, 0, 1, 128), bank(5, 0, 128, 0, 1),
                  1.0 / C, None, OP.mult))
        em.op("vector",
              lambda e: e.tensor_mul(
                  A("sqs", 0, 0, 1, 128), A("qs1", 0, 0, 1, 128),
                  A("qs1", 0, 0, 1, 128)))
        em.op("vector",
              lambda e: e.tensor_sub(
                  A("qs2", 0, 0, 1, 128), A("qs2", 0, 0, 1, 128),
                  A("sqs", 0, 0, 1, 128)))
        em.op("vector",
              lambda e: e.tensor_scalar(
                  A("qs2", 0, 0, 1, 128), A("qs2", 0, 0, 1, 128),
                  EPS, None, OP.add))
        em.op("vector",
              lambda e: e.reciprocal(
                  A("sqs", 0, 0, 1, 128), A("qs2", 0, 0, 1, 128)),
              inc=DV)
        em.op("scalar",
              lambda e: e.activation(
                  A("qs2", 0, 0, 1, 128), A("sqs", 0, 0, 1, 128), AF.Sqrt),
              waits=[(DV, DV.n)], inc=AC)
        em.op("vector",
              lambda e: e.tensor_mul(
                  A("sqs", 0, 0, 1, 128), A("qs1", 0, 0, 1, 128),
                  A("qs2", 0, 0, 1, 128)),
              waits=[(AC, AC.n)])
        em.op("vector",
              lambda e: e.tensor_scalar(
                  A("sqs", 0, 0, 1, 128), A("sqs", 0, 0, 1, 128),
                  -1.0, None, OP.mult),
              inc=DV)
        ln_rows = DV.n
        em.op("tensor",
              lambda e: e.matmul(
                  bank(4, 0, 128), A("c_or", 0, 0, 1, 128),
                  A("qs2", 0, 0, 1, 128), start=True, stop=True),
              waits=[(DV, ln_rows)] + bw(4))
        em.op("tensor",
              lambda e: e.matmul(
                  bank(5, 0, 128), A("c_or", 0, 0, 1, 128),
                  A("sqs", 0, 0, 1, 128), start=True, stop=True),
              waits=bw(5), inc=PE)
        em.op("vector",
              lambda e: e.tensor_copy(A("bcx", 0, 0, 128, 128),
                                      bank(4, 0, 128)),
              waits=[(PE, PE.n)])
        em.op("vector",
              lambda e: e.tensor_copy(A("bcx", 128, 0, 128, 128),
                                      bank(5, 0, 128)),
              inc=DV)
        bset(4, DV)
        bset(5, DV)
        for cc in range(CCH):
            em.op("vector",
                  lambda e, cc=cc: e.tensor_mul(
                      A("qs1", 0, 0, 128, 128),
                      A("Qr1", cc * 128, 0, 128, 128),
                      A("bcx", 0, 0, 128, 128)),
                  waits=([(DV, DV.n)] if cc == 0 else ()))
            em.op("vector",
                  lambda e, cc=cc: e.tensor_add(
                      A("Qr1", cc * 128, 0, 128, 128),
                      A("qs1", 0, 0, 128, 128),
                      A("bcx", 128, 0, 128, 128)),
                  inc=DV)
            em.op("scalar",
                  lambda e, cc=cc, oc=oc: e.copy(
                      XH(cc, oc, 128), A("Qr1", cc * 128, 0, 128, 128)),
                  waits=[(DV, DV.n)], inc=AC, drain=False)
            em.op("vector",
                  lambda e, cc=cc, oc=oc: e.tensor_tensor(
                      XL(cc, oc, 128), A("Qr1", cc * 128, 0, 128, 128),
                      XH(cc, oc, 128), OP.subtract),
                  waits=[(AC, AC.n)], inc=DV, drain=False)

    sb.release("Qr1", "xrc", "attnT", "PT", "PTs", "wb",
               "qs1", "qs2", "sqs", "bcx", "c_cosq", "c_sinq", "K", "V")
    sbb.release("maskb")
    z_done = DV.n
    fenceC = [(PE, PE.n), (DV, DV.n), (AC, AC.n)]

    sb.alloc("probs", NE)
    sb.alloc("prim", NE)
    sb.alloc("sec", NE)
    sb.alloc("rw", NE)
    sb.alloc("bs", NE)
    sb.alloc("comb", NQT * NE)
    sb.alloc("mx", 8)
    sb.alloc("rcol", 4)
    sb.alloc("rscr", 2 * NE)
    sb.alloc("cwrow", 512)
    sb.alloc("gtmp", 512)
    sb.alloc("c_simB", NE * NE)
    sbb.alloc("gateh", CCH * NE)
    sbb.alloc("gatel", CCH * NE)

    tok_simB = dma(A("c_simB", 0, 0, 128, NE * NE), d_simB[:, :],
                   waits=fenceC)

    tok_gt = dma(A("gtmp", 0, 0, 128, CCH * NE).rearrange(
        "p (cc ee) -> p cc ee", ee=NE),
        d_gateT[:, :].rearrange("(cc p) ee -> p cc ee", p=128),
        waits=fenceC)
    em.op("scalar",
          lambda e: e.copy(Bp("gateh", 0, 0, 128, CCH * NE),
                           A("gtmp", 0, 0, 128, CCH * NE)),
          waits=[tok_gt] + fenceC, inc=AC)
    em.op("vector",
          lambda e: e.tensor_tensor(
              Bp("gatel", 0, 0, 128, CCH * NE),
              A("gtmp", 0, 0, 128, CCH * NE),
              Bp("gateh", 0, 0, 128, CCH * NE), OP.subtract),
          waits=[(AC, AC.n)] + fenceC, inc=DV)
    gate_ready = DV.n

    for r in range(NQT):
        oc = r * 128
        lb = r % 2
        passes = [("h", "h"), ("h", "l"), ("l", "h")]
        nmm = 3 * CCH
        k = 0
        for cc in range(CCH):
            for zs, gs_ in passes:
                w = ([(DV, max(z_done, gate_ready))] + bw(lb)) if k == 0 \
                    else ()
                em.op("tensor",
                      lambda e, cc=cc, oc=oc, lb=lb, zs=zs, gs_=gs_,
                      k=k: e.matmul(
                          bank(lb, 0, NE),
                          (XH if zs == "h" else XL)(cc, oc, 128),
                          Bp("gateh" if gs_ == "h" else "gatel",
                             cc * NE, 0, 128, NE),
                          start=(k == 0), stop=(k == nmm - 1)),
                      waits=w, inc=(PE if k == nmm - 1 else None))
                k += 1
        em.op("vector",
              lambda e, lb=lb: e.reduce_max(
                  A("rcol", 0, 0, 128, 1), bank(lb, 0, NE), AX.X,
                  negate=True),
              waits=[(PE, PE.n)], inc=DV)
        em.op("scalar",
              lambda e, lb=lb: e.activation(
                  A("probs", 0, 0, 128, NE), bank(lb, 0, NE), AF.Exp,
                  bias=A("rcol", 0, 0, 128, 1),
                  accum_out=A("rcol", 1, 0, 128, 1)),
              waits=[(DV, DV.n)], inc=AC)
        bset(lb, AC)
        em.op("vector",
              lambda e: e.reciprocal(A("rcol", 2, 0, 128, 1),
                                     A("rcol", 1, 0, 128, 1)),
              waits=[(AC, AC.n)])
        em.op("vector",
              lambda e: e.tensor_scalar(
                  A("probs", 0, 0, 128, NE), A("probs", 0, 0, 128, NE),
                  A("rcol", 2, 0, 128, 1), None, OP.mult))
        em.op("vector",
              lambda e: e.max(A("mx", 0, 0, 128, 8),
                              A("probs", 0, 0, 128, NE)))
        em.op("vector",
              lambda e: e.reduce_sum(
                  A("rcol", 0, 0, 128, 1), A("mx", 0, 0, 128, 8), AX.X))
        em.op("vector",
              lambda e: e.reciprocal(A("rcol", 1, 0, 128, 1),
                                     A("rcol", 0, 0, 128, 1)))
        em.op("vector",
              lambda e: e.tensor_scalar(
                  A("prim", 0, 0, 128, NE), A("probs", 0, 0, 128, NE),
                  A("mx", 3, 0, 128, 1), None, OP.is_ge))
        em.op("vector",
              lambda e: e.tensor_scalar(
                  A("rscr", 0, 0, 128, NE), A("probs", 0, 0, 128, NE),
                  A("mx", 7, 0, 128, 1), None, OP.is_ge))
        em.op("vector",
              lambda e: e.tensor_sub(
                  A("sec", 0, 0, 128, NE), A("rscr", 0, 0, 128, NE),
                  A("prim", 0, 0, 128, NE)))
        em.op("vector",
              lambda e: e.tensor_scalar(
                  A("rw", 0, 0, 128, NE), A("probs", 0, 0, 128, NE),
                  A("rcol", 1, 0, 128, 1), None, OP.mult))
        em.op("vector",
              lambda e: e.tensor_mul(
                  A("rw", 0, 0, 128, NE), A("rw", 0, 0, 128, NE),
                  A("rscr", 0, 0, 128, NE)))
        for s in range(NE):
            em.op("vector",
                  lambda e, s=s: e.tensor_mul(
                      A("rscr", NE, 0, 128, NE), A("prim", 0, 0, 128, NE),
                      A("c_simB", s * NE, 0, 128, NE)),
                  waits=([tok_simB] if s == 0 else ()))
            em.op("vector",
                  lambda e, s=s: e.reduce_max(
                      A("bs", s, 0, 128, 1),
                      A("rscr", NE, 0, 128, NE), AX.X))
        em.op("vector",
              lambda e: e.tensor_scalar(
                  A("rscr", 0, 0, 128, NE), A("bs", 0, 0, 128, NE),
                  0.5, None, OP.is_lt))
        em.op("vector",
              lambda e: e.tensor_mul(
                  A("rscr", NE, 0, 128, NE), A("sec", 0, 0, 128, NE),
                  A("rscr", 0, 0, 128, NE)))
        em.op("vector",
              lambda e: e.tensor_add(
                  A("rscr", NE, 0, 128, NE), A("rscr", NE, 0, 128, NE),
                  A("prim", 0, 0, 128, NE)))
        em.op("vector",
              lambda e, r=r: e.tensor_mul(
                  A("comb", r * NE, 0, 128, NE), A("rw", 0, 0, 128, NE),
                  A("rscr", NE, 0, 128, NE)))
        em.op("vector",
              lambda e: e.tensor_scalar(
                  A("rscr", 0, 0, 128, NE), A("rscr", 0, 0, 128, NE),
                  -1.0, 1.0, OP.mult, OP.add))
        em.op("vector",
              lambda e: e.tensor_mul(
                  A("sec", 0, 0, 128, NE), A("sec", 0, 0, 128, NE),
                  A("rw", 0, 0, 128, NE)))
        em.op("vector",
              lambda e: e.tensor_mul(
                  A("sec", 0, 0, 128, NE), A("sec", 0, 0, 128, NE),
                  A("rscr", 0, 0, 128, NE)))
        for s in range(NE):
            em.op("vector",
                  lambda e, s=s: e.tensor_scalar(
                      A("rscr", 0, 0, 128, NE), A("c_simB", s * NE, 0, 128, NE),
                      A("bs", s, 0, 128, 1), None, OP.is_ge))
            em.op("vector",
                  lambda e: e.tensor_mul(
                      A("rscr", 0, 0, 128, NE), A("rscr", 0, 0, 128, NE),
                      A("prim", 0, 0, 128, NE)))
            em.op("vector",
                  lambda e, r=r, s=s: e.scalar_tensor_tensor(
                      A("comb", r * NE, 0, 128, NE),
                      A("rscr", 0, 0, 128, NE),
                      A("sec", s, 0, 128, 1),
                      A("comb", r * NE, 0, 128, NE),
                      OP.mult, OP.add),
                  inc=(DV if s == NE - 1 else None))
    routing_done = DV.n

    sb.release("probs", "prim", "rw", "bs", "mx", "rcol", "rscr", "sec",
               "gtmp", "c_simB")
    fenceC = [(PE, PE.n), (DV, DV.n), (AC, AC.n)]

    sb.alloc("wgu", 2 * CCH * 128)
    sb.alloc("wd", 6 * 256)
    sbb.alloc("wguh", 2 * CCH * 128)
    sbb.alloc("wgul", 2 * CCH * 128)
    sbb.alloc("wdh", 2 * 6 * 256)
    sbb.alloc("wdl", 2 * 6 * 256)
    sb.alloc("sg", 6 * 512)
    sb.alloc("hsc", 512)
    sbb.alloc("hhh", 6 * 512)
    sbb.alloc("hhl", 6 * 512)
    sb.alloc("y", CCH * 512)
    sb.alloc("cw", 512)
    sb.alloc("zrc", 512)
    sb.alloc("sq2", 512)
    sb.alloc("lrow", 2 * 512)
    sb.alloc("lbc", 1024)
    sb.alloc("otm", C)

    def ZH(cc, ih, n=512):
        return Bp("xh", cc * 1024 + ih * 512, 0, 128, n)

    def ZL(cc, ih, n=512):
        return Bp("xl", cc * 1024 + ih * 512, 0, 128, n)

    wgu_use = [0, 0]
    wd_use = [0, 0]
    wgu_f32_free = [[], []]
    wd_f32_free = [[]]
    tok_st = None
    gu_splits = {}
    wd_splits = {}
    for ih in range(2):
        def emit_gu_split(e_ix, ft):
            par = ft % 2
            tok_g = dma(
                A("wgu", par * CCH * 128, 0, 128, CCH * 128).rearrange(
                    "p (cc f) -> p cc f", f=128),
                d_wguT[e_ix, :, ft * 128:(ft + 1) * 128].rearrange(
                    "(cc p) f -> p cc f", p=128),
                waits=wgu_f32_free[par] + fenceC)
            em.op("scalar",
                  lambda e, par=par: e.copy(
                      Bp("wguh", par * CCH * 128, 0, 128, CCH * 128),
                      A("wgu", par * CCH * 128, 0, 128, CCH * 128)),
                  waits=[tok_g, (PE, wgu_use[par])], inc=AC,
                  drain=False)
            cast_n = AC.n
            em.op("vector",
                  lambda e, par=par: e.tensor_tensor(
                      Bp("wgul", par * CCH * 128, 0, 128, CCH * 128),
                      A("wgu", par * CCH * 128, 0, 128, CCH * 128),
                      Bp("wguh", par * CCH * 128, 0, 128, CCH * 128),
                      OP.subtract),
                  waits=[(AC, cast_n)], inc=DV, drain=False)
            sub_n = DV.n
            wgu_f32_free[par] = [(AC, cast_n), (DV, sub_n)]
            gu_splits[(e_ix, ft)] = (cast_n, sub_n)

        def emit_wd_split(e_ix, mp):
            par = mp % 2
            tok_d = dma(
                A("wd", 0, 0, 128, 6 * 256).rearrange(
                    "p (fc m) -> p fc m", m=256),
                d_wdT[e_ix, :, mp * 256:(mp + 1) * 256].rearrange(
                    "(fc p) m -> p fc m", p=128),
                waits=wd_f32_free[0] + fenceC)
            em.op("scalar",
                  lambda e, par=par: e.copy(
                      Bp("wdh", par * 6 * 256, 0, 128, 6 * 256),
                      A("wd", 0, 0, 128, 6 * 256)),
                  waits=[tok_d, (PE, wd_use[par])], inc=AC,
                  drain=False)
            dcast_n = AC.n
            em.op("vector",
                  lambda e, par=par: e.tensor_tensor(
                      Bp("wdl", par * 6 * 256, 0, 128, 6 * 256),
                      A("wd", 0, 0, 128, 6 * 256),
                      Bp("wdh", par * 6 * 256, 0, 128, 6 * 256),
                      OP.subtract),
                  waits=[(AC, dcast_n)], inc=DV, drain=False)
            dsub_n = DV.n
            wd_f32_free[0] = [(AC, dcast_n), (DV, dsub_n)]
            wd_splits[(e_ix, mp)] = (dcast_n, dsub_n)

        def emit_down(e_ix, h_done):
            for mp in range(8):
                par = mp % 2
                dcast_n, dsub_n = wd_splits.pop((e_ix, mp))
                for sub in range(2):
                    mt = mp * 2 + sub
                    db = 2 + mt % 2
                    k = 0
                    for fc in range(6):
                        for ws, hs in [("h", "h"), ("h", "l"), ("l", "h")]:
                            w = ([(AC, dcast_n), (DV, max(dsub_n, h_done))]
                                 + bw(db)) if k == 0 else ()
                            em.op("tensor",
                                  lambda e, fc=fc, par=par, sub=sub, db=db,
                                  ws=ws, hs=hs, k=k: e.matmul(
                                      bank(db, 0, 512),
                                      Bp("wdh" if ws == "h" else "wdl",
                                         par * 6 * 256 + fc * 256
                                         + sub * 128, 0, 128, 128),
                                      Bp("hhh" if hs == "h" else "hhl",
                                         fc * 512, 0, 128, 512),
                                      start=(k == 0), stop=(k == 17)),
                                  waits=w, inc=(PE if k == 17 else None))
                            k += 1
                    wd_use[par] = PE.n
                    if sub == 1 and mp + 2 < 8:
                        emit_wd_split(e_ix, mp + 2)
                    if e_ix == 0:
                        em.op("vector",
                              lambda e, mt=mt, db=db: e.tensor_copy(
                                  A("y", mt * 512, 0, 128, 512),
                                  bank(db, 0, 512)),
                              waits=[(PE, PE.n)], inc=DV, drain=False)
                    else:
                        em.op("vector",
                              lambda e, mt=mt, db=db: e.tensor_add(
                                  A("y", mt * 512, 0, 128, 512),
                                  A("y", mt * 512, 0, 128, 512),
                                  bank(db, 0, 512)),
                              waits=[(PE, PE.n)], inc=DV, drain=False)
                    bset(db, DV)

        pending = None
        emit_gu_split(0, 0)
        emit_gu_split(0, 1)
        for e_ix in range(NE):
            for rr in range(4):
                em.op("tensor",
                      lambda e, rr=rr, e_ix=e_ix, ih=ih: e.transpose(
                          bank(4, rr * 128, 128, 0, 1),
                          A("comb", (ih * 4 + rr) * NE + e_ix, 0, 128, 1),
                          A("c_id", 0, 0, 128, 128)),
                      waits=([(DV, routing_done)] + bw(4)) if rr == 0
                      else (), inc=(PE if rr == 3 else None))
            em.op("scalar",
                  lambda e: e.copy(A("cwrow", 0, 0, 1, 512),
                                   bank(4, 0, 512, 0, 1)),
                  waits=[(PE, PE.n)], inc=AC, drain=False)
            em.op("tensor",
                  lambda e: e.matmul(
                      bank(4, 0, 512), A("c_or", 0, 0, 1, 128),
                      A("cwrow", 0, 0, 1, 512),
                      start=True, stop=True),
                  waits=[(AC, AC.n)], inc=PE)
            em.op("vector",
                  lambda e: e.tensor_copy(A("cw", 0, 0, 128, 512),
                                          bank(4, 0, 512)),
                  waits=[(PE, PE.n)], inc=DV, drain=False)
            bset(4, DV)
            cw_done = DV.n
            for ft in range(12):
                par = ft % 2
                cast_n, sub_n = gu_splits.pop((e_ix, ft))
                gb = ft % 2
                k = 0
                for cc in range(CCH):
                    for ws, zs in [("h", "h"), ("h", "l"), ("l", "h")]:
                        w = ([(AC, cast_n), (DV, sub_n)] + bw(gb)) \
                            if k == 0 else ()
                        em.op("tensor",
                              lambda e, cc=cc, par=par, gb=gb, ih=ih,
                              ws=ws, zs=zs, k=k: e.matmul(
                                  bank(gb, 0, 512),
                                  Bp("wguh" if ws == "h" else "wgul",
                                     par * CCH * 128 + cc * 128,
                                     0, 128, 128),
                                  (ZH if zs == "h" else ZL)(cc, ih),
                                  start=(k == 0), stop=(k == 3 * CCH - 1)),
                              waits=w,
                              inc=(PE if k == 3 * CCH - 1 else None))
                        k += 1
                wgu_use[par] = PE.n
                gmm = PE.n
                if ft + 2 < 12:
                    emit_gu_split(e_ix, ft + 2)
                elif e_ix + 1 < NE:
                    emit_gu_split(e_ix + 1, ft + 2 - 12)
                if ft < 6:
                    em.op("scalar",
                          lambda e, ft=ft, gb=gb: e.activation(
                              A("sg", ft * 512, 0, 128, 512),
                              bank(gb, 0, 512), AF.Silu),
                          waits=[(PE, gmm), (DV, DV.n)], inc=AC,
                          drain=False)
                    bset(gb, AC)
                else:
                    kk = ft - 6
                    em.op("vector",
                          lambda e, kk=kk, gb=gb: e.tensor_mul(
                              A("hsc", 0, 0, 128, 512),
                              A("sg", kk * 512, 0, 128, 512),
                              bank(gb, 0, 512)),
                          waits=[(PE, gmm), (AC, AC.n), (DV, cw_done)])
                    em.op("vector",
                          lambda e: e.tensor_mul(
                              A("hsc", 0, 0, 128, 512),
                              A("hsc", 0, 0, 128, 512),
                              A("cw", 0, 0, 128, 512)),
                          inc=DV)
                    bset(gb, DV)
                    em.op("scalar",
                          lambda e, kk=kk: e.copy(
                              Bp("hhh", kk * 512, 0, 128, 512),
                              A("hsc", 0, 0, 128, 512)),
                          waits=[(DV, DV.n), (PE, gmm)], inc=AC,
                          drain=False)
                    em.op("vector",
                          lambda e, kk=kk: e.tensor_tensor(
                              Bp("hhl", kk * 512, 0, 128, 512),
                              A("hsc", 0, 0, 128, 512),
                              Bp("hhh", kk * 512, 0, 128, 512),
                              OP.subtract),
                          waits=[(AC, AC.n), (PE, gmm)], inc=DV)
                if ft == 0:
                    emit_wd_split(e_ix if pending is None else pending[0],
                                  0) if False else None
                if ft == 1 and pending is not None:
                    emit_wd_split(pending[0], 0)
                    emit_wd_split(pending[0], 1)
                    emit_down(*pending)
                    pending = None
            pending = (e_ix, DV.n)
        emit_wd_split(pending[0], 0)
        emit_wd_split(pending[0], 1)
        emit_down(*pending)
        pending = None

        for mt in range(CCH):
            em.op("vector",
                  lambda e, mt=mt, ih=ih: e.tensor_add(
                      A("zrc", 0, 0, 128, 512), ZH(mt, ih), ZL(mt, ih)))
            em.op("vector",
                  lambda e, mt=mt: e.tensor_add(
                      A("y", mt * 512, 0, 128, 512),
                      A("y", mt * 512, 0, 128, 512),
                      A("zrc", 0, 0, 128, 512)),
                  inc=DV)
        r2_done = DV.n
        for cc in range(CCH):
            w = ([(DV, r2_done)] + bw(5)) if cc == 0 else ()
            em.op("tensor",
                  lambda e, cc=cc: e.matmul(
                      bank(5, 0, 512, 0, 1), A("c_oc", 0, 0, 128, 1),
                      A("y", cc * 512, 0, 128, 512),
                      start=(cc == 0), stop=(cc == CCH - 1)),
                  waits=w, inc=(PE if cc == CCH - 1 else None))
        for cc in range(CCH):
            em.op("scalar",
                  lambda e, cc=cc: e.activation(
                      A("sq2", 0, 0, 128, 512), A("y", cc * 512, 0, 128, 512),
                      AF.Square),
                  waits=([(DV, r2_done), (PE, PE.n)] if cc == 0
                         else [(PE, PE.n)]),
                  inc=AC, drain=False)
            w = ([(AC, AC.n)] + bw(6)) if cc == 0 else [(AC, AC.n)]
            em.op("tensor",
                  lambda e, cc=cc: e.matmul(
                      bank(6, 0, 512, 0, 1), A("c_oc", 0, 0, 128, 1),
                      A("sq2", 0, 0, 128, 512),
                      start=(cc == 0), stop=(cc == CCH - 1)),
                  waits=w, inc=PE)
        st2 = PE.n
        em.op("vector",
              lambda e: e.tensor_scalar(
                  A("lrow", 0, 0, 1, 512), bank(5, 0, 512, 0, 1),
                  1.0 / C, None, OP.mult),
              waits=[(PE, st2)])
        em.op("vector",
              lambda e: e.tensor_scalar(
                  A("lrow", 512, 0, 1, 512), bank(6, 0, 512, 0, 1),
                  1.0 / C, None, OP.mult))
        em.op("vector",
              lambda e: e.tensor_mul(
                  A("zrc", 0, 0, 1, 512), A("lrow", 0, 0, 1, 512),
                  A("lrow", 0, 0, 1, 512)))
        em.op("vector",
              lambda e: e.tensor_sub(
                  A("lrow", 512, 0, 1, 512), A("lrow", 512, 0, 1, 512),
                  A("zrc", 0, 0, 1, 512)))
        em.op("vector",
              lambda e: e.tensor_scalar(
                  A("lrow", 512, 0, 1, 512), A("lrow", 512, 0, 1, 512),
                  EPS, None, OP.add))
        em.op("vector",
              lambda e: e.reciprocal(
                  A("zrc", 0, 0, 1, 512), A("lrow", 512, 0, 1, 512)),
              inc=DV)
        em.op("scalar",
              lambda e: e.activation(
                  A("lrow", 512, 0, 1, 512), A("zrc", 0, 0, 1, 512),
                  AF.Sqrt),
              waits=[(DV, DV.n)], inc=AC)
        em.op("vector",
              lambda e: e.tensor_mul(
                  A("zrc", 0, 0, 1, 512), A("lrow", 0, 0, 1, 512),
                  A("lrow", 512, 0, 1, 512)),
              waits=[(AC, AC.n)])
        em.op("vector",
              lambda e: e.tensor_scalar(
                  A("zrc", 0, 0, 1, 512), A("zrc", 0, 0, 1, 512),
                  -1.0, None, OP.mult),
              inc=DV)
        lnr = DV.n
        em.op("tensor",
              lambda e: e.matmul(
                  bank(5, 0, 512), A("c_or", 0, 0, 1, 128),
                  A("lrow", 512, 0, 1, 512), start=True, stop=True),
              waits=[(DV, lnr)] + bw(5))
        em.op("tensor",
              lambda e: e.matmul(
                  bank(6, 0, 512), A("c_or", 0, 0, 1, 128),
                  A("zrc", 0, 0, 1, 512), start=True, stop=True),
              waits=bw(6), inc=PE)
        em.op("vector",
              lambda e: e.tensor_copy(A("lbc", 0, 0, 128, 512),
                                      bank(5, 0, 512)),
              waits=[(PE, PE.n)])
        em.op("vector",
              lambda e: e.tensor_copy(A("lbc", 512, 0, 128, 512),
                                      bank(6, 0, 512)),
              inc=DV)
        bset(5, DV)
        bset(6, DV)
        bc_done = DV.n
        for j in range(4):
            for mt in range(CCH):
                em.op("vector",
                      lambda e, mt=mt, j=j: e.tensor_mul(
                          A("sq2", 0, 0, 128, 128),
                          A("y", mt * 512 + j * 128, 0, 128, 128),
                          A("lbc", j * 128, 0, 128, 128)),
                      waits=(([(DV, bc_done), (PE, PE.n)]
                              + ([tok_st] if tok_st else []))
                             if mt == 0 else [(PE, PE.n)]), drain=False)
                em.op("vector",
                      lambda e, mt=mt, j=j: e.tensor_add(
                          A("sq2", 0, 0, 128, 128),
                          A("sq2", 0, 0, 128, 128),
                          A("lbc", 512 + j * 128, 0, 128, 128)),
                      inc=DV)
                tb = 7 if mt % 2 == 0 else 0
                em.op("tensor",
                      lambda e, tb=tb: e.transpose(
                          bank(tb, 0, 128), A("sq2", 0, 0, 128, 128),
                          A("c_id", 0, 0, 128, 128)),
                      waits=[(DV, DV.n)] + bw(tb), inc=PE)
                em.op("scalar",
                      lambda e, mt=mt, tb=tb: e.copy(
                          A("otm", mt * 128, 0, 128, 128), bank(tb, 0, 128)),
                      waits=[(PE, PE.n)], inc=AC, drain=False)
                bset(tb, AC)
            r0 = ih * 512 + j * 128
            tok_st = dma(d_out[r0:r0 + 128, :], A("otm", 0, 0, 128, C),
                         waits=[(AC, AC.n)], q=qst)

    em.op("gpsimd", lambda e: e.nop(),
          waits=[(s, s.n) for s in qst.sems])

    with nc.Block() as blk:
        blk.sync(lambda e: em.replay("sync", e))
        blk.tensor(lambda e: em.replay("tensor", e))
        blk.vector(lambda e: em.replay("vector", e))
        blk.scalar(lambda e: em.replay("scalar", e))
        blk.gpsimd(lambda e: em.replay("gpsimd", e))

    nc.dbg_sb = sb
    nc.dbg_sbb = sbb
    nc.dbg_ps = ps_h
    return nc


def kernel(**inputs):
    import ml_dtypes

    x = np.asarray(inputs["L0_attn_x"], dtype=np.float32)
    wq = np.asarray(inputs["L0_attn_Wq"], dtype=np.float32)
    bq = np.asarray(inputs["L0_attn_bq"], dtype=np.float32)
    wk = np.asarray(inputs["L0_attn_Wk"], dtype=np.float32)
    bk = np.asarray(inputs["L0_attn_bk"], dtype=np.float32)
    wv = np.asarray(inputs["L0_attn_Wv"], dtype=np.float32)
    bv = np.asarray(inputs["L0_attn_bv"], dtype=np.float32)
    wo = np.asarray(inputs["L0_attn_Wo"], dtype=np.float32)
    bo = np.asarray(inputs["L0_attn_bo"], dtype=np.float32)
    gate = np.asarray(inputs["L0_ffn_gate_weight"], dtype=np.float32)
    gup = np.asarray(inputs["L0_ffn_gate_up_proj"], dtype=np.float32)
    dwn = np.asarray(inputs["L0_ffn_down_proj"], dtype=np.float32)
    sim = np.asarray(inputs["L0_ffn_similarity"], dtype=np.float32)

    wguT = np.ascontiguousarray(np.transpose(gup, (0, 2, 1)))
    wdT = np.ascontiguousarray(np.transpose(dwn, (0, 2, 1)))
    gateT = np.ascontiguousarray(gate.T)
    simd = sim.copy()
    np.fill_diagonal(simd, 1.0)
    simB = np.ascontiguousarray(
        np.tile(simd.reshape(1, NE * NE), (128, 1)))

    cos_dup, sin_sgn = _rope_tables()
    ident = np.eye(128, dtype=np.float32)
    rot = np.zeros((128, 128), dtype=np.float32)
    for m in range(128):
        rot[(m + 64) % 128, m] = 1.0
    onescol = np.ones((128, 1), dtype=np.float32)
    onesrow = np.ones((1, 128), dtype=np.float32)
    bqc = np.ascontiguousarray(bq.reshape(CCH, 128).T)
    boc = np.ascontiguousarray(bo.reshape(CCH, 128).T)
    bkc = np.ascontiguousarray(bk.reshape(NG, 128).T)
    bvr = np.ascontiguousarray(bv.reshape(1, 512))

    in_maps = []
    for core in range(NCORES):
        b, h = core // 2, core % 2
        _, glob = _glob_tokens(h)
        gl = np.clip(glob, 0, T - 1)
        in_maps.append({
            "xloc": _xloc_for_core(x[b], h),
            "wq": wq, "wk": wk, "wv": wv, "wo": wo,
            "bqc": bqc, "bkc": bkc, "bvr": bvr, "boc": boc,
            "gateT": gateT, "wguT": wguT, "wdT": wdT, "simB": simB,
            "masks": _masks_for_core(h).astype(ml_dtypes.bfloat16),
            "sinkb": _sinkbias_for_core(h),
            "cosdup": np.ascontiguousarray(cos_dup[:, gl]),
            "sinsgn": np.ascontiguousarray(sin_sgn[:, gl]),
            "ident": ident, "rotmat": rot,
            "onescol": onescol, "onesrow": onesrow,
        })

    nc = build_program()
    res = run_bass_kernel_spmd(nc, in_maps, core_ids=list(range(NCORES)))
    out = np.empty((B, T, C), dtype=np.float32)
    for core in range(NCORES):
        b, h = core // 2, core % 2
        out[b, h * 1024:(h + 1) * 1024] = res.results[core]["out"]
    return out


# version 10
# speedup vs baseline: 1.6364x; 1.0368x over previous
import contextlib

import numpy as np

import concourse.bass as bass
import concourse.mybir as mybir
from concourse.bass_utils import run_bass_kernel_spmd

F32 = mybir.dt.float32
BF16 = mybir.dt.bfloat16
AF = mybir.ActivationFunctionType
OP = mybir.AluOpType
AX = mybir.AxisListType

B, T, C = 4, 2048, 2048
NG, GS, DH = 4, 4, 128
WIN, SINK = 512, 4
SCALE = float(DH) ** -0.5
NE = 32
DFF = 768
EPS = 1e-5
CCH = C // 128
NKV_TILES = 13
NKV = NKV_TILES * 128
NQT = 8
QT0 = 5
BIG = 1.0e38
NCORES = 8


def _rope_tables():
    half = DH // 2
    inv = 1.0 / (10000.0 ** (np.arange(half, dtype=np.float64) * 2.0 / DH))
    t = np.arange(T, dtype=np.float64)
    ang = inv[:, None] * t[None, :]
    cos = np.cos(ang)
    sin = np.sin(ang)
    cos_dup = np.concatenate([cos, cos], 0).astype(np.float32)
    sin_sgn = np.concatenate([-sin, sin], 0).astype(np.float32)
    return cos_dup, sin_sgn


def _glob_tokens(h):
    w0 = h * 1024 - 512
    g = np.empty(NKV, dtype=np.int64)
    g[0:128] = np.arange(128)
    g[128:] = w0 + np.arange(12 * 128)
    return w0, g


def _masks_for_core(h):
    w0, _ = _glob_tokens(h)
    out = np.empty((128, NQT * 5 * 128), dtype=np.float32)
    j = np.arange(128)[:, None]
    i = np.arange(128)[None, :]
    for qt in range(NQT):
        qg = h * 1024 + 128 * qt
        for s in range(5):
            kg = w0 + 128 * (qt + s)
            kj = kg + j
            qi = qg + i
            keep = (kj >= 0) & (qi >= kj) & (((qi - kj) <= WIN) | (kj < SINK))
            out[:, (qt * 5 + s) * 128:(qt * 5 + s + 1) * 128] = np.where(
                keep, BIG, -BIG).astype(np.float32)
    return out


def _sinkbias_for_core(h):
    w0, _ = _glob_tokens(h)
    out = np.zeros((4, NQT), dtype=np.float32)
    for qt in range(NQT):
        if any(w0 + 128 * (qt + s) == 0 for s in range(5)):
            out[:, qt] = -BIG
    return out


def _xloc_for_core(x_b, h):
    rows = [x_b[0:128]]
    if h == 0:
        rows.append(np.zeros((512, C), dtype=x_b.dtype))
        rows.append(x_b[0:1024])
    else:
        rows.append(x_b[512:2048])
    return np.ascontiguousarray(np.concatenate(rows, axis=0))


class Sem:
    def __init__(self, stack, nc, name):
        self.h = stack.enter_context(nc.semaphore(name))
        self.n = 0


class SB:

    def __init__(self, handle, cols):
        self.h = handle
        self.cols = cols
        self.freelist = [(0, cols)]
        self.live = {}
        self.dead = {}

    def alloc(self, name, n):
        assert name not in self.live and name not in self.dead
        for idx, (c0, sz) in enumerate(self.freelist):
            if sz >= n:
                self.live[name] = (c0, n)
                if sz == n:
                    self.freelist.pop(idx)
                else:
                    self.freelist[idx] = (c0 + n, sz - n)
                return
        raise RuntimeError(f"SBUF arena overflow: {name} needs {n} cols; "
                           f"free={self.freelist}")

    def release(self, *names):
        for name in names:
            c0, n = self.live.pop(name)
            self.dead[name] = (c0, n)
            self.freelist.append((c0, n))
        self.freelist.sort()
        merged = []
        for c, s in self.freelist:
            if merged and merged[-1][0] + merged[-1][1] == c:
                merged[-1] = (merged[-1][0], merged[-1][1] + s)
            else:
                merged.append((c, s))
        self.freelist = merged

    def ap(self, name, col=0, p0=0, np_=128, n=1):
        c0, tot = self.live.get(name) or self.dead[name]
        assert col + n <= tot, (name, col, n, tot)
        return self.h[p0:p0 + np_, c0 + col:c0 + col + n]


class Emit:

    ENGINES = ("sync", "tensor", "vector", "scalar", "gpsimd")

    def __init__(self):
        self.ops = {k: [] for k in self.ENGINES}
        self.lastwait = {k: {} for k in self.ENGINES}

    def op(self, engine, fn, waits=(), inc=None, inc_by=1, drain=True):
        waits = [(s, v) for (s, v) in waits if v > 0]
        for s, v in waits:
            assert v <= s.n, f"wait on future count {v} > {s.n}"
        if inc is not None:
            inc.n += inc_by
        lw = self.lastwait[engine]
        eff = []
        for s, v in waits:
            if lw.get(id(s), 0) < v:
                lw[id(s)] = v
                eff.append((s, v))

        def run(eng, fn=fn, eff=tuple(eff), inc=inc, inc_by=inc_by):
            for s, v in eff:
                eng.wait_ge(s.h, v)
            inst = fn(eng)
            if inc is not None:
                inst.then_inc(inc.h, inc_by)

        self.ops[engine].append((run, drain))

    def replay(self, engine, eng):
        needs_drain = engine in ("vector", "scalar")
        for i, (fn, dr) in enumerate(self.ops[engine]):
            if needs_drain and dr and i > 0:
                eng.drain()
            fn(eng)


class DmaQ:

    def __init__(self, stack, nc, name, engine, em, pool):
        self.sems = [Sem(stack, nc, f"{name}{i}") for i in range(pool)]
        self.engine = engine
        self.em = em
        self.i = 0

    def issue(self, out_ap, in_ap, waits=()):
        s = self.sems[self.i % len(self.sems)]
        w = list(waits)
        if s.n:
            w.append((s, s.n))
        self.em.op(self.engine,
                   lambda e, o=out_ap, i=in_ap: e.dma_start(out=o, in_=i),
                   waits=w, inc=s, inc_by=16)
        self.i += 1
        return (s, s.n)


def build_program():
    nc = bass.Bass("TRN2", target_bir_lowering=False)

    d_x = nc.dram_tensor("xloc", [NKV, C], F32, kind="ExternalInput")
    d_wq = nc.dram_tensor("wq", [C, C], F32, kind="ExternalInput")
    d_wk = nc.dram_tensor("wk", [C, 512], F32, kind="ExternalInput")
    d_wv = nc.dram_tensor("wv", [C, 512], F32, kind="ExternalInput")
    d_wo = nc.dram_tensor("wo", [C, C], F32, kind="ExternalInput")
    d_bq = nc.dram_tensor("bqc", [128, CCH], F32, kind="ExternalInput")
    d_bk = nc.dram_tensor("bkc", [128, NG], F32, kind="ExternalInput")
    d_bv = nc.dram_tensor("bvr", [1, 512], F32, kind="ExternalInput")
    d_bo = nc.dram_tensor("boc", [128, CCH], F32, kind="ExternalInput")
    d_gateT = nc.dram_tensor("gateT", [C, NE], F32, kind="ExternalInput")
    d_wguT = nc.dram_tensor("wguT", [NE, C, 2 * DFF], F32,
                            kind="ExternalInput")
    d_wdT = nc.dram_tensor("wdT", [NE, DFF, C], F32, kind="ExternalInput")
    d_simB = nc.dram_tensor("simB", [128, NE * NE], F32,
                            kind="ExternalInput")
    d_masks = nc.dram_tensor("masks", [128, NQT * 5 * 128], BF16,
                             kind="ExternalInput")
    d_sinkb = nc.dram_tensor("sinkb", [4, NQT], F32, kind="ExternalInput")
    d_cos = nc.dram_tensor("cosdup", [128, NKV], F32, kind="ExternalInput")
    d_sin = nc.dram_tensor("sinsgn", [128, NKV], F32, kind="ExternalInput")
    d_id = nc.dram_tensor("ident", [128, 128], F32, kind="ExternalInput")
    d_rot = nc.dram_tensor("rotmat", [128, 128], F32, kind="ExternalInput")
    d_oc = nc.dram_tensor("onescol", [128, 1], F32, kind="ExternalInput")
    d_or = nc.dram_tensor("onesrow", [1, 128], F32, kind="ExternalInput")
    d_out = nc.dram_tensor("out", [1024, C], F32, kind="ExternalOutput")

    FP_TOT = 26072
    BF_TOT = 54272
    stack = contextlib.ExitStack()
    sb_h = stack.enter_context(nc.sbuf_tensor([128, FP_TOT], F32))
    sbf_h = stack.enter_context(nc.sbuf_tensor([128, BF_TOT], BF16))
    ps_h = stack.enter_context(nc.psum_tensor([128, 4096], F32))
    sb = SB(sb_h, FP_TOT)
    sbb = SB(sbf_h, BF_TOT)
    em = Emit()

    PE = Sem(stack, nc, "sPE")
    DV = Sem(stack, nc, "sDV")
    AC = Sem(stack, nc, "sAC")
    qld = DmaQ(stack, nc, "qL", "sync", em, pool=8)
    qst = DmaQ(stack, nc, "qS", "gpsimd", em, pool=4)

    def bank(b, off=0, n=512, p0=0, np_=128):
        assert 0 <= b < 8 and off + n <= 512
        return ps_h[p0:p0 + np_, b * 512 + off:b * 512 + off + n]

    def wide(b, n, p0=0, np_=128):
        return ps_h[p0:p0 + np_, b * 512:b * 512 + n]

    bankdep = [None] * 8

    def bw(*bs):
        return [bankdep[b] for b in bs if bankdep[b] is not None]

    def bset(b, sem):
        bankdep[b] = (sem, sem.n)

    def dma(out_ap, in_ap, waits=(), q=None):
        return (q or qld).issue(out_ap, in_ap, waits)

    A = sb.ap
    Bp = sbb.ap

    for nm, sz in [("c_id", 128), ("c_rot", 128), ("c_oc", 1), ("c_or", 128),
                   ("c_snk", NQT), ("c_bq", CCH), ("c_bo", CCH),
                   ("c_bk", NG), ("c_bv", 512),
                   ("c_cos", NKV), ("c_sin", NKV)]:
        sb.alloc(nm, sz)
    ctoks = [
        dma(A("c_id", 0, 0, 128, 128), d_id[:, :]),
        dma(A("c_rot", 0, 0, 128, 128), d_rot[:, :]),
        dma(A("c_oc", 0, 0, 128, 1), d_oc[:, :]),
        dma(A("c_or", 0, 0, 1, 128), d_or[:, :]),
        dma(A("c_snk", 0, 0, 4, NQT), d_sinkb[:, :]),
        dma(A("c_bq", 0, 0, 128, CCH), d_bq[:, :]),
        dma(A("c_bo", 0, 0, 128, CCH), d_bo[:, :]),
        dma(A("c_bk", 0, 0, 128, NG), d_bk[:, :]),
        dma(A("c_bv", 0, 0, 1, 512), d_bv[:, :]),
        dma(A("c_cos", 0, 0, 128, NKV), d_cos[:, :]),
        dma(A("c_sin", 0, 0, 128, NKV), d_sin[:, :]),
    ]

    sb.alloc("K", NG * NKV)
    sb.alloc("V", NKV_TILES * 512)
    sbb.alloc("xh", CCH * 1024)
    sbb.alloc("xl", CCH * 1024)

    def Kap(g, kvcol, n, p0=0, np_=128):
        return A("K", g * NKV + kvcol, p0, np_, n)

    def Vap(tile, d0, n, p0=0, np_=128):
        return A("V", tile * 512 + d0, p0, np_, n)

    def XH(cc, qcol, n):
        return Bp("xh", cc * 1024 + qcol, 0, 128, n)

    def XL(cc, qcol, n):
        return Bp("xl", cc * 1024 + qcol, 0, 128, n)

    sb.alloc("xtile", C)
    sb.alloc("xts", CCH * 128)
    sb.alloc("wkv", 2 * 2048)
    sb.alloc("kraw", 128)
    sb.alloc("rs1", 128)

    xtile_use = 0
    wkv_use = [0, 0]
    wchunks = [("k", g) for g in range(NG)] + [("v", vh) for vh in range(4)]
    for t in range(NKV_TILES):
        own = t >= QT0
        oc = (t - QT0) * 128
        tok_x = dma(A("xtile", 0, 0, 128, C), d_x[t * 128:(t + 1) * 128, :],
                    waits=[(PE, xtile_use)])
        for cc in range(CCH):
            b = cc % 2
            em.op("tensor",
                  lambda e, cc=cc, b=b: e.transpose(
                      bank(b, 0, 128),
                      A("xtile", cc * 128, 0, 128, 128),
                      A("c_id", 0, 0, 128, 128)),
                  waits=[tok_x] + ctoks + bw(b), inc=PE)
            em.op("scalar",
                  lambda e, cc=cc, b=b: e.copy(
                      A("xts", cc * 128, 0, 128, 128), bank(b, 0, 128)),
                  waits=[(PE, PE.n)], inc=AC, drain=False)
            if own:
                em.op("scalar",
                      lambda e, cc=cc, oc=oc, b=b: e.copy(
                          XH(cc, oc, 128), bank(b, 0, 128)),
                      inc=AC, drain=False)
                em.op("vector",
                      lambda e, cc=cc, oc=oc, b=b: e.tensor_tensor(
                          XL(cc, oc, 128), bank(b, 0, 128),
                          XH(cc, oc, 128), OP.subtract),
                      waits=[(AC, AC.n)], inc=DV, drain=False)
                bset(b, DV)
            else:
                bset(b, AC)
        xtile_use = PE.n
        xts_ready = AC.n

        for wi, (kind, idx) in enumerate(wchunks):
            wpar = wi % 2
            wsrc = d_wk if kind == "k" else d_wv
            tok_w = dma(A("wkv", wpar * 2048, 0, 128, 2048).rearrange(
                "p (cc d) -> p cc d", d=128),
                wsrc[:, idx * 128:(idx + 1) * 128].rearrange(
                    "(cc p) d -> p cc d", p=128),
                waits=[(PE, wkv_use[wpar])])
            if kind == "k":
                kb = 2 + wi % 2
                for cc in range(CCH):
                    w = ([tok_w, (AC, xts_ready)] + bw(kb)) if cc == 0 \
                        else ()
                    em.op("tensor",
                          lambda e, cc=cc, kb=kb, wpar=wpar: e.matmul(
                              bank(kb, 0, 128),
                              A("wkv", wpar * 2048 + cc * 128, 0, 128, 128),
                              A("xts", cc * 128, 0, 128, 128),
                              start=(cc == 0), stop=(cc == CCH - 1)),
                          waits=w, inc=(PE if cc == CCH - 1 else None))
                mmk = PE.n
                em.op("scalar",
                      lambda e, idx=idx, kb=kb: e.activation(
                          A("kraw", 0, 0, 128, 128), bank(kb, 0, 128),
                          AF.Identity, bias=A("c_bk", idx, 0, 128, 1)),
                      waits=[(PE, mmk), (DV, DV.n)] + ctoks, inc=AC,
                      drain=False)
                bset(kb, AC)
                rb = 4 + wi % 2
                em.op("tensor",
                      lambda e, rb=rb: e.matmul(
                          bank(rb, 0, 128), A("c_rot", 0, 0, 128, 128),
                          A("kraw", 0, 0, 128, 128), start=True, stop=True),
                      waits=[(AC, AC.n)] + bw(rb), inc=PE)
                rotk = PE.n
                em.op("vector",
                      lambda e, t=t: e.tensor_mul(
                          A("rs1", 0, 0, 128, 128), A("kraw", 0, 0, 128, 128),
                          A("c_cos", t * 128, 0, 128, 128)),
                      waits=[(AC, AC.n)] + ctoks, drain=False)
                em.op("vector",
                      lambda e, idx=idx, t=t, rb=rb: e.tensor_mul(
                          Kap(idx, t * 128, 128), bank(rb, 0, 128),
                          A("c_sin", t * 128, 0, 128, 128)),
                      waits=[(PE, rotk)])
                em.op("vector",
                      lambda e, idx=idx, t=t: e.tensor_add(
                          Kap(idx, t * 128, 128), Kap(idx, t * 128, 128),
                          A("rs1", 0, 0, 128, 128)),
                      inc=DV)
                bset(rb, DV)
            else:
                vb = 6 + wi % 2
                for cc in range(CCH):
                    w = ([tok_w, (AC, xts_ready)] + bw(vb)) if cc == 0 \
                        else ()
                    em.op("tensor",
                          lambda e, cc=cc, vb=vb, wpar=wpar: e.matmul(
                              bank(vb, 0, 128),
                              A("xts", cc * 128, 0, 128, 128),
                              A("wkv", wpar * 2048 + cc * 128, 0, 128, 128),
                              start=(cc == 0), stop=False),
                          waits=w)
                em.op("tensor",
                      lambda e, idx=idx, vb=vb: e.matmul(
                          bank(vb, 0, 128), A("c_or", 0, 0, 1, 128),
                          A("c_bv", idx * 128, 0, 1, 128),
                          start=False, stop=True),
                      inc=PE)
                em.op("scalar",
                      lambda e, t=t, idx=idx, vb=vb: e.copy(
                          Vap(t, idx * 128, 128), bank(vb, 0, 128)),
                      waits=[(PE, PE.n)], inc=AC, drain=False)
                bset(vb, AC)
            wkv_use[wpar] = PE.n

    sb.release("xtile", "xts", "wkv", "kraw", "rs1", "c_cos", "c_sin")
    fenceB = [(PE, PE.n), (DV, DV.n), (AC, AC.n)]

    sbb.alloc("maskb", NQT * 5 * 128)
    sb.alloc("Qr1", CCH * 128)
    sb.alloc("xrc", CCH * 128)
    sb.alloc("attnT", CCH * 128)
    sb.alloc("PT", 5 * 128)
    sb.alloc("PTs", 128)
    sb.alloc("wb", 2 * 2048)
    sb.alloc("ctab", 256)
    sb.alloc("qs1", 128)
    sb.alloc("sqs", 128)
    sb.alloc("bcx", 256)

    tok_masks = dma(Bp("maskb", 0, 0, 128, NQT * 5 * 128), d_masks[:, :],
                    waits=fenceB)


    wb_use = [0, 0]
    rope_use = 0
    for qt in range(NQT):
        oc = qt * 128
        tok_tab = dma(A("ctab", 0, 0, 128, 128),
                      d_cos[:, (QT0 + qt) * 128:(QT0 + qt + 1) * 128],
                      waits=[(DV, rope_use)] + (fenceB if qt == 0 else []))
        tok_tab2 = dma(A("ctab", 128, 0, 128, 128),
                       d_sin[:, (QT0 + qt) * 128:(QT0 + qt + 1) * 128],
                       waits=[(DV, rope_use)] + (fenceB if qt == 0 else []))

        for cc in range(CCH):
            em.op("vector",
                  lambda e, cc=cc, oc=oc: e.tensor_add(
                      A("xrc", cc * 128, 0, 128, 128),
                      XH(cc, oc, 128), XL(cc, oc, 128)),
                  inc=(DV if cc == CCH - 1 else None), drain=False)
        xrc_ready = DV.n

        for dc in range(CCH):
            wbp = dc % 2
            tok_wb = dma(A("wb", wbp * 2048, 0, 128, 2048).rearrange(
                "p (cc d) -> p cc d", d=128),
                d_wq[:, dc * 128:(dc + 1) * 128].rearrange(
                    "(cc p) d -> p cc d", p=128),
                waits=[(PE, wb_use[wbp])] + fenceB)
            qb = dc % 2
            for cc in range(CCH):
                w = ([tok_wb, (DV, xrc_ready)] + bw(qb)) if cc == 0 else ()
                em.op("tensor",
                      lambda e, cc=cc, qb=qb, wbp=wbp: e.matmul(
                          bank(qb, 0, 128),
                          A("wb", wbp * 2048 + cc * 128, 0, 128, 128),
                          A("xrc", cc * 128, 0, 128, 128),
                          start=(cc == 0), stop=(cc == CCH - 1)),
                      waits=w, inc=(PE if cc == CCH - 1 else None))
            wb_use[wbp] = PE.n
            mmq = PE.n
            em.op("scalar",
                  lambda e, dc=dc, qb=qb: e.activation(
                      A("sqs", 0, 0, 128, 128), bank(qb, 0, 128),
                      AF.Identity, bias=A("c_bq", dc, 0, 128, 1)),
                  waits=[(PE, mmq), (DV, DV.n)] + ctoks, inc=AC,
                  drain=False)
            bset(qb, AC)
            rb = 2 + dc % 2
            em.op("tensor",
                  lambda e, rb=rb: e.matmul(
                      bank(rb, 0, 128), A("c_rot", 0, 0, 128, 128),
                      A("sqs", 0, 0, 128, 128), start=True, stop=True),
                  waits=[(AC, AC.n)] + bw(rb), inc=PE)
            rotq = PE.n
            em.op("vector",
                  lambda e: e.tensor_mul(
                      A("qs1", 0, 0, 128, 128), A("sqs", 0, 0, 128, 128),
                      A("ctab", 0, 0, 128, 128)),
                  waits=[(AC, AC.n), tok_tab, tok_tab2], drain=False)
            em.op("vector",
                  lambda e, rb=rb, dc=dc: e.tensor_mul(
                      A("Qr1", dc * 128, 0, 128, 128), bank(rb, 0, 128),
                      A("ctab", 128, 0, 128, 128)),
                  waits=[(PE, rotq)], drain=False)
            em.op("vector",
                  lambda e, dc=dc: e.tensor_add(
                      A("Qr1", dc * 128, 0, 128, 128),
                      A("Qr1", dc * 128, 0, 128, 128),
                      A("qs1", 0, 0, 128, 128)),
                  inc=DV)
            bset(rb, DV)
        q_ready = DV.n
        rope_use = DV.n

        for g in range(NG):
            for hh in range(GS):
                gh = g * GS + hh
                pp = gh % 2
                bA, bB, bav, bdn = 2 * pp, 2 * pp + 1, 4 + pp, 6 + pp
                for s in range(5):
                    tb, toff = (bA, s * 128) if s < 4 else (bB, 0)
                    w = [(DV, q_ready), tok_masks] + bw(bA, bB) \
                        if s == 0 else ()
                    em.op("tensor",
                          lambda e, s=s, g=g, gh=gh, qt=qt, tb=tb,
                          toff=toff: e.matmul(
                              bank(tb, toff, 128),
                              Kap(g, (1 + qt + s) * 128, 128),
                              A("Qr1", gh * 128, 0, 128, 128),
                              start=True, stop=True),
                          waits=w)
                em.op("tensor",
                      lambda e, g=g, gh=gh, bB=bB: e.matmul(
                          bank(bB, 128, 128, 0, 4),
                          Kap(g, 0, 4),
                          A("Qr1", gh * 128, 0, 128, 128),
                          start=True, stop=True),
                      inc=PE)
                smm = PE.n
                em.op("vector",
                      lambda e, bA=bA, qt=qt: e.tensor_tensor(
                          wide(bA, 640),
                          wide(bA, 640),
                          Bp("maskb", qt * 640, 0, 128, 640), OP.min),
                      waits=[(PE, smm), (AC, AC.n)], inc=DV, drain=False)
                em.op("scalar",
                      lambda e, bA=bA: e.activation(
                          A("PT", 0, 0, 128, 640),
                          wide(bA, 640), AF.Exp, scale=SCALE),
                      waits=[(DV, DV.n)], inc=AC, drain=False)
                em.op("scalar",
                      lambda e, qt=qt, bB=bB: e.activation(
                          A("PTs", 0, 0, 4, 128), bank(bB, 128, 128, 0, 4),
                          AF.Exp, bias=A("c_snk", qt, 0, 4, 1), scale=SCALE),
                      waits=[(PE, smm)], inc=AC, drain=False)
                exp_done = AC.n
                bset(bA, AC)
                bset(bB, AC)
                for s in range(5):
                    w = ([(AC, exp_done)] + bw(bav, bdn)) if s == 0 else ()
                    em.op("tensor",
                          lambda e, s=s, g=g, qt=qt, bav=bav: e.matmul(
                              bank(bav, 0, 128),
                              Vap(1 + qt + s, g * 128, 128),
                              A("PT", s * 128, 0, 128, 128),
                              start=(s == 0), stop=False),
                          waits=w)
                em.op("tensor",
                      lambda e, g=g, bav=bav: e.matmul(
                          bank(bav, 0, 128),
                          Vap(0, g * 128, 128, 0, 4),
                          A("PTs", 0, 0, 4, 128),
                          start=False, stop=True))
                for s in range(5):
                    em.op("tensor",
                          lambda e, s=s, bdn=bdn: e.matmul(
                              bank(bdn, 0, 128, 0, 1),
                              A("c_oc", 0, 0, 128, 1),
                              A("PT", s * 128, 0, 128, 128),
                              start=(s == 0), stop=False))
                em.op("tensor",
                      lambda e, bdn=bdn: e.matmul(
                          bank(bdn, 0, 128, 0, 1),
                          A("c_oc", 0, 0, 4, 1),
                          A("PTs", 0, 0, 4, 128),
                          start=False, stop=True),
                      inc=PE)
                av_done = PE.n
                em.op("vector",
                      lambda e, bdn=bdn: e.reciprocal(
                          A("bcx", 0, 0, 1, 128), bank(bdn, 0, 128, 0, 1)),
                      waits=[(PE, av_done)], inc=DV, drain=False)
                em.op("tensor",
                      lambda e, bdn=bdn: e.matmul(
                          bank(bdn, 128, 128), A("c_or", 0, 0, 1, 128),
                          A("bcx", 0, 0, 1, 128), start=True, stop=True),
                      waits=[(DV, DV.n)], inc=PE)
                em.op("vector",
                      lambda e, bdn=bdn: e.tensor_copy(
                          A("bcx", 128, 0, 128, 128), bank(bdn, 128, 128)),
                      waits=[(PE, PE.n)], drain=False)
                em.op("vector",
                      lambda e, gh=gh, bav=bav: e.tensor_mul(
                          A("attnT", gh * 128, 0, 128, 128),
                          bank(bav, 0, 128), A("bcx", 128, 0, 128, 128)),
                      inc=DV)
                bset(bav, DV)
                bset(bdn, DV)
        attn_done = DV.n

        for mt in range(CCH):
            wbp = mt % 2
            tok_wb = dma(A("wb", wbp * 2048, 0, 128, 2048).rearrange(
                "p (cc d) -> p cc d", d=128),
                d_wo[:, mt * 128:(mt + 1) * 128].rearrange(
                    "(cc p) d -> p cc d", p=128),
                waits=[(PE, wb_use[wbp])] + fenceB)
            ob = mt % 2
            for cc in range(CCH):
                w = ([tok_wb, (DV, attn_done)] + bw(ob)) if cc == 0 \
                    else ()
                em.op("tensor",
                      lambda e, cc=cc, ob=ob, wbp=wbp: e.matmul(
                          bank(ob, 0, 128),
                          A("wb", wbp * 2048 + cc * 128, 0, 128, 128),
                          A("attnT", cc * 128, 0, 128, 128),
                          start=(cc == 0), stop=(cc == CCH - 1)),
                      waits=w, inc=(PE if cc == CCH - 1 else None))
            wb_use[wbp] = PE.n
            em.op("vector",
                  lambda e, mt=mt, ob=ob: e.scalar_tensor_tensor(
                      A("Qr1", mt * 128, 0, 128, 128), bank(ob, 0, 128),
                      A("c_bo", mt, 0, 128, 1),
                      A("xrc", mt * 128, 0, 128, 128),
                      OP.add, OP.add),
                  waits=[(PE, PE.n)], inc=DV, drain=False)
            bset(ob, DV)
        r1_done = DV.n

        for cc in range(CCH):
            w = ([(DV, r1_done)] + bw(4)) if cc == 0 else ()
            em.op("tensor",
                  lambda e, cc=cc: e.matmul(
                      bank(4, 0, 128, 0, 1), A("c_oc", 0, 0, 128, 1),
                      A("Qr1", cc * 128, 0, 128, 128),
                      start=(cc == 0), stop=(cc == CCH - 1)),
                  waits=w, inc=(PE if cc == CCH - 1 else None))
        for cc in range(CCH):
            em.op("scalar",
                  lambda e, cc=cc: e.activation(
                      A("sqs", 0, 0, 128, 128), A("Qr1", cc * 128, 0, 128, 128),
                      AF.Square),
                  waits=([(DV, r1_done), (PE, PE.n)] if cc == 0
                         else [(PE, PE.n)]),
                  inc=AC, drain=False)
            w = ([(AC, AC.n)] + bw(5)) if cc == 0 else [(AC, AC.n)]
            em.op("tensor",
                  lambda e, cc=cc: e.matmul(
                      bank(5, 0, 128, 0, 1), A("c_oc", 0, 0, 128, 1),
                      A("sqs", 0, 0, 128, 128),
                      start=(cc == 0), stop=(cc == CCH - 1)),
                  waits=w, inc=PE)
        stat_mm = PE.n
        em.op("vector",
              lambda e: e.tensor_scalar(
                  A("qs1", 0, 0, 1, 128), bank(4, 0, 128, 0, 1),
                  1.0 / C, None, OP.mult),
              waits=[(PE, stat_mm)])
        em.op("vector",
              lambda e: e.tensor_scalar(
                  A("bcx", 0, 0, 1, 128), bank(5, 0, 128, 0, 1),
                  1.0 / C, None, OP.mult))
        em.op("vector",
              lambda e: e.tensor_mul(
                  A("sqs", 0, 0, 1, 128), A("qs1", 0, 0, 1, 128),
                  A("qs1", 0, 0, 1, 128)))
        em.op("vector",
              lambda e: e.tensor_sub(
                  A("bcx", 0, 0, 1, 128), A("bcx", 0, 0, 1, 128),
                  A("sqs", 0, 0, 1, 128)))
        em.op("vector",
              lambda e: e.tensor_scalar(
                  A("bcx", 0, 0, 1, 128), A("bcx", 0, 0, 1, 128),
                  EPS, None, OP.add))
        em.op("vector",
              lambda e: e.reciprocal(
                  A("sqs", 0, 0, 1, 128), A("bcx", 0, 0, 1, 128)),
              inc=DV)
        em.op("scalar",
              lambda e: e.activation(
                  A("bcx", 0, 0, 1, 128), A("sqs", 0, 0, 1, 128), AF.Sqrt),
              waits=[(DV, DV.n)], inc=AC)
        em.op("vector",
              lambda e: e.tensor_mul(
                  A("sqs", 0, 0, 1, 128), A("qs1", 0, 0, 1, 128),
                  A("bcx", 0, 0, 1, 128)),
              waits=[(AC, AC.n)])
        em.op("vector",
              lambda e: e.tensor_scalar(
                  A("sqs", 0, 0, 1, 128), A("sqs", 0, 0, 1, 128),
                  -1.0, None, OP.mult),
              inc=DV)
        ln_rows = DV.n
        em.op("tensor",
              lambda e: e.matmul(
                  bank(4, 0, 128), A("c_or", 0, 0, 1, 128),
                  A("bcx", 0, 0, 1, 128), start=True, stop=True),
              waits=[(DV, ln_rows)] + bw(4))
        em.op("tensor",
              lambda e: e.matmul(
                  bank(5, 0, 128), A("c_or", 0, 0, 1, 128),
                  A("sqs", 0, 0, 1, 128), start=True, stop=True),
              waits=bw(5), inc=PE)
        em.op("vector",
              lambda e: e.tensor_copy(A("bcx", 0, 0, 128, 128),
                                      bank(4, 0, 128)),
              waits=[(PE, PE.n)])
        em.op("vector",
              lambda e: e.tensor_copy(A("bcx", 128, 0, 128, 128),
                                      bank(5, 0, 128)),
              inc=DV)
        bset(4, DV)
        bset(5, DV)
        for cc in range(CCH):
            em.op("vector",
                  lambda e, cc=cc: e.tensor_mul(
                      A("qs1", 0, 0, 128, 128),
                      A("Qr1", cc * 128, 0, 128, 128),
                      A("bcx", 0, 0, 128, 128)),
                  waits=([(DV, DV.n)] if cc == 0 else ()))
            em.op("vector",
                  lambda e, cc=cc: e.tensor_add(
                      A("Qr1", cc * 128, 0, 128, 128),
                      A("qs1", 0, 0, 128, 128),
                      A("bcx", 128, 0, 128, 128)),
                  inc=DV)
            em.op("scalar",
                  lambda e, cc=cc, oc=oc: e.copy(
                      XH(cc, oc, 128), A("Qr1", cc * 128, 0, 128, 128)),
                  waits=[(DV, DV.n)], inc=AC, drain=False)
            em.op("vector",
                  lambda e, cc=cc, oc=oc: e.tensor_tensor(
                      XL(cc, oc, 128), A("Qr1", cc * 128, 0, 128, 128),
                      XH(cc, oc, 128), OP.subtract),
                  waits=[(AC, AC.n)], inc=DV, drain=False)

    sb.release("Qr1", "xrc", "attnT", "PT", "PTs", "wb",
               "qs1", "sqs", "bcx", "ctab", "K", "V")
    sbb.release("maskb")
    z_done = DV.n
    fenceC = [(PE, PE.n), (DV, DV.n), (AC, AC.n)]

    sb.alloc("probs", NE)
    sb.alloc("prim", NE)
    sb.alloc("sec", NE)
    sb.alloc("rw", NE)
    sb.alloc("bs", NE)
    sb.alloc("comb", NQT * NE)
    sb.alloc("mx", 8)
    sb.alloc("rcol", 4)
    sb.alloc("rscr", 2 * NE)
    sb.alloc("cwrow", 512)
    sb.alloc("gtmp", 512)
    sb.alloc("c_simB", NE * NE)
    sbb.alloc("gateh", CCH * NE)
    sbb.alloc("gatel", CCH * NE)

    tok_simB = dma(A("c_simB", 0, 0, 128, NE * NE), d_simB[:, :],
                   waits=fenceC)

    tok_gt = dma(A("gtmp", 0, 0, 128, CCH * NE).rearrange(
        "p (cc ee) -> p cc ee", ee=NE),
        d_gateT[:, :].rearrange("(cc p) ee -> p cc ee", p=128),
        waits=fenceC)
    em.op("scalar",
          lambda e: e.copy(Bp("gateh", 0, 0, 128, CCH * NE),
                           A("gtmp", 0, 0, 128, CCH * NE)),
          waits=[tok_gt] + fenceC, inc=AC)
    em.op("vector",
          lambda e: e.tensor_tensor(
              Bp("gatel", 0, 0, 128, CCH * NE),
              A("gtmp", 0, 0, 128, CCH * NE),
              Bp("gateh", 0, 0, 128, CCH * NE), OP.subtract),
          waits=[(AC, AC.n)] + fenceC, inc=DV)
    gate_ready = DV.n

    for r in range(NQT):
        oc = r * 128
        lb = r % 2
        passes = [("h", "h"), ("h", "l"), ("l", "h")]
        nmm = 3 * CCH
        k = 0
        for cc in range(CCH):
            for zs, gs_ in passes:
                w = ([(DV, max(z_done, gate_ready))] + bw(lb)) if k == 0 \
                    else ()
                em.op("tensor",
                      lambda e, cc=cc, oc=oc, lb=lb, zs=zs, gs_=gs_,
                      k=k: e.matmul(
                          bank(lb, 0, NE),
                          (XH if zs == "h" else XL)(cc, oc, 128),
                          Bp("gateh" if gs_ == "h" else "gatel",
                             cc * NE, 0, 128, NE),
                          start=(k == 0), stop=(k == nmm - 1)),
                      waits=w, inc=(PE if k == nmm - 1 else None))
                k += 1
        em.op("vector",
              lambda e, lb=lb: e.reduce_max(
                  A("rcol", 0, 0, 128, 1), bank(lb, 0, NE), AX.X,
                  negate=True),
              waits=[(PE, PE.n)], inc=DV)
        em.op("scalar",
              lambda e, lb=lb: e.activation(
                  A("probs", 0, 0, 128, NE), bank(lb, 0, NE), AF.Exp,
                  bias=A("rcol", 0, 0, 128, 1),
                  accum_out=A("rcol", 1, 0, 128, 1)),
              waits=[(DV, DV.n)], inc=AC)
        bset(lb, AC)
        em.op("vector",
              lambda e: e.reciprocal(A("rcol", 2, 0, 128, 1),
                                     A("rcol", 1, 0, 128, 1)),
              waits=[(AC, AC.n)])
        em.op("vector",
              lambda e: e.tensor_scalar(
                  A("probs", 0, 0, 128, NE), A("probs", 0, 0, 128, NE),
                  A("rcol", 2, 0, 128, 1), None, OP.mult))
        em.op("vector",
              lambda e: e.max(A("mx", 0, 0, 128, 8),
                              A("probs", 0, 0, 128, NE)))
        em.op("vector",
              lambda e: e.reduce_sum(
                  A("rcol", 0, 0, 128, 1), A("mx", 0, 0, 128, 8), AX.X))
        em.op("vector",
              lambda e: e.reciprocal(A("rcol", 1, 0, 128, 1),
                                     A("rcol", 0, 0, 128, 1)))
        em.op("vector",
              lambda e: e.tensor_scalar(
                  A("prim", 0, 0, 128, NE), A("probs", 0, 0, 128, NE),
                  A("mx", 3, 0, 128, 1), None, OP.is_ge))
        em.op("vector",
              lambda e: e.tensor_scalar(
                  A("rscr", 0, 0, 128, NE), A("probs", 0, 0, 128, NE),
                  A("mx", 7, 0, 128, 1), None, OP.is_ge))
        em.op("vector",
              lambda e: e.tensor_sub(
                  A("sec", 0, 0, 128, NE), A("rscr", 0, 0, 128, NE),
                  A("prim", 0, 0, 128, NE)))
        em.op("vector",
              lambda e: e.tensor_scalar(
                  A("rw", 0, 0, 128, NE), A("probs", 0, 0, 128, NE),
                  A("rcol", 1, 0, 128, 1), None, OP.mult))
        em.op("vector",
              lambda e: e.tensor_mul(
                  A("rw", 0, 0, 128, NE), A("rw", 0, 0, 128, NE),
                  A("rscr", 0, 0, 128, NE)))
        for s in range(NE):
            em.op("vector",
                  lambda e, s=s: e.tensor_mul(
                      A("rscr", NE, 0, 128, NE), A("prim", 0, 0, 128, NE),
                      A("c_simB", s * NE, 0, 128, NE)),
                  waits=([tok_simB] if s == 0 else ()))
            em.op("vector",
                  lambda e, s=s: e.reduce_max(
                      A("bs", s, 0, 128, 1),
                      A("rscr", NE, 0, 128, NE), AX.X))
        em.op("vector",
              lambda e: e.tensor_scalar(
                  A("rscr", 0, 0, 128, NE), A("bs", 0, 0, 128, NE),
                  0.5, None, OP.is_lt))
        em.op("vector",
              lambda e: e.tensor_mul(
                  A("rscr", NE, 0, 128, NE), A("sec", 0, 0, 128, NE),
                  A("rscr", 0, 0, 128, NE)))
        em.op("vector",
              lambda e: e.tensor_add(
                  A("rscr", NE, 0, 128, NE), A("rscr", NE, 0, 128, NE),
                  A("prim", 0, 0, 128, NE)))
        em.op("vector",
              lambda e, r=r: e.tensor_mul(
                  A("comb", r * NE, 0, 128, NE), A("rw", 0, 0, 128, NE),
                  A("rscr", NE, 0, 128, NE)))
        em.op("vector",
              lambda e: e.tensor_scalar(
                  A("rscr", 0, 0, 128, NE), A("rscr", 0, 0, 128, NE),
                  -1.0, 1.0, OP.mult, OP.add))
        em.op("vector",
              lambda e: e.tensor_mul(
                  A("sec", 0, 0, 128, NE), A("sec", 0, 0, 128, NE),
                  A("rw", 0, 0, 128, NE)))
        em.op("vector",
              lambda e: e.tensor_mul(
                  A("sec", 0, 0, 128, NE), A("sec", 0, 0, 128, NE),
                  A("rscr", 0, 0, 128, NE)))
        for s in range(NE):
            em.op("vector",
                  lambda e, s=s: e.tensor_scalar(
                      A("rscr", 0, 0, 128, NE), A("c_simB", s * NE, 0, 128, NE),
                      A("bs", s, 0, 128, 1), None, OP.is_ge))
            em.op("vector",
                  lambda e: e.tensor_mul(
                      A("rscr", 0, 0, 128, NE), A("rscr", 0, 0, 128, NE),
                      A("prim", 0, 0, 128, NE)))
            em.op("vector",
                  lambda e, r=r, s=s: e.scalar_tensor_tensor(
                      A("comb", r * NE, 0, 128, NE),
                      A("rscr", 0, 0, 128, NE),
                      A("sec", s, 0, 128, 1),
                      A("comb", r * NE, 0, 128, NE),
                      OP.mult, OP.add),
                  inc=(DV if s == NE - 1 else None))
    routing_done = DV.n

    sb.release("probs", "prim", "rw", "bs", "mx", "rcol", "rscr", "sec",
               "gtmp", "c_simB")
    fenceC = [(PE, PE.n), (DV, DV.n), (AC, AC.n)]

    sb.alloc("wgu", 2 * CCH * 128)
    sb.alloc("wd", 6 * 256)
    sbb.alloc("wguh", 2 * CCH * 128)
    sbb.alloc("wgul", 2 * CCH * 128)
    sbb.alloc("wdh", 2 * 6 * 256)
    sbb.alloc("wdl", 2 * 6 * 256)
    sb.alloc("sg", 6 * 512)
    sb.alloc("hsc", 512)
    sbb.alloc("hhh", 6 * 512)
    sbb.alloc("hhl", 6 * 512)
    sb.alloc("y", CCH * 512)
    sb.alloc("cw", 512)
    sb.alloc("zrc", 512)
    sb.alloc("sq2", 512)
    sb.alloc("lrow", 2 * 512)
    sb.alloc("lbc", 1024)
    sb.alloc("otm", C)

    def ZH(cc, ih, n=512):
        return Bp("xh", cc * 1024 + ih * 512, 0, 128, n)

    def ZL(cc, ih, n=512):
        return Bp("xl", cc * 1024 + ih * 512, 0, 128, n)

    wgu_use = [0, 0]
    wd_use = [0, 0]
    wgu_f32_free = [[], []]
    wd_f32_free = [[]]
    tok_st = None
    gu_splits = {}
    wd_splits = {}
    for ih in range(2):
        def emit_gu_split(e_ix, ft):
            par = ft % 2
            tok_g = dma(
                A("wgu", par * CCH * 128, 0, 128, CCH * 128).rearrange(
                    "p (cc f) -> p cc f", f=128),
                d_wguT[e_ix, :, ft * 128:(ft + 1) * 128].rearrange(
                    "(cc p) f -> p cc f", p=128),
                waits=wgu_f32_free[par] + fenceC)
            em.op("scalar",
                  lambda e, par=par: e.copy(
                      Bp("wguh", par * CCH * 128, 0, 128, CCH * 128),
                      A("wgu", par * CCH * 128, 0, 128, CCH * 128)),
                  waits=[tok_g, (PE, wgu_use[par])], inc=AC,
                  drain=False)
            cast_n = AC.n
            em.op("vector",
                  lambda e, par=par: e.tensor_tensor(
                      Bp("wgul", par * CCH * 128, 0, 128, CCH * 128),
                      A("wgu", par * CCH * 128, 0, 128, CCH * 128),
                      Bp("wguh", par * CCH * 128, 0, 128, CCH * 128),
                      OP.subtract),
                  waits=[(AC, cast_n)], inc=DV, drain=False)
            sub_n = DV.n
            wgu_f32_free[par] = [(AC, cast_n), (DV, sub_n)]
            gu_splits[(e_ix, ft)] = (cast_n, sub_n)

        def emit_wd_split(e_ix, mp):
            par = mp % 2
            tok_d = dma(
                A("wd", 0, 0, 128, 6 * 256).rearrange(
                    "p (fc m) -> p fc m", m=256),
                d_wdT[e_ix, :, mp * 256:(mp + 1) * 256].rearrange(
                    "(fc p) m -> p fc m", p=128),
                waits=wd_f32_free[0] + fenceC)
            em.op("scalar",
                  lambda e, par=par: e.copy(
                      Bp("wdh", par * 6 * 256, 0, 128, 6 * 256),
                      A("wd", 0, 0, 128, 6 * 256)),
                  waits=[tok_d, (PE, wd_use[par])], inc=AC,
                  drain=False)
            dcast_n = AC.n
            em.op("vector",
                  lambda e, par=par: e.tensor_tensor(
                      Bp("wdl", par * 6 * 256, 0, 128, 6 * 256),
                      A("wd", 0, 0, 128, 6 * 256),
                      Bp("wdh", par * 6 * 256, 0, 128, 6 * 256),
                      OP.subtract),
                  waits=[(AC, dcast_n)], inc=DV, drain=False)
            dsub_n = DV.n
            wd_f32_free[0] = [(AC, dcast_n), (DV, dsub_n)]
            wd_splits[(e_ix, mp)] = (dcast_n, dsub_n)

        def emit_down(e_ix, h_done):
            for mp in range(8):
                par = mp % 2
                dcast_n, dsub_n = wd_splits.pop((e_ix, mp))
                for sub in range(2):
                    mt = mp * 2 + sub
                    db = 2 + mt % 2
                    k = 0
                    for fc in range(6):
                        for ws, hs in [("h", "h"), ("h", "l"), ("l", "h")]:
                            w = ([(AC, dcast_n), (DV, max(dsub_n, h_done))]
                                 + bw(db)) if k == 0 else ()
                            em.op("tensor",
                                  lambda e, fc=fc, par=par, sub=sub, db=db,
                                  ws=ws, hs=hs, k=k: e.matmul(
                                      bank(db, 0, 512),
                                      Bp("wdh" if ws == "h" else "wdl",
                                         par * 6 * 256 + fc * 256
                                         + sub * 128, 0, 128, 128),
                                      Bp("hhh" if hs == "h" else "hhl",
                                         fc * 512, 0, 128, 512),
                                      start=(k == 0), stop=(k == 17)),
                                  waits=w, inc=(PE if k == 17 else None))
                            k += 1
                    wd_use[par] = PE.n
                    if sub == 1 and mp + 2 < 8:
                        emit_wd_split(e_ix, mp + 2)
                    if e_ix == 0:
                        em.op("vector",
                              lambda e, mt=mt, db=db: e.tensor_copy(
                                  A("y", mt * 512, 0, 128, 512),
                                  bank(db, 0, 512)),
                              waits=[(PE, PE.n)], inc=DV, drain=False)
                    else:
                        em.op("vector",
                              lambda e, mt=mt, db=db: e.tensor_add(
                                  A("y", mt * 512, 0, 128, 512),
                                  A("y", mt * 512, 0, 128, 512),
                                  bank(db, 0, 512)),
                              waits=[(PE, PE.n)], inc=DV, drain=False)
                    bset(db, DV)

        pending = None
        emit_gu_split(0, 0)
        emit_gu_split(0, 1)
        for e_ix in range(NE):
            for rr in range(4):
                em.op("tensor",
                      lambda e, rr=rr, e_ix=e_ix, ih=ih: e.transpose(
                          bank(4, rr * 128, 128, 0, 1),
                          A("comb", (ih * 4 + rr) * NE + e_ix, 0, 128, 1),
                          A("c_id", 0, 0, 128, 128)),
                      waits=([(DV, routing_done)] + bw(4)) if rr == 0
                      else (), inc=(PE if rr == 3 else None))
            em.op("scalar",
                  lambda e: e.copy(A("cwrow", 0, 0, 1, 512),
                                   bank(4, 0, 512, 0, 1)),
                  waits=[(PE, PE.n)], inc=AC, drain=False)
            em.op("tensor",
                  lambda e: e.matmul(
                      bank(4, 0, 512), A("c_or", 0, 0, 1, 128),
                      A("cwrow", 0, 0, 1, 512),
                      start=True, stop=True),
                  waits=[(AC, AC.n)], inc=PE)
            em.op("vector",
                  lambda e: e.tensor_copy(A("cw", 0, 0, 128, 512),
                                          bank(4, 0, 512)),
                  waits=[(PE, PE.n)], inc=DV, drain=False)
            bset(4, DV)
            cw_done = DV.n
            for ft in range(12):
                par = ft % 2
                cast_n, sub_n = gu_splits.pop((e_ix, ft))
                gb = ft % 2
                k = 0
                for cc in range(CCH):
                    for ws, zs in [("h", "h"), ("h", "l"), ("l", "h")]:
                        w = ([(AC, cast_n), (DV, sub_n)] + bw(gb)) \
                            if k == 0 else ()
                        em.op("tensor",
                              lambda e, cc=cc, par=par, gb=gb, ih=ih,
                              ws=ws, zs=zs, k=k: e.matmul(
                                  bank(gb, 0, 512),
                                  Bp("wguh" if ws == "h" else "wgul",
                                     par * CCH * 128 + cc * 128,
                                     0, 128, 128),
                                  (ZH if zs == "h" else ZL)(cc, ih),
                                  start=(k == 0), stop=(k == 3 * CCH - 1)),
                              waits=w,
                              inc=(PE if k == 3 * CCH - 1 else None))
                        k += 1
                wgu_use[par] = PE.n
                gmm = PE.n
                if ft + 2 < 12:
                    emit_gu_split(e_ix, ft + 2)
                elif e_ix + 1 < NE:
                    emit_gu_split(e_ix + 1, ft + 2 - 12)
                if ft < 6:
                    em.op("scalar",
                          lambda e, ft=ft, gb=gb: e.activation(
                              A("sg", ft * 512, 0, 128, 512),
                              bank(gb, 0, 512), AF.Silu),
                          waits=[(PE, gmm), (DV, DV.n)], inc=AC,
                          drain=False)
                    bset(gb, AC)
                else:
                    kk = ft - 6
                    em.op("vector",
                          lambda e, kk=kk, gb=gb: e.tensor_mul(
                              A("hsc", 0, 0, 128, 512),
                              A("sg", kk * 512, 0, 128, 512),
                              bank(gb, 0, 512)),
                          waits=[(PE, gmm), (AC, AC.n), (DV, cw_done)])
                    em.op("vector",
                          lambda e: e.tensor_mul(
                              A("hsc", 0, 0, 128, 512),
                              A("hsc", 0, 0, 128, 512),
                              A("cw", 0, 0, 128, 512)),
                          inc=DV)
                    bset(gb, DV)
                    em.op("scalar",
                          lambda e, kk=kk: e.copy(
                              Bp("hhh", kk * 512, 0, 128, 512),
                              A("hsc", 0, 0, 128, 512)),
                          waits=[(DV, DV.n), (PE, gmm)], inc=AC,
                          drain=False)
                    em.op("vector",
                          lambda e, kk=kk: e.tensor_tensor(
                              Bp("hhl", kk * 512, 0, 128, 512),
                              A("hsc", 0, 0, 128, 512),
                              Bp("hhh", kk * 512, 0, 128, 512),
                              OP.subtract),
                          waits=[(AC, AC.n), (PE, gmm)], inc=DV)
                if ft == 0:
                    emit_wd_split(e_ix if pending is None else pending[0],
                                  0) if False else None
                if ft == 1 and pending is not None:
                    emit_wd_split(pending[0], 0)
                    emit_wd_split(pending[0], 1)
                    emit_down(*pending)
                    pending = None
            pending = (e_ix, DV.n)
        emit_wd_split(pending[0], 0)
        emit_wd_split(pending[0], 1)
        emit_down(*pending)
        pending = None

        for mt in range(CCH):
            em.op("vector",
                  lambda e, mt=mt, ih=ih: e.tensor_add(
                      A("zrc", 0, 0, 128, 512), ZH(mt, ih), ZL(mt, ih)))
            em.op("vector",
                  lambda e, mt=mt: e.tensor_add(
                      A("y", mt * 512, 0, 128, 512),
                      A("y", mt * 512, 0, 128, 512),
                      A("zrc", 0, 0, 128, 512)),
                  inc=DV)
        r2_done = DV.n
        for cc in range(CCH):
            w = ([(DV, r2_done)] + bw(5)) if cc == 0 else ()
            em.op("tensor",
                  lambda e, cc=cc: e.matmul(
                      bank(5, 0, 512, 0, 1), A("c_oc", 0, 0, 128, 1),
                      A("y", cc * 512, 0, 128, 512),
                      start=(cc == 0), stop=(cc == CCH - 1)),
                  waits=w, inc=(PE if cc == CCH - 1 else None))
        for cc in range(CCH):
            em.op("scalar",
                  lambda e, cc=cc: e.activation(
                      A("sq2", 0, 0, 128, 512), A("y", cc * 512, 0, 128, 512),
                      AF.Square),
                  waits=([(DV, r2_done), (PE, PE.n)] if cc == 0
                         else [(PE, PE.n)]),
                  inc=AC, drain=False)
            w = ([(AC, AC.n)] + bw(6)) if cc == 0 else [(AC, AC.n)]
            em.op("tensor",
                  lambda e, cc=cc: e.matmul(
                      bank(6, 0, 512, 0, 1), A("c_oc", 0, 0, 128, 1),
                      A("sq2", 0, 0, 128, 512),
                      start=(cc == 0), stop=(cc == CCH - 1)),
                  waits=w, inc=PE)
        st2 = PE.n
        em.op("vector",
              lambda e: e.tensor_scalar(
                  A("lrow", 0, 0, 1, 512), bank(5, 0, 512, 0, 1),
                  1.0 / C, None, OP.mult),
              waits=[(PE, st2)])
        em.op("vector",
              lambda e: e.tensor_scalar(
                  A("lrow", 512, 0, 1, 512), bank(6, 0, 512, 0, 1),
                  1.0 / C, None, OP.mult))
        em.op("vector",
              lambda e: e.tensor_mul(
                  A("zrc", 0, 0, 1, 512), A("lrow", 0, 0, 1, 512),
                  A("lrow", 0, 0, 1, 512)))
        em.op("vector",
              lambda e: e.tensor_sub(
                  A("lrow", 512, 0, 1, 512), A("lrow", 512, 0, 1, 512),
                  A("zrc", 0, 0, 1, 512)))
        em.op("vector",
              lambda e: e.tensor_scalar(
                  A("lrow", 512, 0, 1, 512), A("lrow", 512, 0, 1, 512),
                  EPS, None, OP.add))
        em.op("vector",
              lambda e: e.reciprocal(
                  A("zrc", 0, 0, 1, 512), A("lrow", 512, 0, 1, 512)),
              inc=DV)
        em.op("scalar",
              lambda e: e.activation(
                  A("lrow", 512, 0, 1, 512), A("zrc", 0, 0, 1, 512),
                  AF.Sqrt),
              waits=[(DV, DV.n)], inc=AC)
        em.op("vector",
              lambda e: e.tensor_mul(
                  A("zrc", 0, 0, 1, 512), A("lrow", 0, 0, 1, 512),
                  A("lrow", 512, 0, 1, 512)),
              waits=[(AC, AC.n)])
        em.op("vector",
              lambda e: e.tensor_scalar(
                  A("zrc", 0, 0, 1, 512), A("zrc", 0, 0, 1, 512),
                  -1.0, None, OP.mult),
              inc=DV)
        lnr = DV.n
        em.op("tensor",
              lambda e: e.matmul(
                  bank(5, 0, 512), A("c_or", 0, 0, 1, 128),
                  A("lrow", 512, 0, 1, 512), start=True, stop=True),
              waits=[(DV, lnr)] + bw(5))
        em.op("tensor",
              lambda e: e.matmul(
                  bank(6, 0, 512), A("c_or", 0, 0, 1, 128),
                  A("zrc", 0, 0, 1, 512), start=True, stop=True),
              waits=bw(6), inc=PE)
        em.op("vector",
              lambda e: e.tensor_copy(A("lbc", 0, 0, 128, 512),
                                      bank(5, 0, 512)),
              waits=[(PE, PE.n)])
        em.op("vector",
              lambda e: e.tensor_copy(A("lbc", 512, 0, 128, 512),
                                      bank(6, 0, 512)),
              inc=DV)
        bset(5, DV)
        bset(6, DV)
        bc_done = DV.n
        for j in range(4):
            for mt in range(CCH):
                em.op("vector",
                      lambda e, mt=mt, j=j: e.tensor_mul(
                          A("sq2", 0, 0, 128, 128),
                          A("y", mt * 512 + j * 128, 0, 128, 128),
                          A("lbc", j * 128, 0, 128, 128)),
                      waits=(([(DV, bc_done), (PE, PE.n)]
                              + ([tok_st] if tok_st else []))
                             if mt == 0 else [(PE, PE.n)]), drain=False)
                em.op("vector",
                      lambda e, mt=mt, j=j: e.tensor_add(
                          A("sq2", 0, 0, 128, 128),
                          A("sq2", 0, 0, 128, 128),
                          A("lbc", 512 + j * 128, 0, 128, 128)),
                      inc=DV)
                tb = 7 if mt % 2 == 0 else 0
                em.op("tensor",
                      lambda e, tb=tb: e.transpose(
                          bank(tb, 0, 128), A("sq2", 0, 0, 128, 128),
                          A("c_id", 0, 0, 128, 128)),
                      waits=[(DV, DV.n)] + bw(tb), inc=PE)
                em.op("scalar",
                      lambda e, mt=mt, tb=tb: e.copy(
                          A("otm", mt * 128, 0, 128, 128), bank(tb, 0, 128)),
                      waits=[(PE, PE.n)], inc=AC, drain=False)
                bset(tb, AC)
            r0 = ih * 512 + j * 128
            tok_st = dma(d_out[r0:r0 + 128, :], A("otm", 0, 0, 128, C),
                         waits=[(AC, AC.n)], q=qst)

    em.op("gpsimd", lambda e: e.nop(),
          waits=[(s, s.n) for s in qst.sems])

    with nc.Block() as blk:
        blk.sync(lambda e: em.replay("sync", e))
        blk.tensor(lambda e: em.replay("tensor", e))
        blk.vector(lambda e: em.replay("vector", e))
        blk.scalar(lambda e: em.replay("scalar", e))
        blk.gpsimd(lambda e: em.replay("gpsimd", e))

    nc.dbg_sb = sb
    nc.dbg_sbb = sbb
    nc.dbg_ps = ps_h
    return nc


def kernel(**inputs):
    import ml_dtypes

    x = np.asarray(inputs["L0_attn_x"], dtype=np.float32)
    wq = np.asarray(inputs["L0_attn_Wq"], dtype=np.float32)
    bq = np.asarray(inputs["L0_attn_bq"], dtype=np.float32)
    wk = np.asarray(inputs["L0_attn_Wk"], dtype=np.float32)
    bk = np.asarray(inputs["L0_attn_bk"], dtype=np.float32)
    wv = np.asarray(inputs["L0_attn_Wv"], dtype=np.float32)
    bv = np.asarray(inputs["L0_attn_bv"], dtype=np.float32)
    wo = np.asarray(inputs["L0_attn_Wo"], dtype=np.float32)
    bo = np.asarray(inputs["L0_attn_bo"], dtype=np.float32)
    gate = np.asarray(inputs["L0_ffn_gate_weight"], dtype=np.float32)
    gup = np.asarray(inputs["L0_ffn_gate_up_proj"], dtype=np.float32)
    dwn = np.asarray(inputs["L0_ffn_down_proj"], dtype=np.float32)
    sim = np.asarray(inputs["L0_ffn_similarity"], dtype=np.float32)

    wguT = np.ascontiguousarray(np.transpose(gup, (0, 2, 1)))
    wdT = np.ascontiguousarray(np.transpose(dwn, (0, 2, 1)))
    gateT = np.ascontiguousarray(gate.T)
    simd = sim.copy()
    np.fill_diagonal(simd, 1.0)
    simB = np.ascontiguousarray(
        np.tile(simd.reshape(1, NE * NE), (128, 1)))

    cos_dup, sin_sgn = _rope_tables()
    ident = np.eye(128, dtype=np.float32)
    rot = np.zeros((128, 128), dtype=np.float32)
    for m in range(128):
        rot[(m + 64) % 128, m] = 1.0
    onescol = np.ones((128, 1), dtype=np.float32)
    onesrow = np.ones((1, 128), dtype=np.float32)
    bqc = np.ascontiguousarray(bq.reshape(CCH, 128).T)
    boc = np.ascontiguousarray(bo.reshape(CCH, 128).T)
    bkc = np.ascontiguousarray(bk.reshape(NG, 128).T)
    bvr = np.ascontiguousarray(bv.reshape(1, 512))

    in_maps = []
    for core in range(NCORES):
        b, h = core // 2, core % 2
        _, glob = _glob_tokens(h)
        gl = np.clip(glob, 0, T - 1)
        in_maps.append({
            "xloc": _xloc_for_core(x[b], h),
            "wq": wq, "wk": wk, "wv": wv, "wo": wo,
            "bqc": bqc, "bkc": bkc, "bvr": bvr, "boc": boc,
            "gateT": gateT, "wguT": wguT, "wdT": wdT, "simB": simB,
            "masks": _masks_for_core(h).astype(ml_dtypes.bfloat16),
            "sinkb": _sinkbias_for_core(h),
            "cosdup": np.ascontiguousarray(cos_dup[:, gl]),
            "sinsgn": np.ascontiguousarray(sin_sgn[:, gl]),
            "ident": ident, "rotmat": rot,
            "onescol": onescol, "onesrow": onesrow,
        })

    nc = build_program()
    res = run_bass_kernel_spmd(nc, in_maps, core_ids=list(range(NCORES)))
    out = np.empty((B, T, C), dtype=np.float32)
    for core in range(NCORES):
        b, h = core // 2, core % 2
        out[b, h * 1024:(h + 1) * 1024] = res.results[core]["out"]
    return out


# version 11 (final)
# speedup vs baseline: 1.6907x; 1.0332x over previous
import contextlib

import numpy as np

import concourse.bass as bass
import concourse.mybir as mybir
from concourse.bass_utils import run_bass_kernel_spmd

F32 = mybir.dt.float32
BF16 = mybir.dt.bfloat16
AF = mybir.ActivationFunctionType
OP = mybir.AluOpType
AX = mybir.AxisListType

B, T, C = 4, 2048, 2048
NG, GS, DH = 4, 4, 128
WIN, SINK = 512, 4
SCALE = float(DH) ** -0.5
NE = 32
DFF = 768
EPS = 1e-5
CCH = C // 128
NKV_TILES = 13
NKV = NKV_TILES * 128
NQT = 8
QT0 = 5
BIG = 1.0e38
NCORES = 8


def _rope_tables():
    half = DH // 2
    inv = 1.0 / (10000.0 ** (np.arange(half, dtype=np.float64) * 2.0 / DH))
    t = np.arange(T, dtype=np.float64)
    ang = inv[:, None] * t[None, :]
    cos = np.cos(ang)
    sin = np.sin(ang)
    cos_dup = np.concatenate([cos, cos], 0).astype(np.float32)
    sin_sgn = np.concatenate([-sin, sin], 0).astype(np.float32)
    return cos_dup, sin_sgn


def _glob_tokens(h):
    w0 = h * 1024 - 512
    g = np.empty(NKV, dtype=np.int64)
    g[0:128] = np.arange(128)
    g[128:] = w0 + np.arange(12 * 128)
    return w0, g


def _masks_for_core(h):
    w0, _ = _glob_tokens(h)
    out = np.empty((128, NQT * 5 * 128), dtype=np.float32)
    j = np.arange(128)[:, None]
    i = np.arange(128)[None, :]
    for qt in range(NQT):
        qg = h * 1024 + 128 * qt
        for s in range(5):
            kg = w0 + 128 * (qt + s)
            kj = kg + j
            qi = qg + i
            keep = (kj >= 0) & (qi >= kj) & (((qi - kj) <= WIN) | (kj < SINK))
            out[:, (qt * 5 + s) * 128:(qt * 5 + s + 1) * 128] = np.where(
                keep, BIG, -BIG).astype(np.float32)
    return out


def _sinkbias_for_core(h):
    w0, _ = _glob_tokens(h)
    out = np.zeros((4, NQT), dtype=np.float32)
    for qt in range(NQT):
        if any(w0 + 128 * (qt + s) == 0 for s in range(5)):
            out[:, qt] = -BIG
    return out


def _xloc_for_core(x_b, h):
    rows = [x_b[0:128]]
    if h == 0:
        rows.append(np.zeros((512, C), dtype=x_b.dtype))
        rows.append(x_b[0:1024])
    else:
        rows.append(x_b[512:2048])
    return np.ascontiguousarray(np.concatenate(rows, axis=0))


class Sem:
    def __init__(self, stack, nc, name):
        self.h = stack.enter_context(nc.semaphore(name))
        self.n = 0


class SB:

    def __init__(self, handle, cols):
        self.h = handle
        self.cols = cols
        self.freelist = [(0, cols)]
        self.live = {}
        self.dead = {}

    def alloc(self, name, n):
        assert name not in self.live and name not in self.dead
        for idx, (c0, sz) in enumerate(self.freelist):
            if sz >= n:
                self.live[name] = (c0, n)
                if sz == n:
                    self.freelist.pop(idx)
                else:
                    self.freelist[idx] = (c0 + n, sz - n)
                return
        raise RuntimeError(f"SBUF arena overflow: {name} needs {n} cols; "
                           f"free={self.freelist}")

    def release(self, *names):
        for name in names:
            c0, n = self.live.pop(name)
            self.dead[name] = (c0, n)
            self.freelist.append((c0, n))
        self.freelist.sort()
        merged = []
        for c, s in self.freelist:
            if merged and merged[-1][0] + merged[-1][1] == c:
                merged[-1] = (merged[-1][0], merged[-1][1] + s)
            else:
                merged.append((c, s))
        self.freelist = merged

    def ap(self, name, col=0, p0=0, np_=128, n=1):
        c0, tot = self.live.get(name) or self.dead[name]
        assert col + n <= tot, (name, col, n, tot)
        return self.h[p0:p0 + np_, c0 + col:c0 + col + n]


class Emit:

    ENGINES = ("sync", "tensor", "vector", "scalar", "gpsimd")

    def __init__(self):
        self.ops = {k: [] for k in self.ENGINES}
        self.lastwait = {k: {} for k in self.ENGINES}

    def op(self, engine, fn, waits=(), inc=None, inc_by=1, drain=True):
        waits = [(s, v) for (s, v) in waits if v > 0]
        for s, v in waits:
            assert v <= s.n, f"wait on future count {v} > {s.n}"
        if inc is not None:
            inc.n += inc_by
        lw = self.lastwait[engine]
        eff = []
        for s, v in waits:
            if lw.get(id(s), 0) < v:
                lw[id(s)] = v
                eff.append((s, v))

        def run(eng, fn=fn, eff=tuple(eff), inc=inc, inc_by=inc_by):
            for s, v in eff:
                eng.wait_ge(s.h, v)
            inst = fn(eng)
            if inc is not None:
                inst.then_inc(inc.h, inc_by)

        self.ops[engine].append((run, drain))

    def replay(self, engine, eng):
        needs_drain = engine in ("vector", "scalar")
        for i, (fn, dr) in enumerate(self.ops[engine]):
            if needs_drain and dr and i > 0:
                eng.drain()
            fn(eng)


class DmaQ:

    def __init__(self, stack, nc, name, engine, em, pool):
        self.sems = [Sem(stack, nc, f"{name}{i}") for i in range(pool)]
        self.engine = engine
        self.em = em
        self.i = 0

    def issue(self, out_ap, in_ap, waits=()):
        s = self.sems[self.i % len(self.sems)]
        w = list(waits)
        if s.n:
            w.append((s, s.n))
        self.em.op(self.engine,
                   lambda e, o=out_ap, i=in_ap: e.dma_start(out=o, in_=i),
                   waits=w, inc=s, inc_by=16)
        self.i += 1
        return (s, s.n)


def build_program():
    nc = bass.Bass("TRN2", target_bir_lowering=False)

    d_x = nc.dram_tensor("xloc", [NKV, C], F32, kind="ExternalInput")
    d_wq = nc.dram_tensor("wq", [C, C], F32, kind="ExternalInput")
    d_wk = nc.dram_tensor("wk", [C, 512], F32, kind="ExternalInput")
    d_wv = nc.dram_tensor("wv", [C, 512], F32, kind="ExternalInput")
    d_wo = nc.dram_tensor("wo", [C, C], F32, kind="ExternalInput")
    d_bq = nc.dram_tensor("bqc", [128, CCH], F32, kind="ExternalInput")
    d_bk = nc.dram_tensor("bkc", [128, NG], F32, kind="ExternalInput")
    d_bv = nc.dram_tensor("bvr", [1, 512], F32, kind="ExternalInput")
    d_bo = nc.dram_tensor("boc", [128, CCH], F32, kind="ExternalInput")
    d_gateT = nc.dram_tensor("gateT", [C, NE], F32, kind="ExternalInput")
    d_wguT = nc.dram_tensor("wguT", [NE, C, 2 * DFF], F32,
                            kind="ExternalInput")
    d_wdT = nc.dram_tensor("wdT", [NE, DFF, C], F32, kind="ExternalInput")
    d_simB = nc.dram_tensor("simB", [128, NE * NE], F32,
                            kind="ExternalInput")
    d_masks = nc.dram_tensor("masks", [128, NQT * 5 * 128], BF16,
                             kind="ExternalInput")
    d_sinkb = nc.dram_tensor("sinkb", [4, NQT], F32, kind="ExternalInput")
    d_cos = nc.dram_tensor("cosdup", [128, NKV], F32, kind="ExternalInput")
    d_sin = nc.dram_tensor("sinsgn", [128, NKV], F32, kind="ExternalInput")
    d_id = nc.dram_tensor("ident", [128, 128], F32, kind="ExternalInput")
    d_rot = nc.dram_tensor("rotmat", [128, 128], F32, kind="ExternalInput")
    d_oc = nc.dram_tensor("onescol", [128, 1], F32, kind="ExternalInput")
    d_or = nc.dram_tensor("onesrow", [1, 128], F32, kind="ExternalInput")
    d_out = nc.dram_tensor("out", [1024, C], F32, kind="ExternalOutput")

    FP_TOT = 26072
    BF_TOT = 54272
    stack = contextlib.ExitStack()
    sb_h = stack.enter_context(nc.sbuf_tensor([128, FP_TOT], F32))
    sbf_h = stack.enter_context(nc.sbuf_tensor([128, BF_TOT], BF16))
    ps_h = stack.enter_context(nc.psum_tensor([128, 4096], F32))
    sb = SB(sb_h, FP_TOT)
    sbb = SB(sbf_h, BF_TOT)
    em = Emit()

    PE = Sem(stack, nc, "sPE")
    DV = Sem(stack, nc, "sDV")
    AC = Sem(stack, nc, "sAC")
    qld = DmaQ(stack, nc, "qL", "sync", em, pool=12)
    qst = DmaQ(stack, nc, "qS", "gpsimd", em, pool=4)

    def bank(b, off=0, n=512, p0=0, np_=128):
        assert 0 <= b < 8 and off + n <= 512
        return ps_h[p0:p0 + np_, b * 512 + off:b * 512 + off + n]

    def wide(b, n, p0=0, np_=128):
        return ps_h[p0:p0 + np_, b * 512:b * 512 + n]

    bankdep = [None] * 8

    def bw(*bs):
        return [bankdep[b] for b in bs if bankdep[b] is not None]

    def bset(b, sem):
        bankdep[b] = (sem, sem.n)

    def dma(out_ap, in_ap, waits=(), q=None):
        return (q or qld).issue(out_ap, in_ap, waits)

    A = sb.ap
    Bp = sbb.ap

    for nm, sz in [("c_id", 128), ("c_rot", 128), ("c_oc", 1), ("c_or", 128),
                   ("c_snk", NQT), ("c_bq", CCH), ("c_bo", CCH),
                   ("c_bk", NG), ("c_bv", 512),
                   ("c_cos", NKV), ("c_sin", NKV)]:
        sb.alloc(nm, sz)
    ctoks = [
        dma(A("c_id", 0, 0, 128, 128), d_id[:, :]),
        dma(A("c_rot", 0, 0, 128, 128), d_rot[:, :]),
        dma(A("c_oc", 0, 0, 128, 1), d_oc[:, :]),
        dma(A("c_or", 0, 0, 1, 128), d_or[:, :]),
        dma(A("c_snk", 0, 0, 4, NQT), d_sinkb[:, :]),
        dma(A("c_bq", 0, 0, 128, CCH), d_bq[:, :]),
        dma(A("c_bo", 0, 0, 128, CCH), d_bo[:, :]),
        dma(A("c_bk", 0, 0, 128, NG), d_bk[:, :]),
        dma(A("c_bv", 0, 0, 1, 512), d_bv[:, :]),
        dma(A("c_cos", 0, 0, 128, NKV), d_cos[:, :]),
        dma(A("c_sin", 0, 0, 128, NKV), d_sin[:, :]),
    ]

    sb.alloc("K", NG * NKV)
    sb.alloc("V", NKV_TILES * 512)
    sbb.alloc("xh", CCH * 1024)
    sbb.alloc("xl", CCH * 1024)

    def Kap(g, kvcol, n, p0=0, np_=128):
        return A("K", g * NKV + kvcol, p0, np_, n)

    def Vap(tile, d0, n, p0=0, np_=128):
        return A("V", tile * 512 + d0, p0, np_, n)

    def XH(cc, qcol, n):
        return Bp("xh", cc * 1024 + qcol, 0, 128, n)

    def XL(cc, qcol, n):
        return Bp("xl", cc * 1024 + qcol, 0, 128, n)

    sb.alloc("xtile", C)
    sb.alloc("xts", CCH * 128)
    sb.alloc("wkv", 2 * 2048)
    sb.alloc("kraw", 128)
    sb.alloc("rs1", 128)

    xtile_use = 0
    wkv_use = [0, 0]
    wchunks = [("k", g) for g in range(NG)] + [("v", vh) for vh in range(4)]
    for t in range(NKV_TILES):
        own = t >= QT0
        oc = (t - QT0) * 128
        tok_x = dma(A("xtile", 0, 0, 128, C), d_x[t * 128:(t + 1) * 128, :],
                    waits=[(PE, xtile_use)])
        for cc in range(CCH):
            b = cc % 2
            em.op("tensor",
                  lambda e, cc=cc, b=b: e.transpose(
                      bank(b, 0, 128),
                      A("xtile", cc * 128, 0, 128, 128),
                      A("c_id", 0, 0, 128, 128)),
                  waits=[tok_x] + ctoks + bw(b), inc=PE)
            em.op("scalar",
                  lambda e, cc=cc, b=b: e.copy(
                      A("xts", cc * 128, 0, 128, 128), bank(b, 0, 128)),
                  waits=[(PE, PE.n)], inc=AC, drain=False)
            if own:
                em.op("scalar",
                      lambda e, cc=cc, oc=oc, b=b: e.copy(
                          XH(cc, oc, 128), bank(b, 0, 128)),
                      inc=AC, drain=False)
                em.op("vector",
                      lambda e, cc=cc, oc=oc, b=b: e.tensor_tensor(
                          XL(cc, oc, 128), bank(b, 0, 128),
                          XH(cc, oc, 128), OP.subtract),
                      waits=[(AC, AC.n)], inc=DV, drain=False)
                bset(b, DV)
            else:
                bset(b, AC)
        xtile_use = PE.n
        xts_ready = AC.n

        for wi, (kind, idx) in enumerate(wchunks):
            wpar = wi % 2
            wsrc = d_wk if kind == "k" else d_wv
            tok_w = dma(A("wkv", wpar * 2048, 0, 128, 2048).rearrange(
                "p (cc d) -> p cc d", d=128),
                wsrc[:, idx * 128:(idx + 1) * 128].rearrange(
                    "(cc p) d -> p cc d", p=128),
                waits=[(PE, wkv_use[wpar])])
            if kind == "k":
                kb = 2 + wi % 2
                for cc in range(CCH):
                    w = ([tok_w, (AC, xts_ready)] + bw(kb)) if cc == 0 \
                        else ()
                    em.op("tensor",
                          lambda e, cc=cc, kb=kb, wpar=wpar: e.matmul(
                              bank(kb, 0, 128),
                              A("wkv", wpar * 2048 + cc * 128, 0, 128, 128),
                              A("xts", cc * 128, 0, 128, 128),
                              start=(cc == 0), stop=(cc == CCH - 1)),
                          waits=w, inc=(PE if cc == CCH - 1 else None))
                mmk = PE.n
                em.op("scalar",
                      lambda e, idx=idx, kb=kb: e.activation(
                          A("kraw", 0, 0, 128, 128), bank(kb, 0, 128),
                          AF.Identity, bias=A("c_bk", idx, 0, 128, 1)),
                      waits=[(PE, mmk), (DV, DV.n)] + ctoks, inc=AC,
                      drain=False)
                bset(kb, AC)
                rb = 4 + wi % 2
                em.op("tensor",
                      lambda e, rb=rb: e.matmul(
                          bank(rb, 0, 128), A("c_rot", 0, 0, 128, 128),
                          A("kraw", 0, 0, 128, 128), start=True, stop=True),
                      waits=[(AC, AC.n)] + bw(rb), inc=PE)
                rotk = PE.n
                em.op("vector",
                      lambda e, t=t: e.tensor_mul(
                          A("rs1", 0, 0, 128, 128), A("kraw", 0, 0, 128, 128),
                          A("c_cos", t * 128, 0, 128, 128)),
                      waits=[(AC, AC.n)] + ctoks, drain=False)
                em.op("vector",
                      lambda e, idx=idx, t=t, rb=rb: e.tensor_mul(
                          Kap(idx, t * 128, 128), bank(rb, 0, 128),
                          A("c_sin", t * 128, 0, 128, 128)),
                      waits=[(PE, rotk)])
                em.op("vector",
                      lambda e, idx=idx, t=t: e.tensor_add(
                          Kap(idx, t * 128, 128), Kap(idx, t * 128, 128),
                          A("rs1", 0, 0, 128, 128)),
                      inc=DV)
                bset(rb, DV)
            else:
                vb = 6 + wi % 2
                for cc in range(CCH):
                    w = ([tok_w, (AC, xts_ready)] + bw(vb)) if cc == 0 \
                        else ()
                    em.op("tensor",
                          lambda e, cc=cc, vb=vb, wpar=wpar: e.matmul(
                              bank(vb, 0, 128),
                              A("xts", cc * 128, 0, 128, 128),
                              A("wkv", wpar * 2048 + cc * 128, 0, 128, 128),
                              start=(cc == 0), stop=False),
                          waits=w)
                em.op("tensor",
                      lambda e, idx=idx, vb=vb: e.matmul(
                          bank(vb, 0, 128), A("c_or", 0, 0, 1, 128),
                          A("c_bv", idx * 128, 0, 1, 128),
                          start=False, stop=True),
                      inc=PE)
                em.op("scalar",
                      lambda e, t=t, idx=idx, vb=vb: e.copy(
                          Vap(t, idx * 128, 128), bank(vb, 0, 128)),
                      waits=[(PE, PE.n)], inc=AC, drain=False)
                bset(vb, AC)
            wkv_use[wpar] = PE.n

    sb.release("xtile", "xts", "wkv", "kraw", "rs1", "c_cos", "c_sin")
    fenceB = [(PE, PE.n), (DV, DV.n), (AC, AC.n)]

    sbb.alloc("maskb", NQT * 5 * 128)
    sb.alloc("Qr1", CCH * 128)
    sb.alloc("xrc", CCH * 128)
    sb.alloc("attnT", CCH * 128)
    sb.alloc("PT", 5 * 128)
    sb.alloc("PTs", 128)
    sb.alloc("wb", 2 * 2048)
    sb.alloc("ctab", 256)
    sb.alloc("qs1", 128)
    sb.alloc("sqs", 128)
    sb.alloc("bcx", 256)

    tok_masks = dma(Bp("maskb", 0, 0, 128, NQT * 5 * 128), d_masks[:, :],
                    waits=fenceB)


    wb_use = [0, 0]
    rope_use = 0
    for qt in range(NQT):
        oc = qt * 128
        tok_tab = dma(A("ctab", 0, 0, 128, 128),
                      d_cos[:, (QT0 + qt) * 128:(QT0 + qt + 1) * 128],
                      waits=[(DV, rope_use)] + (fenceB if qt == 0 else []))
        tok_tab2 = dma(A("ctab", 128, 0, 128, 128),
                       d_sin[:, (QT0 + qt) * 128:(QT0 + qt + 1) * 128],
                       waits=[(DV, rope_use)] + (fenceB if qt == 0 else []))

        for cc in range(CCH):
            em.op("vector",
                  lambda e, cc=cc, oc=oc: e.tensor_add(
                      A("xrc", cc * 128, 0, 128, 128),
                      XH(cc, oc, 128), XL(cc, oc, 128)),
                  inc=(DV if cc == CCH - 1 else None), drain=False)
        xrc_ready = DV.n

        for dc in range(CCH):
            wbp = dc % 2
            tok_wb = dma(A("wb", wbp * 2048, 0, 128, 2048).rearrange(
                "p (cc d) -> p cc d", d=128),
                d_wq[:, dc * 128:(dc + 1) * 128].rearrange(
                    "(cc p) d -> p cc d", p=128),
                waits=[(PE, wb_use[wbp])] + fenceB)
            qb = dc % 2
            for cc in range(CCH):
                w = ([tok_wb, (DV, xrc_ready)] + bw(qb)) if cc == 0 else ()
                em.op("tensor",
                      lambda e, cc=cc, qb=qb, wbp=wbp: e.matmul(
                          bank(qb, 0, 128),
                          A("wb", wbp * 2048 + cc * 128, 0, 128, 128),
                          A("xrc", cc * 128, 0, 128, 128),
                          start=(cc == 0), stop=(cc == CCH - 1)),
                      waits=w, inc=(PE if cc == CCH - 1 else None))
            wb_use[wbp] = PE.n
            mmq = PE.n
            em.op("scalar",
                  lambda e, dc=dc, qb=qb: e.activation(
                      A("sqs", 0, 0, 128, 128), bank(qb, 0, 128),
                      AF.Identity, bias=A("c_bq", dc, 0, 128, 1)),
                  waits=[(PE, mmq), (DV, DV.n)] + ctoks, inc=AC,
                  drain=False)
            bset(qb, AC)
            rb = 2 + dc % 2
            em.op("tensor",
                  lambda e, rb=rb: e.matmul(
                      bank(rb, 0, 128), A("c_rot", 0, 0, 128, 128),
                      A("sqs", 0, 0, 128, 128), start=True, stop=True),
                  waits=[(AC, AC.n)] + bw(rb), inc=PE)
            rotq = PE.n
            em.op("vector",
                  lambda e: e.tensor_mul(
                      A("qs1", 0, 0, 128, 128), A("sqs", 0, 0, 128, 128),
                      A("ctab", 0, 0, 128, 128)),
                  waits=[(AC, AC.n), tok_tab, tok_tab2], drain=False)
            em.op("vector",
                  lambda e, rb=rb, dc=dc: e.tensor_mul(
                      A("Qr1", dc * 128, 0, 128, 128), bank(rb, 0, 128),
                      A("ctab", 128, 0, 128, 128)),
                  waits=[(PE, rotq)], drain=False)
            em.op("vector",
                  lambda e, dc=dc: e.tensor_add(
                      A("Qr1", dc * 128, 0, 128, 128),
                      A("Qr1", dc * 128, 0, 128, 128),
                      A("qs1", 0, 0, 128, 128)),
                  inc=DV)
            bset(rb, DV)
        q_ready = DV.n
        rope_use = DV.n

        for g in range(NG):
            for hh in range(GS):
                gh = g * GS + hh
                pp = gh % 2
                bA, bB, bav, bdn = 2 * pp, 2 * pp + 1, 4 + pp, 6 + pp
                for s in range(5):
                    tb, toff = (bA, s * 128) if s < 4 else (bB, 0)
                    w = [(DV, q_ready), tok_masks] + bw(bA, bB) \
                        if s == 0 else ()
                    em.op("tensor",
                          lambda e, s=s, g=g, gh=gh, qt=qt, tb=tb,
                          toff=toff: e.matmul(
                              bank(tb, toff, 128),
                              Kap(g, (1 + qt + s) * 128, 128),
                              A("Qr1", gh * 128, 0, 128, 128),
                              start=True, stop=True),
                          waits=w)
                em.op("tensor",
                      lambda e, g=g, gh=gh, bB=bB: e.matmul(
                          bank(bB, 128, 128, 0, 4),
                          Kap(g, 0, 4),
                          A("Qr1", gh * 128, 0, 128, 128),
                          start=True, stop=True),
                      inc=PE)
                smm = PE.n
                em.op("vector",
                      lambda e, bA=bA, qt=qt: e.tensor_tensor(
                          wide(bA, 640),
                          wide(bA, 640),
                          Bp("maskb", qt * 640, 0, 128, 640), OP.min),
                      waits=[(PE, smm), (AC, AC.n)], inc=DV, drain=False)
                em.op("scalar",
                      lambda e, bA=bA: e.activation(
                          A("PT", 0, 0, 128, 640),
                          wide(bA, 640), AF.Exp, scale=SCALE),
                      waits=[(DV, DV.n)], inc=AC, drain=False)
                em.op("scalar",
                      lambda e, qt=qt, bB=bB: e.activation(
                          A("PTs", 0, 0, 4, 128), bank(bB, 128, 128, 0, 4),
                          AF.Exp, bias=A("c_snk", qt, 0, 4, 1), scale=SCALE),
                      waits=[(PE, smm)], inc=AC, drain=False)
                exp_done = AC.n
                bset(bA, AC)
                bset(bB, AC)
                for s in range(5):
                    w = ([(AC, exp_done)] + bw(bav, bdn)) if s == 0 else ()
                    em.op("tensor",
                          lambda e, s=s, g=g, qt=qt, bav=bav: e.matmul(
                              bank(bav, 0, 128),
                              Vap(1 + qt + s, g * 128, 128),
                              A("PT", s * 128, 0, 128, 128),
                              start=(s == 0), stop=False),
                          waits=w)
                em.op("tensor",
                      lambda e, g=g, bav=bav: e.matmul(
                          bank(bav, 0, 128),
                          Vap(0, g * 128, 128, 0, 4),
                          A("PTs", 0, 0, 4, 128),
                          start=False, stop=True))
                for s in range(5):
                    em.op("tensor",
                          lambda e, s=s, bdn=bdn: e.matmul(
                              bank(bdn, 0, 128, 0, 1),
                              A("c_oc", 0, 0, 128, 1),
                              A("PT", s * 128, 0, 128, 128),
                              start=(s == 0), stop=False))
                em.op("tensor",
                      lambda e, bdn=bdn: e.matmul(
                          bank(bdn, 0, 128, 0, 1),
                          A("c_oc", 0, 0, 4, 1),
                          A("PTs", 0, 0, 4, 128),
                          start=False, stop=True),
                      inc=PE)
                av_done = PE.n
                em.op("vector",
                      lambda e, bdn=bdn: e.reciprocal(
                          A("bcx", 0, 0, 1, 128), bank(bdn, 0, 128, 0, 1)),
                      waits=[(PE, av_done)], inc=DV, drain=False)
                em.op("tensor",
                      lambda e, bdn=bdn: e.matmul(
                          bank(bdn, 128, 128), A("c_or", 0, 0, 1, 128),
                          A("bcx", 0, 0, 1, 128), start=True, stop=True),
                      waits=[(DV, DV.n)], inc=PE)
                em.op("vector",
                      lambda e, bdn=bdn: e.tensor_copy(
                          A("bcx", 128, 0, 128, 128), bank(bdn, 128, 128)),
                      waits=[(PE, PE.n)], drain=False)
                em.op("vector",
                      lambda e, gh=gh, bav=bav: e.tensor_mul(
                          A("attnT", gh * 128, 0, 128, 128),
                          bank(bav, 0, 128), A("bcx", 128, 0, 128, 128)),
                      inc=DV)
                bset(bav, DV)
                bset(bdn, DV)
        attn_done = DV.n

        for mt in range(CCH):
            wbp = mt % 2
            tok_wb = dma(A("wb", wbp * 2048, 0, 128, 2048).rearrange(
                "p (cc d) -> p cc d", d=128),
                d_wo[:, mt * 128:(mt + 1) * 128].rearrange(
                    "(cc p) d -> p cc d", p=128),
                waits=[(PE, wb_use[wbp])] + fenceB)
            ob = mt % 2
            for cc in range(CCH):
                w = ([tok_wb, (DV, attn_done)] + bw(ob)) if cc == 0 \
                    else ()
                em.op("tensor",
                      lambda e, cc=cc, ob=ob, wbp=wbp: e.matmul(
                          bank(ob, 0, 128),
                          A("wb", wbp * 2048 + cc * 128, 0, 128, 128),
                          A("attnT", cc * 128, 0, 128, 128),
                          start=(cc == 0), stop=(cc == CCH - 1)),
                      waits=w, inc=(PE if cc == CCH - 1 else None))
            wb_use[wbp] = PE.n
            em.op("vector",
                  lambda e, mt=mt, ob=ob: e.scalar_tensor_tensor(
                      A("Qr1", mt * 128, 0, 128, 128), bank(ob, 0, 128),
                      A("c_bo", mt, 0, 128, 1),
                      A("xrc", mt * 128, 0, 128, 128),
                      OP.add, OP.add),
                  waits=[(PE, PE.n)], inc=DV, drain=False)
            bset(ob, DV)
        r1_done = DV.n

        for cc in range(CCH):
            w = ([(DV, r1_done)] + bw(4)) if cc == 0 else ()
            em.op("tensor",
                  lambda e, cc=cc: e.matmul(
                      bank(4, 0, 128, 0, 1), A("c_oc", 0, 0, 128, 1),
                      A("Qr1", cc * 128, 0, 128, 128),
                      start=(cc == 0), stop=(cc == CCH - 1)),
                  waits=w, inc=(PE if cc == CCH - 1 else None))
        for cc in range(CCH):
            em.op("scalar",
                  lambda e, cc=cc: e.activation(
                      A("sqs", 0, 0, 128, 128), A("Qr1", cc * 128, 0, 128, 128),
                      AF.Square),
                  waits=([(DV, r1_done), (PE, PE.n)] if cc == 0
                         else [(PE, PE.n)]),
                  inc=AC, drain=False)
            w = ([(AC, AC.n)] + bw(5)) if cc == 0 else [(AC, AC.n)]
            em.op("tensor",
                  lambda e, cc=cc: e.matmul(
                      bank(5, 0, 128, 0, 1), A("c_oc", 0, 0, 128, 1),
                      A("sqs", 0, 0, 128, 128),
                      start=(cc == 0), stop=(cc == CCH - 1)),
                  waits=w, inc=PE)
        stat_mm = PE.n
        em.op("vector",
              lambda e: e.tensor_scalar(
                  A("qs1", 0, 0, 1, 128), bank(4, 0, 128, 0, 1),
                  1.0 / C, None, OP.mult),
              waits=[(PE, stat_mm)])
        em.op("vector",
              lambda e: e.tensor_scalar(
                  A("bcx", 0, 0, 1, 128), bank(5, 0, 128, 0, 1),
                  1.0 / C, None, OP.mult))
        em.op("vector",
              lambda e: e.tensor_mul(
                  A("sqs", 0, 0, 1, 128), A("qs1", 0, 0, 1, 128),
                  A("qs1", 0, 0, 1, 128)))
        em.op("vector",
              lambda e: e.tensor_sub(
                  A("bcx", 0, 0, 1, 128), A("bcx", 0, 0, 1, 128),
                  A("sqs", 0, 0, 1, 128)))
        em.op("vector",
              lambda e: e.tensor_scalar(
                  A("bcx", 0, 0, 1, 128), A("bcx", 0, 0, 1, 128),
                  EPS, None, OP.add))
        em.op("vector",
              lambda e: e.reciprocal(
                  A("sqs", 0, 0, 1, 128), A("bcx", 0, 0, 1, 128)),
              inc=DV)
        em.op("scalar",
              lambda e: e.activation(
                  A("bcx", 0, 0, 1, 128), A("sqs", 0, 0, 1, 128), AF.Sqrt),
              waits=[(DV, DV.n)], inc=AC)
        em.op("vector",
              lambda e: e.tensor_mul(
                  A("sqs", 0, 0, 1, 128), A("qs1", 0, 0, 1, 128),
                  A("bcx", 0, 0, 1, 128)),
              waits=[(AC, AC.n)])
        em.op("vector",
              lambda e: e.tensor_scalar(
                  A("sqs", 0, 0, 1, 128), A("sqs", 0, 0, 1, 128),
                  -1.0, None, OP.mult),
              inc=DV)
        ln_rows = DV.n
        em.op("tensor",
              lambda e: e.matmul(
                  bank(4, 0, 128), A("c_or", 0, 0, 1, 128),
                  A("bcx", 0, 0, 1, 128), start=True, stop=True),
              waits=[(DV, ln_rows)] + bw(4))
        em.op("tensor",
              lambda e: e.matmul(
                  bank(5, 0, 128), A("c_or", 0, 0, 1, 128),
                  A("sqs", 0, 0, 1, 128), start=True, stop=True),
              waits=bw(5), inc=PE)
        em.op("vector",
              lambda e: e.tensor_copy(A("bcx", 0, 0, 128, 128),
                                      bank(4, 0, 128)),
              waits=[(PE, PE.n)])
        em.op("vector",
              lambda e: e.tensor_copy(A("bcx", 128, 0, 128, 128),
                                      bank(5, 0, 128)),
              inc=DV)
        bset(4, DV)
        bset(5, DV)
        for cc in range(CCH):
            em.op("vector",
                  lambda e, cc=cc: e.tensor_mul(
                      A("qs1", 0, 0, 128, 128),
                      A("Qr1", cc * 128, 0, 128, 128),
                      A("bcx", 0, 0, 128, 128)),
                  waits=([(DV, DV.n)] if cc == 0 else ()))
            em.op("vector",
                  lambda e, cc=cc: e.tensor_add(
                      A("Qr1", cc * 128, 0, 128, 128),
                      A("qs1", 0, 0, 128, 128),
                      A("bcx", 128, 0, 128, 128)),
                  inc=DV)
            em.op("scalar",
                  lambda e, cc=cc, oc=oc: e.copy(
                      XH(cc, oc, 128), A("Qr1", cc * 128, 0, 128, 128)),
                  waits=[(DV, DV.n)], inc=AC, drain=False)
            em.op("vector",
                  lambda e, cc=cc, oc=oc: e.tensor_tensor(
                      XL(cc, oc, 128), A("Qr1", cc * 128, 0, 128, 128),
                      XH(cc, oc, 128), OP.subtract),
                  waits=[(AC, AC.n)], inc=DV, drain=False)

    sb.release("Qr1", "xrc", "attnT", "PT", "PTs", "wb",
               "qs1", "sqs", "bcx", "ctab", "K", "V")
    sbb.release("maskb")
    z_done = DV.n
    fenceC = [(PE, PE.n), (DV, DV.n), (AC, AC.n)]

    sb.alloc("probs", NE)
    sb.alloc("prim", NE)
    sb.alloc("sec", NE)
    sb.alloc("rw", NE)
    sb.alloc("bs", NE)
    sb.alloc("comb", NQT * NE)
    sb.alloc("mx", 8)
    sb.alloc("rcol", 4)
    sb.alloc("rscr", 2 * NE)
    sb.alloc("cwrow", 512)
    sb.alloc("gtmp", 512)
    sb.alloc("c_simB", NE * NE)
    sbb.alloc("gateh", CCH * NE)
    sbb.alloc("gatel", CCH * NE)

    tok_simB = dma(A("c_simB", 0, 0, 128, NE * NE), d_simB[:, :],
                   waits=fenceC)

    tok_gt = dma(A("gtmp", 0, 0, 128, CCH * NE).rearrange(
        "p (cc ee) -> p cc ee", ee=NE),
        d_gateT[:, :].rearrange("(cc p) ee -> p cc ee", p=128),
        waits=fenceC)
    em.op("scalar",
          lambda e: e.copy(Bp("gateh", 0, 0, 128, CCH * NE),
                           A("gtmp", 0, 0, 128, CCH * NE)),
          waits=[tok_gt] + fenceC, inc=AC)
    em.op("vector",
          lambda e: e.tensor_tensor(
              Bp("gatel", 0, 0, 128, CCH * NE),
              A("gtmp", 0, 0, 128, CCH * NE),
              Bp("gateh", 0, 0, 128, CCH * NE), OP.subtract),
          waits=[(AC, AC.n)] + fenceC, inc=DV)
    gate_ready = DV.n

    for r in range(NQT):
        oc = r * 128
        lb = r % 2
        passes = [("h", "h"), ("h", "l"), ("l", "h")]
        nmm = 3 * CCH
        k = 0
        for cc in range(CCH):
            for zs, gs_ in passes:
                w = ([(DV, max(z_done, gate_ready))] + bw(lb)) if k == 0 \
                    else ()
                em.op("tensor",
                      lambda e, cc=cc, oc=oc, lb=lb, zs=zs, gs_=gs_,
                      k=k: e.matmul(
                          bank(lb, 0, NE),
                          (XH if zs == "h" else XL)(cc, oc, 128),
                          Bp("gateh" if gs_ == "h" else "gatel",
                             cc * NE, 0, 128, NE),
                          start=(k == 0), stop=(k == nmm - 1)),
                      waits=w, inc=(PE if k == nmm - 1 else None))
                k += 1
        em.op("vector",
              lambda e, lb=lb: e.reduce_max(
                  A("rcol", 0, 0, 128, 1), bank(lb, 0, NE), AX.X,
                  negate=True),
              waits=[(PE, PE.n)], inc=DV)
        em.op("scalar",
              lambda e, lb=lb: e.activation(
                  A("probs", 0, 0, 128, NE), bank(lb, 0, NE), AF.Exp,
                  bias=A("rcol", 0, 0, 128, 1),
                  accum_out=A("rcol", 1, 0, 128, 1)),
              waits=[(DV, DV.n)], inc=AC)
        bset(lb, AC)
        em.op("vector",
              lambda e: e.reciprocal(A("rcol", 2, 0, 128, 1),
                                     A("rcol", 1, 0, 128, 1)),
              waits=[(AC, AC.n)])
        em.op("vector",
              lambda e: e.tensor_scalar(
                  A("probs", 0, 0, 128, NE), A("probs", 0, 0, 128, NE),
                  A("rcol", 2, 0, 128, 1), None, OP.mult))
        em.op("vector",
              lambda e: e.max(A("mx", 0, 0, 128, 8),
                              A("probs", 0, 0, 128, NE)))
        em.op("vector",
              lambda e: e.reduce_sum(
                  A("rcol", 0, 0, 128, 1), A("mx", 0, 0, 128, 8), AX.X))
        em.op("vector",
              lambda e: e.reciprocal(A("rcol", 1, 0, 128, 1),
                                     A("rcol", 0, 0, 128, 1)))
        em.op("vector",
              lambda e: e.tensor_scalar(
                  A("prim", 0, 0, 128, NE), A("probs", 0, 0, 128, NE),
                  A("mx", 3, 0, 128, 1), None, OP.is_ge))
        em.op("vector",
              lambda e: e.tensor_scalar(
                  A("rscr", 0, 0, 128, NE), A("probs", 0, 0, 128, NE),
                  A("mx", 7, 0, 128, 1), None, OP.is_ge))
        em.op("vector",
              lambda e: e.tensor_sub(
                  A("sec", 0, 0, 128, NE), A("rscr", 0, 0, 128, NE),
                  A("prim", 0, 0, 128, NE)))
        em.op("vector",
              lambda e: e.tensor_scalar(
                  A("rw", 0, 0, 128, NE), A("probs", 0, 0, 128, NE),
                  A("rcol", 1, 0, 128, 1), None, OP.mult))
        em.op("vector",
              lambda e: e.tensor_mul(
                  A("rw", 0, 0, 128, NE), A("rw", 0, 0, 128, NE),
                  A("rscr", 0, 0, 128, NE)))
        for s in range(NE):
            em.op("vector",
                  lambda e, s=s: e.tensor_mul(
                      A("rscr", NE, 0, 128, NE), A("prim", 0, 0, 128, NE),
                      A("c_simB", s * NE, 0, 128, NE)),
                  waits=([tok_simB] if s == 0 else ()))
            em.op("vector",
                  lambda e, s=s: e.reduce_max(
                      A("bs", s, 0, 128, 1),
                      A("rscr", NE, 0, 128, NE), AX.X))
        em.op("vector",
              lambda e: e.tensor_scalar(
                  A("rscr", 0, 0, 128, NE), A("bs", 0, 0, 128, NE),
                  0.5, None, OP.is_lt))
        em.op("vector",
              lambda e: e.tensor_mul(
                  A("rscr", NE, 0, 128, NE), A("sec", 0, 0, 128, NE),
                  A("rscr", 0, 0, 128, NE)))
        em.op("vector",
              lambda e: e.tensor_add(
                  A("rscr", NE, 0, 128, NE), A("rscr", NE, 0, 128, NE),
                  A("prim", 0, 0, 128, NE)))
        em.op("vector",
              lambda e, r=r: e.tensor_mul(
                  A("comb", r * NE, 0, 128, NE), A("rw", 0, 0, 128, NE),
                  A("rscr", NE, 0, 128, NE)))
        em.op("vector",
              lambda e: e.tensor_scalar(
                  A("rscr", 0, 0, 128, NE), A("rscr", 0, 0, 128, NE),
                  -1.0, 1.0, OP.mult, OP.add))
        em.op("vector",
              lambda e: e.tensor_mul(
                  A("sec", 0, 0, 128, NE), A("sec", 0, 0, 128, NE),
                  A("rw", 0, 0, 128, NE)))
        em.op("vector",
              lambda e: e.tensor_mul(
                  A("sec", 0, 0, 128, NE), A("sec", 0, 0, 128, NE),
                  A("rscr", 0, 0, 128, NE)))
        for s in range(NE):
            em.op("vector",
                  lambda e, s=s: e.tensor_scalar(
                      A("rscr", 0, 0, 128, NE), A("c_simB", s * NE, 0, 128, NE),
                      A("bs", s, 0, 128, 1), None, OP.is_ge))
            em.op("vector",
                  lambda e: e.tensor_mul(
                      A("rscr", 0, 0, 128, NE), A("rscr", 0, 0, 128, NE),
                      A("prim", 0, 0, 128, NE)))
            em.op("vector",
                  lambda e, r=r, s=s: e.scalar_tensor_tensor(
                      A("comb", r * NE, 0, 128, NE),
                      A("rscr", 0, 0, 128, NE),
                      A("sec", s, 0, 128, 1),
                      A("comb", r * NE, 0, 128, NE),
                      OP.mult, OP.add),
                  inc=(DV if s == NE - 1 else None))
    routing_done = DV.n

    sb.release("probs", "prim", "rw", "bs", "mx", "rcol", "rscr", "sec",
               "gtmp", "c_simB")
    fenceC = [(PE, PE.n), (DV, DV.n), (AC, AC.n)]

    sb.alloc("wgu", 2 * CCH * 128)
    sb.alloc("wd", 6 * 256)
    sbb.alloc("wguh", 2 * CCH * 128)
    sbb.alloc("wgul", 2 * CCH * 128)
    sbb.alloc("wdh", 2 * 6 * 256)
    sbb.alloc("wdl", 2 * 6 * 256)
    sb.alloc("sg", 6 * 512)
    sb.alloc("hsc", 512)
    sbb.alloc("hhh", 6 * 512)
    sbb.alloc("hhl", 6 * 512)
    sb.alloc("y", CCH * 512)
    sb.alloc("cw", 512)
    sb.alloc("zrc", 512)
    sb.alloc("sq2", 512)
    sb.alloc("lrow", 2 * 512)
    sb.alloc("lbc", 1024)
    sb.alloc("otm", C)

    def ZH(cc, ih, n=512):
        return Bp("xh", cc * 1024 + ih * 512, 0, 128, n)

    def ZL(cc, ih, n=512):
        return Bp("xl", cc * 1024 + ih * 512, 0, 128, n)

    wgu_use = [0, 0]
    wd_use = [0, 0]
    wgu_f32_free = [[], []]
    wd_f32_free = [[]]
    tok_st = None
    gu_splits = {}
    wd_splits = {}
    for ih in range(2):
        def emit_gu_split(e_ix, ft):
            par = ft % 2
            tok_g = dma(
                A("wgu", par * CCH * 128, 0, 128, CCH * 128).rearrange(
                    "p (cc f) -> p cc f", f=128),
                d_wguT[e_ix, :, ft * 128:(ft + 1) * 128].rearrange(
                    "(cc p) f -> p cc f", p=128),
                waits=wgu_f32_free[par] + fenceC)
            em.op("scalar",
                  lambda e, par=par: e.copy(
                      Bp("wguh", par * CCH * 128, 0, 128, CCH * 128),
                      A("wgu", par * CCH * 128, 0, 128, CCH * 128)),
                  waits=[tok_g, (PE, wgu_use[par])], inc=AC,
                  drain=False)
            cast_n = AC.n
            em.op("vector",
                  lambda e, par=par: e.tensor_tensor(
                      Bp("wgul", par * CCH * 128, 0, 128, CCH * 128),
                      A("wgu", par * CCH * 128, 0, 128, CCH * 128),
                      Bp("wguh", par * CCH * 128, 0, 128, CCH * 128),
                      OP.subtract),
                  waits=[(AC, cast_n)], inc=DV, drain=False)
            sub_n = DV.n
            wgu_f32_free[par] = [(AC, cast_n), (DV, sub_n)]
            gu_splits[(e_ix, ft)] = (cast_n, sub_n)

        def emit_wd_split(e_ix, mp):
            par = mp % 2
            tok_d = dma(
                A("wd", 0, 0, 128, 6 * 256).rearrange(
                    "p (fc m) -> p fc m", m=256),
                d_wdT[e_ix, :, mp * 256:(mp + 1) * 256].rearrange(
                    "(fc p) m -> p fc m", p=128),
                waits=wd_f32_free[0] + fenceC)
            em.op("scalar",
                  lambda e, par=par: e.copy(
                      Bp("wdh", par * 6 * 256, 0, 128, 6 * 256),
                      A("wd", 0, 0, 128, 6 * 256)),
                  waits=[tok_d, (PE, wd_use[par])], inc=AC,
                  drain=False)
            dcast_n = AC.n
            em.op("vector",
                  lambda e, par=par: e.tensor_tensor(
                      Bp("wdl", par * 6 * 256, 0, 128, 6 * 256),
                      A("wd", 0, 0, 128, 6 * 256),
                      Bp("wdh", par * 6 * 256, 0, 128, 6 * 256),
                      OP.subtract),
                  waits=[(AC, dcast_n)], inc=DV, drain=False)
            dsub_n = DV.n
            wd_f32_free[0] = [(AC, dcast_n), (DV, dsub_n)]
            wd_splits[(e_ix, mp)] = (dcast_n, dsub_n)

        def emit_down(e_ix, h_done):
            for mp in range(8):
                par = mp % 2
                dcast_n, dsub_n = wd_splits.pop((e_ix, mp))
                for sub in range(2):
                    mt = mp * 2 + sub
                    db = 2 + mt % 2
                    k = 0
                    for fc in range(6):
                        for ws, hs in [("h", "h"), ("h", "l"), ("l", "h")]:
                            w = ([(AC, dcast_n), (DV, max(dsub_n, h_done))]
                                 + bw(db)) if k == 0 else ()
                            em.op("tensor",
                                  lambda e, fc=fc, par=par, sub=sub, db=db,
                                  ws=ws, hs=hs, k=k: e.matmul(
                                      bank(db, 0, 512),
                                      Bp("wdh" if ws == "h" else "wdl",
                                         par * 6 * 256 + fc * 256
                                         + sub * 128, 0, 128, 128),
                                      Bp("hhh" if hs == "h" else "hhl",
                                         fc * 512, 0, 128, 512),
                                      start=(k == 0), stop=(k == 17)),
                                  waits=w, inc=(PE if k == 17 else None))
                            k += 1
                    wd_use[par] = PE.n
                    if sub == 1 and mp + 2 < 8:
                        emit_wd_split(e_ix, mp + 2)
                    if e_ix == 0:
                        em.op("vector",
                              lambda e, mt=mt, db=db: e.tensor_copy(
                                  A("y", mt * 512, 0, 128, 512),
                                  bank(db, 0, 512)),
                              waits=[(PE, PE.n)], inc=DV, drain=False)
                    else:
                        em.op("vector",
                              lambda e, mt=mt, db=db: e.tensor_add(
                                  A("y", mt * 512, 0, 128, 512),
                                  A("y", mt * 512, 0, 128, 512),
                                  bank(db, 0, 512)),
                              waits=[(PE, PE.n)], inc=DV, drain=False)
                    bset(db, DV)

        pending = None
        emit_gu_split(0, 0)
        emit_gu_split(0, 1)
        for e_ix in range(NE):
            for rr in range(4):
                em.op("tensor",
                      lambda e, rr=rr, e_ix=e_ix, ih=ih: e.transpose(
                          bank(4, rr * 128, 128, 0, 1),
                          A("comb", (ih * 4 + rr) * NE + e_ix, 0, 128, 1),
                          A("c_id", 0, 0, 128, 128)),
                      waits=([(DV, routing_done)] + bw(4)) if rr == 0
                      else (), inc=(PE if rr == 3 else None))
            em.op("scalar",
                  lambda e: e.copy(A("cwrow", 0, 0, 1, 512),
                                   bank(4, 0, 512, 0, 1)),
                  waits=[(PE, PE.n)], inc=AC, drain=False)
            em.op("tensor",
                  lambda e: e.matmul(
                      bank(4, 0, 512), A("c_or", 0, 0, 1, 128),
                      A("cwrow", 0, 0, 1, 512),
                      start=True, stop=True),
                  waits=[(AC, AC.n)], inc=PE)
            em.op("vector",
                  lambda e: e.tensor_copy(A("cw", 0, 0, 128, 512),
                                          bank(4, 0, 512)),
                  waits=[(PE, PE.n)], inc=DV, drain=False)
            bset(4, DV)
            cw_done = DV.n
            for ft in range(12):
                par = ft % 2
                cast_n, sub_n = gu_splits.pop((e_ix, ft))
                gb = ft % 2
                k = 0
                for cc in range(CCH):
                    for ws, zs in [("h", "h"), ("h", "l"), ("l", "h")]:
                        w = ([(AC, cast_n), (DV, sub_n)] + bw(gb)) \
                            if k == 0 else ()
                        em.op("tensor",
                              lambda e, cc=cc, par=par, gb=gb, ih=ih,
                              ws=ws, zs=zs, k=k: e.matmul(
                                  bank(gb, 0, 512),
                                  Bp("wguh" if ws == "h" else "wgul",
                                     par * CCH * 128 + cc * 128,
                                     0, 128, 128),
                                  (ZH if zs == "h" else ZL)(cc, ih),
                                  start=(k == 0), stop=(k == 3 * CCH - 1)),
                              waits=w,
                              inc=(PE if k == 3 * CCH - 1 else None))
                        k += 1
                wgu_use[par] = PE.n
                gmm = PE.n
                if ft + 2 < 12:
                    emit_gu_split(e_ix, ft + 2)
                elif e_ix + 1 < NE:
                    emit_gu_split(e_ix + 1, ft + 2 - 12)
                if ft < 6:
                    em.op("scalar",
                          lambda e, ft=ft, gb=gb: e.activation(
                              A("sg", ft * 512, 0, 128, 512),
                              bank(gb, 0, 512), AF.Silu),
                          waits=[(PE, gmm), (DV, DV.n)], inc=AC,
                          drain=False)
                    bset(gb, AC)
                else:
                    kk = ft - 6
                    em.op("vector",
                          lambda e, kk=kk, gb=gb: e.tensor_mul(
                              A("hsc", 0, 0, 128, 512),
                              A("sg", kk * 512, 0, 128, 512),
                              bank(gb, 0, 512)),
                          waits=[(PE, gmm), (AC, AC.n), (DV, cw_done)])
                    em.op("vector",
                          lambda e: e.tensor_mul(
                              A("hsc", 0, 0, 128, 512),
                              A("hsc", 0, 0, 128, 512),
                              A("cw", 0, 0, 128, 512)),
                          inc=DV)
                    bset(gb, DV)
                    em.op("scalar",
                          lambda e, kk=kk: e.copy(
                              Bp("hhh", kk * 512, 0, 128, 512),
                              A("hsc", 0, 0, 128, 512)),
                          waits=[(DV, DV.n), (PE, gmm)], inc=AC,
                          drain=False)
                    em.op("vector",
                          lambda e, kk=kk: e.tensor_tensor(
                              Bp("hhl", kk * 512, 0, 128, 512),
                              A("hsc", 0, 0, 128, 512),
                              Bp("hhh", kk * 512, 0, 128, 512),
                              OP.subtract),
                          waits=[(AC, AC.n), (PE, gmm)], inc=DV)
                if ft == 0:
                    emit_wd_split(e_ix if pending is None else pending[0],
                                  0) if False else None
                if ft == 1 and pending is not None:
                    emit_wd_split(pending[0], 0)
                    emit_wd_split(pending[0], 1)
                    emit_down(*pending)
                    pending = None
            pending = (e_ix, DV.n)
        emit_wd_split(pending[0], 0)
        emit_wd_split(pending[0], 1)
        emit_down(*pending)
        pending = None

        for mt in range(CCH):
            em.op("vector",
                  lambda e, mt=mt, ih=ih: e.tensor_add(
                      A("zrc", 0, 0, 128, 512), ZH(mt, ih), ZL(mt, ih)))
            em.op("vector",
                  lambda e, mt=mt: e.tensor_add(
                      A("y", mt * 512, 0, 128, 512),
                      A("y", mt * 512, 0, 128, 512),
                      A("zrc", 0, 0, 128, 512)),
                  inc=DV)
        r2_done = DV.n
        for cc in range(CCH):
            w = ([(DV, r2_done)] + bw(5)) if cc == 0 else ()
            em.op("tensor",
                  lambda e, cc=cc: e.matmul(
                      bank(5, 0, 512, 0, 1), A("c_oc", 0, 0, 128, 1),
                      A("y", cc * 512, 0, 128, 512),
                      start=(cc == 0), stop=(cc == CCH - 1)),
                  waits=w, inc=(PE if cc == CCH - 1 else None))
        for cc in range(CCH):
            em.op("scalar",
                  lambda e, cc=cc: e.activation(
                      A("sq2", 0, 0, 128, 512), A("y", cc * 512, 0, 128, 512),
                      AF.Square),
                  waits=([(DV, r2_done), (PE, PE.n)] if cc == 0
                         else [(PE, PE.n)]),
                  inc=AC, drain=False)
            w = ([(AC, AC.n)] + bw(6)) if cc == 0 else [(AC, AC.n)]
            em.op("tensor",
                  lambda e, cc=cc: e.matmul(
                      bank(6, 0, 512, 0, 1), A("c_oc", 0, 0, 128, 1),
                      A("sq2", 0, 0, 128, 512),
                      start=(cc == 0), stop=(cc == CCH - 1)),
                  waits=w, inc=PE)
        st2 = PE.n
        em.op("vector",
              lambda e: e.tensor_scalar(
                  A("lrow", 0, 0, 1, 512), bank(5, 0, 512, 0, 1),
                  1.0 / C, None, OP.mult),
              waits=[(PE, st2)])
        em.op("vector",
              lambda e: e.tensor_scalar(
                  A("lrow", 512, 0, 1, 512), bank(6, 0, 512, 0, 1),
                  1.0 / C, None, OP.mult))
        em.op("vector",
              lambda e: e.tensor_mul(
                  A("zrc", 0, 0, 1, 512), A("lrow", 0, 0, 1, 512),
                  A("lrow", 0, 0, 1, 512)))
        em.op("vector",
              lambda e: e.tensor_sub(
                  A("lrow", 512, 0, 1, 512), A("lrow", 512, 0, 1, 512),
                  A("zrc", 0, 0, 1, 512)))
        em.op("vector",
              lambda e: e.tensor_scalar(
                  A("lrow", 512, 0, 1, 512), A("lrow", 512, 0, 1, 512),
                  EPS, None, OP.add))
        em.op("vector",
              lambda e: e.reciprocal(
                  A("zrc", 0, 0, 1, 512), A("lrow", 512, 0, 1, 512)),
              inc=DV)
        em.op("scalar",
              lambda e: e.activation(
                  A("lrow", 512, 0, 1, 512), A("zrc", 0, 0, 1, 512),
                  AF.Sqrt),
              waits=[(DV, DV.n)], inc=AC)
        em.op("vector",
              lambda e: e.tensor_mul(
                  A("zrc", 0, 0, 1, 512), A("lrow", 0, 0, 1, 512),
                  A("lrow", 512, 0, 1, 512)),
              waits=[(AC, AC.n)])
        em.op("vector",
              lambda e: e.tensor_scalar(
                  A("zrc", 0, 0, 1, 512), A("zrc", 0, 0, 1, 512),
                  -1.0, None, OP.mult),
              inc=DV)
        lnr = DV.n
        em.op("tensor",
              lambda e: e.matmul(
                  bank(5, 0, 512), A("c_or", 0, 0, 1, 128),
                  A("lrow", 512, 0, 1, 512), start=True, stop=True),
              waits=[(DV, lnr)] + bw(5))
        em.op("tensor",
              lambda e: e.matmul(
                  bank(6, 0, 512), A("c_or", 0, 0, 1, 128),
                  A("zrc", 0, 0, 1, 512), start=True, stop=True),
              waits=bw(6), inc=PE)
        em.op("vector",
              lambda e: e.tensor_copy(A("lbc", 0, 0, 128, 512),
                                      bank(5, 0, 512)),
              waits=[(PE, PE.n)])
        em.op("vector",
              lambda e: e.tensor_copy(A("lbc", 512, 0, 128, 512),
                                      bank(6, 0, 512)),
              inc=DV)
        bset(5, DV)
        bset(6, DV)
        bc_done = DV.n
        for j in range(4):
            for mt in range(CCH):
                em.op("vector",
                      lambda e, mt=mt, j=j: e.tensor_mul(
                          A("sq2", 0, 0, 128, 128),
                          A("y", mt * 512 + j * 128, 0, 128, 128),
                          A("lbc", j * 128, 0, 128, 128)),
                      waits=(([(DV, bc_done), (PE, PE.n)]
                              + ([tok_st] if tok_st else []))
                             if mt == 0 else [(PE, PE.n)]), drain=False)
                em.op("vector",
                      lambda e, mt=mt, j=j: e.tensor_add(
                          A("sq2", 0, 0, 128, 128),
                          A("sq2", 0, 0, 128, 128),
                          A("lbc", 512 + j * 128, 0, 128, 128)),
                      inc=DV)
                tb = 7 if mt % 2 == 0 else 0
                em.op("tensor",
                      lambda e, tb=tb: e.transpose(
                          bank(tb, 0, 128), A("sq2", 0, 0, 128, 128),
                          A("c_id", 0, 0, 128, 128)),
                      waits=[(DV, DV.n)] + bw(tb), inc=PE)
                em.op("scalar",
                      lambda e, mt=mt, tb=tb: e.copy(
                          A("otm", mt * 128, 0, 128, 128), bank(tb, 0, 128)),
                      waits=[(PE, PE.n)], inc=AC, drain=False)
                bset(tb, AC)
            r0 = ih * 512 + j * 128
            tok_st = dma(d_out[r0:r0 + 128, :], A("otm", 0, 0, 128, C),
                         waits=[(AC, AC.n)], q=qst)

    em.op("gpsimd", lambda e: e.nop(),
          waits=[(s, s.n) for s in qst.sems])

    with nc.Block() as blk:
        blk.sync(lambda e: em.replay("sync", e))
        blk.tensor(lambda e: em.replay("tensor", e))
        blk.vector(lambda e: em.replay("vector", e))
        blk.scalar(lambda e: em.replay("scalar", e))
        blk.gpsimd(lambda e: em.replay("gpsimd", e))

    nc.dbg_sb = sb
    nc.dbg_sbb = sbb
    nc.dbg_ps = ps_h
    return nc


def kernel(**inputs):
    import ml_dtypes

    x = np.asarray(inputs["L0_attn_x"], dtype=np.float32)
    wq = np.asarray(inputs["L0_attn_Wq"], dtype=np.float32)
    bq = np.asarray(inputs["L0_attn_bq"], dtype=np.float32)
    wk = np.asarray(inputs["L0_attn_Wk"], dtype=np.float32)
    bk = np.asarray(inputs["L0_attn_bk"], dtype=np.float32)
    wv = np.asarray(inputs["L0_attn_Wv"], dtype=np.float32)
    bv = np.asarray(inputs["L0_attn_bv"], dtype=np.float32)
    wo = np.asarray(inputs["L0_attn_Wo"], dtype=np.float32)
    bo = np.asarray(inputs["L0_attn_bo"], dtype=np.float32)
    gate = np.asarray(inputs["L0_ffn_gate_weight"], dtype=np.float32)
    gup = np.asarray(inputs["L0_ffn_gate_up_proj"], dtype=np.float32)
    dwn = np.asarray(inputs["L0_ffn_down_proj"], dtype=np.float32)
    sim = np.asarray(inputs["L0_ffn_similarity"], dtype=np.float32)

    wguT = np.ascontiguousarray(np.transpose(gup, (0, 2, 1)))
    wdT = np.ascontiguousarray(np.transpose(dwn, (0, 2, 1)))
    gateT = np.ascontiguousarray(gate.T)
    simd = sim.copy()
    np.fill_diagonal(simd, 1.0)
    simB = np.ascontiguousarray(
        np.tile(simd.reshape(1, NE * NE), (128, 1)))

    cos_dup, sin_sgn = _rope_tables()
    ident = np.eye(128, dtype=np.float32)
    rot = np.zeros((128, 128), dtype=np.float32)
    for m in range(128):
        rot[(m + 64) % 128, m] = 1.0
    onescol = np.ones((128, 1), dtype=np.float32)
    onesrow = np.ones((1, 128), dtype=np.float32)
    bqc = np.ascontiguousarray(bq.reshape(CCH, 128).T)
    boc = np.ascontiguousarray(bo.reshape(CCH, 128).T)
    bkc = np.ascontiguousarray(bk.reshape(NG, 128).T)
    bvr = np.ascontiguousarray(bv.reshape(1, 512))

    in_maps = []
    for core in range(NCORES):
        b, h = core // 2, core % 2
        _, glob = _glob_tokens(h)
        gl = np.clip(glob, 0, T - 1)
        in_maps.append({
            "xloc": _xloc_for_core(x[b], h),
            "wq": wq, "wk": wk, "wv": wv, "wo": wo,
            "bqc": bqc, "bkc": bkc, "bvr": bvr, "boc": boc,
            "gateT": gateT, "wguT": wguT, "wdT": wdT, "simB": simB,
            "masks": _masks_for_core(h).astype(ml_dtypes.bfloat16),
            "sinkb": _sinkbias_for_core(h),
            "cosdup": np.ascontiguousarray(cos_dup[:, gl]),
            "sinsgn": np.ascontiguousarray(sin_sgn[:, gl]),
            "ident": ident, "rotmat": rot,
            "onescol": onescol, "onesrow": onesrow,
        })

    nc = build_program()
    res = run_bass_kernel_spmd(nc, in_maps, core_ids=list(range(NCORES)))
    out = np.empty((B, T, C), dtype=np.float32)
    for core in range(NCORES):
        b, h = core // 2, core % 2
        out[b, h * 1024:(h + 1) * 1024] = res.results[core]["out"]
    return out
